# version 10
# baseline (speedup 1.0000x reference)
import contextlib

import numpy as np
import ml_dtypes

import concourse.bass as bass
import concourse.mybir as mybir
from concourse.bass_utils import run_bass_kernel_spmd

F32 = mybir.dt.float32
BF16 = mybir.dt.bfloat16
FP8 = mybir.dt.float8e4
U8 = mybir.dt.uint8
AF = mybir.ActivationFunctionType
OP = mybir.AluOpType
BF = ml_dtypes.bfloat16

B_C = 4
T = 2048
NHP = 8
NTQ = 4
TQ = 512
NS0 = 16
NDC = 8
EXPB = -4.0
NB_IL = NHP * NTQ * NS0
GDPB = 16 + 16 * 2 * NHP

NUM_BUCKETS = 320
MAX_DISTANCE = 800


def _bucket_table():
    import math
    d = np.arange(-(T - 1), T)
    nb = NUM_BUCKETS // 2
    rel_b = (d > 0).astype(np.int64) * nb
    rp = np.abs(d)
    me = nb // 2
    small = rp < me
    rl = np.log(np.maximum(rp, 1).astype(np.float32) / me)
    rl = rl / math.log(MAX_DISTANCE / me)
    rl = rl * (nb - me)
    rl = (me + rl).astype(np.int64)
    rl = np.minimum(rl, nb - 1)
    return rel_b + np.where(small, rp, rl)


def build_nc():
    nc = bass.Bass("TRN2", target_bir_lowering=False)

    xd = nc.dram_tensor("x", [B_C, NDC, 128, T], F32, kind="ExternalInput")
    wqd = nc.dram_tensor("wq", [128, 8192], F32, kind="ExternalInput")
    wkd = nc.dram_tensor("wk", [128, 8192], F32, kind="ExternalInput")
    wvd = nc.dram_tensor("wv", [128, 8192], F32, kind="ExternalInput")
    wod = nc.dram_tensor("wo", [128, 8192], F32, kind="ExternalInput")
    w8d = nc.dram_tensor("w8", [128, 8192], F32, kind="ExternalInput")
    bqd = nc.dram_tensor("bq", [128, 8], F32, kind="ExternalInput")
    bkd = nc.dram_tensor("bk", [128, 8], F32, kind="ExternalInput")
    bvrd = nc.dram_tensor("bvr", [1, 1024], F32, kind="ExternalInput")
    bod = nc.dram_tensor("bo", [128, 8], F32, kind="ExternalInput")
    bfd = nc.dram_tensor("bfn", [128, 8], F32, kind="ExternalInput")
    reld = nc.dram_tensor("rel", [320, 16], F32, kind="ExternalInput")
    ohtd = nc.dram_tensor("oht", [320, 4096], BF16, kind="ExternalInput")
    gwd_d = nc.dram_tensor("gw", [8, 64], F32, kind="ExternalInput")
    gbid = nc.dram_tensor("gbi", [8, 1], F32, kind="ExternalInput")
    gcid = nc.dram_tensor("gci", [16, 1], F32, kind="ExternalInput")
    sclid = nc.dram_tensor("scli", [1, 2], F32, kind="ExternalInput")
    i128d = nc.dram_tensor("i128", [128, 128], BF16, kind="ExternalInput")
    selRd = nc.dram_tensor("selR", [2, 128], BF16, kind="ExternalInput")
    sel232d = nc.dram_tensor("sel232", [2, 32], BF16, kind="ExternalInput")
    p82d = nc.dram_tensor("p82", [8, 2], BF16, kind="ExternalInput")

    outd = nc.dram_tensor("out", [B_C, NDC, 128, T], F32, kind="ExternalOutput")

    rbd = nc.dram_tensor("rbd", [16, 4096], BF16, kind="Internal")
    x1d = nc.dram_tensor("x1d", [B_C, NDC, 128, T], F32, kind="Internal")
    statscr = nc.dram_tensor("statscr", [4096], F32, kind="Internal")
    denscr = nc.dram_tensor("denscr", [4096], F32, kind="Internal")
    gatescr = nc.dram_tensor("gatescr", [16, 2048], BF16, kind="Internal")

    es = contextlib.ExitStack()
    nc._kernel_es = es

    def sb(name, shape, dt_):
        return es.enter_context(nc.sbuf_tensor(name, shape, dt_))

    def sem(name):
        return es.enter_context(nc.semaphore(name))

    arA = sb("arA", [128, 49408], U8)
    arB = sb("arB", [128, 32768], U8)
    arC = sb("arC", [128, 32768], U8)

    vT = arA[:, :].bitcast(BF16)
    vTr = vT.rearrange("p (s h c) -> p s h c", s=NS0, h=NHP, c=193)
    wosv = arA[:, 0:16384].bitcast(BF16)
    w8sv = arA[:, 16384:24576].bitcast(FP8)
    z8v = arA[:, 24576:40960].bitcast(FP8)
    xbf = arB[:, :].bitcast(BF16)
    ctxn = arC[:, :].bitcast(BF16)
    mbc = arC[:, 0:8192].bitcast(F32)
    ibc = arC[:, 8192:16384].bitcast(F32)
    gsigA = arC[:, 16384:24576].bitcast(F32)
    gsigB = arC[:, 24576:32768].bitcast(F32)
    wvsv = arC[:, 0:16384].bitcast(BF16)

    xstage_t = [sb(f"xst{i}", [128, 2048], F32) for i in range(2)]
    xstage = [x[:, :] for x in xstage_t]
    ctxu = [sb(f"ctxu{i}", [128, T], BF16) for i in range(2)]
    kqp = sb("kqp", [128, 2 * T], BF16)
    sqst = kqp[:, 0:2048]
    sq2 = kqp[:, 2048:4096]
    gbct = [sb(f"gbc{i}", [128, 2 * T], BF16) for i in range(2)]
    tstt = [g[:, :].bitcast(F32) for g in gbct]
    wqs = [sb(f"wqs{i}", [128, 1024], BF16) for i in range(2)]
    wks = [sb(f"wks{i}", [128, 1024], BF16) for i in range(2)]
    obuf = [w[:, :].bitcast(F32) for w in wqs]
    xrst = [w[:, :].bitcast(F32) for w in wks]
    pbb = [x[:, :].bitcast(BF16) for x in xstage_t]
    Gb = [sb(f"Gb{i}", [128, 1024], BF16) for i in range(2)]
    exb = [sb(f"exb{i}", [128, 1024], BF16) for i in range(3)]
    o2buf = [g[:, :].bitcast(F32) for g in Gb]
    dstg = sb("dstg", [128, 2048], F32)
    sstv = dstg[0:1, 0:2048]
    d128 = sb("d128", [128, 32], F32)
    d128r = sb("d128r", [128, 32], F32)
    bcps = sb("bcps", [128, 2048], BF16)
    gate16 = sb("gate16", [16, 2048], BF16)
    dstat = sb("dstat", [128, 48], F32)

    bq8 = sb("bq8", [128, 8], F32)
    bqraw = sb("bqraw", [128, 8], F32)
    bks = sb("bks", [128, 8], F32)
    bos = sb("bos", [128, 8], F32)
    bfs = sb("bfs", [128, 8], F32)
    vbr = sb("vbr", [1, 1024], BF16)
    gwdb = sb("gwdb", [8, 128], BF16)
    gbr = sb("gbr", [8, 1], BF16)
    gcs = sb("gcs", [16, 1], F32)
    scl = sb("scl", [1, 2], F32)
    invs_sb = sb("invs", [1, 1], F32)
    sws_sb = sb("sws", [1, 1], F32)
    isc = sb("isc", [128, 2], F32)
    i128s = sb("i128s", [128, 128], BF16)
    selRs = sb("selRs", [2, 128], BF16)
    sel232s = sb("sel232s", [2, 32], BF16)
    p82s = sb("p82s", [8, 2], BF16)
    Tg = sb("Tg", [128, 256], BF16)
    gb2s = sb("gb2s", [2, 1], BF16)
    gb32f = sb("gb32f", [16, 2], F32)
    smalls = sb("smalls", [128, 4], F32)
    onesb = smalls[:, 0:1].bitcast(BF16)[:, 0:1]
    ones1b = sb("ones1b", [1, 128], BF16)
    ones1f = sb("ones1f", [1, 128], F32)
    ebias = smalls[:, 1:2]
    eighth = smalls[:, 2:3]
    dumm = smalls[0:1, 3:4]

    sWD = sem("sWD")
    sGS = sem("sGS")
    sGDp = [sem("sGD0"), sem("sGD1")]
    sXp = [sem("sX0"), sem("sX1")]
    sPp = [sem(f"sP{i}") for i in range(2)]
    sWqp = [sem("sWq0"), sem("sWq1")]
    sWkp = [sem("sWk0"), sem("sWk1")]
    sXWp = [sem("sXW0"), sem("sXW1")]
    sOp = [sem("sO0"), sem("sO1")]
    sSQ = sem("sSQ")
    sCB = sem("sCB")
    sST = sem("sST")
    sSC = sem("sSC")
    sBD = sem("sBD")
    sAP = sem("sAP")
    sQK = sem("sQK")
    sEV = sem("sEV")
    sG = sem("sG")
    sS = sem("sS")
    sE = sem("sE")
    sA = sem("sA")
    sV = sem("sV")
    sDD = sem("sDD")
    sR = sem("sR")
    sB = sem("sB")
    sN = sem("sN")
    sOT = sem("sOT")
    sOE = sem("sOE")
    sRV = sem("sRV")
    sTI = sem("sTI")
    sVI = sem("sVI")
    sGE = sem("sGE")
    sVC = sem("sVC")
    sVS = sem("sVS")

    C = {k: 0 for k in ("sWD sGS sGD0 sGD1 sX0 sX1 sP0 sP1 sWq0 sWq1 sWk0 sWk1 "
                        "sXW0 sXW1 sO0 sO1 sSQ sCB sST sSC sBD sAP sQK sEV sDD "
                        "sOT sOE sRV sGE sVC").split()}

    def ig(b, il):
        return b * NB_IL + il

    def jslot(b, hp, tq):
        return (b * NHP + hp) * NTQ + tq

    ohts = arA[:, 0:24576].bitcast(BF16)
    rbsb = arA[:, 24576:40960].bitcast(BF16)[0:16, 0:4096]
    rels = arC[:, 0:96].bitcast(BF16)
    with nc.Block() as blk:
        @blk.gpsimd
        def _(g):
            for dst, src in (
                (bqraw[:, :], bqd[:, :]), (bks[:, :], bkd[:, :]),
                (bos[:, :], bod[:, :]), (bfs[:, :], bfd[:, :]),
                (vbr[:, :], bvrd[:, :]),
                (rels[0:128, 0:16], reld[0:128, :]),
                (rels[0:128, 16:32], reld[128:256, :]),
                (rels[0:64, 32:48], reld[256:320, :]),
                (ohts[0:128, 0:4096], ohtd[0:128, :]),
                (ohts[0:128, 4096:8192], ohtd[128:256, :]),
                (ohts[0:64, 8192:12288], ohtd[256:320, :]),
                (gwdb[:, 0:64], gwd_d[:, :]), (gwdb[:, 64:128], gwd_d[:, :]),
                (gbr[:, :], gbid[:, :]), (gcs[:, :], gcid[:, :]),
                (scl[:, :], sclid[:, :]), (i128s[:, :], i128d[:, :]),
                (selRs[:, :], selRd[:, :]), (sel232s[:, :], sel232d[:, :]),
                (p82s[:, :], p82d[:, :]),
            ):
                g.dma_start(dst, src).then_inc(sWD, 16)
            g.memset(onesb[:, :], 1.0)
            g.memset(ones1b[:, :], 1.0)
            g.memset(ones1f[:, :], 1.0)
            g.memset(ebias[:, :], EXPB)
            g.memset(eighth[:, :], 0.125)
            g.memset(dumm[:, :], 0.0)
            g.memset(dstg[:, :], 0.0)
            g.memset(Tg[:, :], 0.0).then_inc(sTI, 1)
            g.wait_ge(sVI, 4)
            g.dma_start(rbd[:, :], rbsb[:, :]).then_inc(sWD, 16)

        with (
            nc.psum_tensor("rbps", [16, 1024], F32) as rbps,
            nc.psum_tensor("gw2ps", [128, 2], F32) as gw2ps,
            nc.psum_tensor("gb2ps", [2, 1], F32) as gb2ps,
            nc.psum_tensor("gb32ps", [16, 2], F32) as gb32ps,
            nc.psum_tensor("iscps", [128, 2], F32) as iscps,
        ):
            @blk.tensor
            def _(t):
                t.wait_ge(sWD, 16 * 20)
                t.wait_ge(sTI, 1)
                cnts = [128, 128, 64]
                for n4 in range(4):
                    if n4 >= 1:
                        t.wait_ge(sVI, n4)
                    for c in range(3):
                        for half in range(2):
                            mm = t.matmul(rbps[:, half * 512:(half + 1) * 512],
                                          rels[0:cnts[c], 16 * c:16 * (c + 1)],
                                          ohts[0:cnts[c],
                                               c * 4096 + n4 * 1024 + half * 512:
                                               c * 4096 + n4 * 1024 + (half + 1) * 512],
                                          start=(c == 0), stop=(c == 2))
                    mm.then_inc(sTI, 1)
                t.matmul(gw2ps[:, :], gwdb[:, :], p82s[:, :],
                         start=True, stop=True).then_inc(sTI, 1)
                t.matmul(gb2ps[:, :], p82s[:, :], gbr[:, :],
                         start=True, stop=True).then_inc(sTI, 1)
                t.wait_ge(sVI, 6)
                t.matmul(gb32ps[:, 0:1], sel232s[:, 0:16], gb2s[:, :],
                         start=True, stop=True)
                t.matmul(gb32ps[:, 1:2], sel232s[:, 16:32], gb2s[:, :],
                         start=True, stop=True).then_inc(sTI, 1)
                t.wait_ge(sVI, 7)
                t.matmul(iscps[:, 0:1], ones1f[:, :], invs_sb[:, :], start=True, stop=True)
                t.matmul(iscps[:, 1:2], ones1f[:, :], sws_sb[:, :],
                         start=True, stop=True).then_inc(sTI, 1)

            @blk.vector
            def _(v):
                for n4 in range(4):
                    v.wait_ge(sTI, 2 + n4)
                    v.tensor_copy(rbsb[:, n4 * 1024:(n4 + 1) * 1024],
                                  rbps[:, :]).then_inc(sVI, 1)
                v.wait_ge(sTI, 6)
                for hp in range(NHP):
                    a0 = hp * 16
                    b0 = 128 + hp * 16
                    v.tensor_copy(Tg[0:64, a0 + 2 * hp:a0 + 2 * hp + 1], gw2ps[0:64, 0:1])
                    v.tensor_copy(Tg[64:128, a0 + 2 * hp + 1:a0 + 2 * hp + 2], gw2ps[64:128, 0:1])
                    v.tensor_copy(Tg[0:64, b0 + 2 * hp:b0 + 2 * hp + 1], gw2ps[0:64, 1:2])
                    cp = v.tensor_copy(Tg[64:128, b0 + 2 * hp + 1:b0 + 2 * hp + 2], gw2ps[64:128, 1:2])
                cp.then_inc(sVI, 1)
                v.wait_ge(sTI, 7)
                v.tensor_copy(gb2s[:, :], gb2ps[:, :]).then_inc(sVI, 1)
                v.reciprocal(invs_sb[:, :], scl[0:1, 0:1])
                v.tensor_mul(sws_sb[:, :], scl[0:1, 0:1], scl[0:1, 1:2]).then_inc(sVI, 1)
                v.wait_ge(sTI, 8)
                v.tensor_copy(gb32f[:, :], gb32ps[:, :])
                v.wait_ge(sTI, 9)
                v.tensor_copy(isc[:, :], iscps[:, :])
                v.tensor_scalar(bq8[:, :], bqraw[:, :], 0.125, None,
                                OP.mult).then_inc(sVI, 1)

    C["sWD"] = 16 * 21
    RBD_READY = C["sWD"]

    def ln_phase(b, which):
        src = xd if which == 0 else x1d
        with (
            nc.Block() as blk,
            nc.psum_tensor(f"sxp{b}_{which}", [1, 2048], F32) as sxp,
            nc.psum_tensor(f"sx2p{b}_{which}", [1, 2048], F32) as sx2p
        ):
            x0 = [C["sX0"], C["sX1"]]
            cb0 = C["sCB"]
            sq0 = C["sSQ"]
            st0 = C["sST"]
            sc0 = C["sSC"]
            bd0 = C["sBD"]

            @blk.gpsimd
            def _(g):
                if which == 1:
                    g.wait_ge(sXWp[0], 16 * 16 * (b + 1))
                    g.wait_ge(sXWp[1], 16 * 16 * (b + 1))
                for dc in range(NDC):
                    if dc >= 2:
                        g.wait_ge(sCB, cb0 + dc - 1)
                        g.wait_ge(sSQ, sq0 + dc - 1)
                    g.dma_start(xstage[dc % 2], src[b, dc, :, :]).then_inc(sPq := sXp[dc % 2], 16)
                g.wait_ge(sSC, sc0 + 1)
                g.dma_start(bass.AP(statscr, 0, [[2048, 1], [1, 2048]]), sstv).then_inc(sBD, 16)
                g.wait_ge(sSC, sc0 + 2)
                g.dma_start(bass.AP(statscr, 2048, [[2048, 1], [1, 2048]]), sstv).then_inc(sBD, 16)
                g.wait_ge(sBD, bd0 + 32)
                g.dma_start(bass.AP(dstat, 0, [[48, 128], [1, 16]]),
                            bass.AP(statscr, 0, [[16, 128], [1, 16]])).then_inc(sBD, 16)
                g.dma_start(bass.AP(dstat, 16, [[48, 128], [1, 16]]),
                            bass.AP(statscr, 2048, [[16, 128], [1, 16]])).then_inc(sBD, 16)
                g.wait_ge(sSC, sc0 + 3)
                g.dma_start(bass.AP(statscr, 0, [[16, 128], [1, 16]]),
                            dstat[:, 0:16]).then_inc(sBD, 16)
                g.wait_ge(sSC, sc0 + 4)
                g.dma_start(bass.AP(statscr, 2048, [[16, 128], [1, 16]]),
                            dstat[:, 32:48]).then_inc(sBD, 16)
                g.wait_ge(sBD, bd0 + 96)
                g.dma_start(mbc, bass.AP(statscr, 0, [[0, 128], [1, 2048]])).then_inc(sBD, 16)
                g.dma_start(ibc, bass.AP(statscr, 2048, [[0, 128], [1, 2048]])).then_inc(sBD, 16)
            C["sX0"] += 16 * (NDC // 2)
            C["sX1"] += 16 * (NDC // 2)
            C["sBD"] += 16 * 8

            @blk.vector
            def _(v):
                for dc in range(NDC):
                    v.wait_ge(sXp[dc % 2], x0[dc % 2] + 16 * (dc // 2 + 1))
                    if which == 1 and dc >= 1:
                        v.wait_ge(sST, st0 + dc)
                    tgt = xbf[:, dc * T:(dc + 1) * T] if which == 0 else sqst
                    v.tensor_copy(tgt, xstage[dc % 2]).then_inc(sCB, 1)
                v.wait_ge(sST, st0 + NDC)
                v.wait_ge(sDD, 16 * 11 * NHP * (b + which))
                v.tensor_copy(sstv, sxp[0:1, :]).then_inc(sSC, 1)
                v.wait_ge(sBD, bd0 + 16)
                v.tensor_copy(sstv, sx2p[0:1, :]).then_inc(sSC, 1)
                v.wait_ge(sBD, bd0 + 64)
                v.tensor_scalar(dstat[:, 0:16], dstat[:, 0:16], 1.0 / 1024.0, None,
                                OP.mult).then_inc(sVC, 1)
                v.tensor_scalar(dstat[:, 16:32], dstat[:, 16:32], 1.0 / 1024.0, 1e-5,
                                OP.mult, OP.add).then_inc(sVC, 1)
                v.wait_ge(sVC, C["sVC"] + 2)
                v.tensor_mul(dstat[:, 32:48], dstat[:, 0:16],
                             dstat[:, 0:16]).then_inc(sVC, 1)
                v.wait_ge(sVC, C["sVC"] + 3)
                v.tensor_sub(dstat[:, 16:32], dstat[:, 16:32],
                             dstat[:, 32:48]).then_inc(sVC, 1)
                v.wait_ge(sVC, C["sVC"] + 4)
                v.reciprocal(dstat[:, 32:48], dstat[:, 16:32]).then_inc(sSC, 1)
                v.wait_ge(sSQ, sq0 + NDC + 1)
                if which == 1:
                    v.tensor_scalar(dstat[:, 32:48], dstat[:, 32:48], isc[:, 0:1],
                                    None, OP.mult).then_inc(sSC, 1)
                else:
                    v.tensor_copy(dumm[:, :], dumm[:, :]).then_inc(sSC, 1)
            C["sCB"] += NDC
            C["sSC"] += 4
            C["sVC"] += 4

            @blk.scalar
            def _(s):
                for dc in range(NDC):
                    s.wait_ge(sXp[dc % 2], x0[dc % 2] + 16 * (dc // 2 + 1))
                    if dc >= 1:
                        s.wait_ge(sST, st0 + dc)
                    tgt = sqst if which == 0 else sq2
                    s.activation(tgt, xstage[dc % 2], AF.Square).then_inc(sSQ, 1)
                s.wait_ge(sSC, sc0 + 3)
                s.activation(dstat[:, 32:48], dstat[:, 32:48], AF.Sqrt).then_inc(sSQ, 1)
            C["sSQ"] += NDC + 1

            @blk.tensor
            def _(t):
                for dc in range(NDC):
                    t.wait_ge(sCB, cb0 + dc + 1)
                    t.wait_ge(sSQ, sq0 + dc + 1)
                    xsrc = xbf[:, dc * T:(dc + 1) * T] if which == 0 else sqst
                    qsrc = sqst if which == 0 else sq2
                    for sl in range(4):
                        t.matmul(sxp[0:1, sl * 512:(sl + 1) * 512], onesb[:, :],
                                 xsrc[:, sl * 512:(sl + 1) * 512],
                                 start=(dc == 0), stop=(dc == NDC - 1))
                    for sl in range(4):
                        mm = t.matmul(sx2p[0:1, sl * 512:(sl + 1) * 512], onesb[:, :],
                                      qsrc[:, sl * 512:(sl + 1) * 512],
                                      start=(dc == 0), stop=(dc == NDC - 1))
                    mm.then_inc(sST, 1)
            C["sST"] += NDC

        with nc.Block() as blk:
            x0 = [C["sX0"], C["sX1"]]
            ap0 = C["sAP"]
            bd_all = C["sBD"]

            apc_g = 2 if which == 0 else 3

            @blk.gpsimd
            def _(g):
                for dc in range(NDC):
                    if dc >= 2:
                        g.wait_ge(sAP, ap0 + apc_g * (dc - 2) + 1)
                    g.dma_start(xstage[dc % 2], src[b, dc, :, :]).then_inc(sXp[dc % 2], 16)
            C["sX0"] += 16 * (NDC // 2)
            C["sX1"] += 16 * (NDC // 2)

            @blk.vector
            def _(v):
                v.wait_ge(sBD, bd_all)
                for dc in range(NDC):
                    v.wait_ge(sXp[dc % 2], x0[dc % 2] + 16 * (dc // 2 + 1))
                    if dc >= 1:
                        v.wait_ge(sAP, ap0 + apc_g * (dc - 1) + 2)
                    v.tensor_sub(tstt[0], xstage[dc % 2], mbc).then_inc(sAP, 1)
                    v.wait_ge(sAP, ap0 + apc_g * dc + 1)
                    if which == 0:
                        v.tensor_mul(xbf[:, dc * T:(dc + 1) * T], tstt[0],
                                     ibc).then_inc(sAP, 1)
                    else:
                        v.tensor_mul(tstt[1], tstt[0], ibc).then_inc(sAP, 1)
                        v.wait_ge(sAP, ap0 + apc_g * dc + 2)
                        v.tensor_scalar(z8v[:, dc * T:(dc + 1) * T], tstt[1],
                                        448.0, -448.0, OP.min,
                                        OP.max).then_inc(sAP, 1)
            C["sAP"] += apc_g * NDC

    def vg_phase(b):
        with (
            nc.Block() as blk,
            nc.psum_tensor(f"vps0_{b}", [128, 512], F32) as vps0,
            nc.psum_tensor(f"vps1_{b}", [128, 512], F32) as vps1,
            nc.psum_tensor(f"gapsA_{b}", [16, 1024], F32) as gapsA,
            nc.psum_tensor(f"gapsB_{b}", [16, 1024], F32) as gapsB
        ):
            vps = [vps0, vps1]
            qk0 = C["sQK"]
            ev0 = C["sEV"]
            ge0 = C["sGE"]
            wd0 = C["sWD"]

            @blk.gpsimd
            def _(g):
                g.wait_ge(sWD, RBD_READY)
                g.memset(vT[:, :], 0.0).then_inc(sGE, 1)
                g.dma_start(wvsv, wvd[:, :]).then_inc(sWD, 16)
                g.wait_ge(sGE, ge0 + 3)
                g.dma_start(gatescr[:, :], gate16[:, :]).then_inc(sGS, 16)
            C["sWD"] += 16
            C["sGS"] += 16

            @blk.vector
            def _(v):
                v.wait_ge(sGE, ge0 + 1)
                v.memset(vTr[:, :, :, 64:65], 1.0)
                v.memset(vTr[:, :, :, 97:98], 1.0).then_inc(sGE, 1)
                v.wait_ge(sEV, ev0 + 2)
                v.tensor_scalar(tstt[0][0:16, 0:T], gsigB[0:16, 0:T], gcs[:, :], -1.0,
                                OP.mult, OP.add).then_inc(sVC, 1)
                v.wait_ge(sVC, C["sVC"] + 1)
                v.tensor_mul(tstt[1][0:16, 0:T], tstt[0][0:16, 0:T],
                             gsigA[0:16, 0:T]).then_inc(sVC, 1)
                v.wait_ge(sVC, C["sVC"] + 2)
                v.tensor_scalar(gate16[:, :], tstt[1][0:16, 0:T], 2.0, None,
                                OP.add).then_inc(sGE, 1)
            C["sGE"] += 3
            C["sVC"] += 2

            @blk.tensor
            def _(t):
                t.wait_ge(sWD, RBD_READY)
                for gh in range(2):
                    if gh == 1:
                        t.wait_ge(sEV, ev0 + 1)
                    for sl in range(2):
                        for hp in range(NHP):
                            xsl = xbf[:, hp * T + gh * 1024 + sl * 512:
                                      hp * T + gh * 1024 + (sl + 1) * 512]
                            t.matmul(gapsA[:, sl * 512:(sl + 1) * 512],
                                     Tg[:, hp * 16:(hp + 1) * 16], xsl,
                                     start=(hp == 0), stop=(hp == NHP - 1))
                            mm = t.matmul(gapsB[:, sl * 512:(sl + 1) * 512],
                                          Tg[:, 128 + hp * 16:128 + (hp + 1) * 16], xsl,
                                          start=(hp == 0), stop=(hp == NHP - 1))
                    mm.then_inc(sQK, 1)
                t.wait_ge(sWD, wd0 + 16)
                for tb in range(NS0):
                    for th in range(2):
                        k = 2 * tb + th
                        if k >= 2:
                            t.wait_ge(sEV, ev0 + 2 * k)
                        for dc in range(NDC):
                            t.matmul(vps[k % 2][:, :],
                                     xbf[:, dc * T + tb * 128:dc * T + (tb + 1) * 128],
                                     wvsv[:, dc * 1024 + th * 512:dc * 1024 + (th + 1) * 512],
                                     start=(dc == 0), stop=False)
                        t.matmul(vps[k % 2][:, :], ones1b[:, :],
                                 vbr[0:1, th * 512:(th + 1) * 512],
                                 start=False, stop=True).then_inc(sQK, 1)
            C["sQK"] += 2 + NS0 * 2

            @blk.scalar
            def _(s):
                for gh in range(2):
                    s.wait_ge(sQK, qk0 + gh + 1)
                    s.activation(gsigA[0:16, gh * 1024:(gh + 1) * 1024], gapsA[:, :],
                                 AF.Sigmoid, bias=gb32f[:, 0:1])
                    s.activation(gsigB[0:16, gh * 1024:(gh + 1) * 1024], gapsB[:, :],
                                 AF.Sigmoid, bias=gb32f[:, 1:2]).then_inc(sEV, 1)
                s.wait_ge(sGE, ge0 + 2)
                for tb in range(NS0):
                    for th in range(2):
                        k = 2 * tb + th
                        s.wait_ge(sQK, qk0 + 2 + k + 1)
                        vsrc = vps[k % 2][:, :].rearrange("p (m x d) -> p m x d",
                                                          m=4, x=2, d=64)
                        s.activation(vTr[:, tb, th * 4:(th + 1) * 4, 0:64],
                                     vsrc[:, :, 0, :], AF.Identity)
                        s.activation(vTr[:, tb, th * 4:(th + 1) * 4, 129:193],
                                     vsrc[:, :, 1, :], AF.Identity).then_inc(sEV, 2)
            C["sEV"] += 2 + NS0 * 2 * 2

    def attn_phase(b):
        with (
            nc.Block() as blk,
            nc.psum_tensor(f"scp0_{b}", [128, 1024], F32) as scp0,
            nc.psum_tensor(f"scp1_{b}", [128, 1024], F32) as scp1,
            nc.psum_tensor(f"cxA0_{b}", [65, 512], F32) as cxA0,
            nc.psum_tensor(f"cxB0_{b}", [128, 512], F32) as cxB0,
            nc.psum_tensor(f"cxA1_{b}", [65, 512], F32) as cxA1,
            nc.psum_tensor(f"cxB1_{b}", [128, 512], F32) as cxB1
        ):
            scp = [scp0, scp1]
            cxA = [cxA0, cxA1]
            cxB = [cxB0, cxB1]
            qk0 = C["sQK"]
            ev0 = C["sEV"]
            wq0 = [C["sWq0"], C["sWq1"]]
            wk0 = [C["sWk0"], C["sWk1"]]
            gd0 = [C["sGD0"], C["sGD1"]]
            p0 = [C["sP0"], C["sP1"]]
            ap_all = C["sAP"]
            cb_all = C["sCB"]
            sq_all = C["sSQ"]

            def emit_av(t, il):
                hp = il // (NTQ * NS0)
                tq = (il // NS0) % NTQ
                s0 = il % NS0
                i_g = ig(b, il)
                slot = jslot(b, hp, tq)
                t.wait_ge(sE, i_g + 1)
                if s0 == 0 and slot > 1:
                    t.wait_ge(sV, slot - 1)
                    t.wait_ge(sVS, slot - 1)
                base = s0 * 1544 + hp * 193
                t.matmul(cxA[slot % 2][:, :], vT[:, base:base + 65], exb[il % 3][:, 0:512],
                         start=(s0 == 0), stop=(s0 == NS0 - 1))
                t.matmul(cxB[slot % 2][:, :], vT[:, base + 65:base + 193], exb[il % 3][:, 512:1024],
                         start=(s0 == 0), stop=(s0 == NS0 - 1)).then_inc(sA, 1)

            @blk.gpsimd
            def _(g):
                def load_pair(hp):
                    g.dma_start(gbct[hp % 2][:, :].rearrange("p (h j) -> p h j", h=2, j=2048),
                                bass.AP(gatescr, (2 * hp) * 2048,
                                        [[0, 128], [2048, 2], [1, 2048]])).then_inc(sGDp[hp % 2], 16)
                    g.dma_start(wqs[hp % 2][:, :].rearrange("p (c d) -> p c d", c=8, d=128),
                                bass.AP(wqd, hp * 128,
                                        [[8192, 128], [1024, 8], [1, 128]])).then_inc(sWqp[hp % 2], 16)
                    g.dma_start(wks[hp % 2][:, :].rearrange("p (c d) -> p c d", c=8, d=128),
                                bass.AP(wkd, hp * 128,
                                        [[8192, 128], [1024, 8], [1, 128]])).then_inc(sWkp[hp % 2], 16)

                g.wait_ge(sGS, 16 * (b + 1))
                load_pair(0)
                load_pair(1)
                for hp in range(NHP):
                    hpg = b * NHP + hp
                    if hp + 2 < NHP:
                        nhp = hp + 2
                        g.wait_ge(sG, ig(b, (hp + 1) * NTQ * NS0))
                        g.wait_ge(sQK, qk0 + 4 * (nhp - 1))
                        load_pair(nhp)
                    g.wait_ge(sVS, jslot(b, hp, NTQ - 1) + 1)
                    g.wait_ge(sV, jslot(b, hp, NTQ - 1) + 1)
                    if hpg > 0:
                        g.wait_ge(sDD, 16 * (11 * (hpg - 1) + 11))
                        g.wait_ge(sN, 4 * hpg)
                    g.dma_start(bass.AP(denscr, 0, [[4096, 1], [1, 4096]]),
                                bass.AP(dstg, 32 * 2048, [[32 * 2048, 2], [1, 2048]])).then_inc(sDD, 16)
                    g.wait_ge(sDD, 16 * (11 * hpg + 1))
                    g.dma_start(bass.AP(d128, 0, [[32, 128], [1, 32]]),
                                bass.AP(denscr, 0, [[32, 128], [1, 32]])).then_inc(sDD, 16)
                    g.wait_ge(sR, hpg + 1)
                    g.dma_start(bass.AP(denscr, 0, [[32, 128], [1, 32]]),
                                d128r[:, :]).then_inc(sDD, 16)
                    g.wait_ge(sDD, 16 * (11 * hpg + 3))
                    for idx in range(4):
                        g.dma_start(bcps[0:64, idx * 512:(idx + 1) * 512],
                                    bass.AP(denscr, 2048 + idx * 512, [[0, 64], [1, 512]])).then_inc(sDD, 16)
                        g.dma_start(bcps[64:128, idx * 512:(idx + 1) * 512],
                                    bass.AP(denscr, idx * 512, [[0, 64], [1, 512]])).then_inc(sDD, 16)
            C["sWq0"] += 16 * (NHP // 2)
            C["sWq1"] += 16 * (NHP // 2)
            C["sWk0"] += 16 * (NHP // 2)
            C["sWk1"] += 16 * (NHP // 2)
            C["sGD0"] += 16 * (NHP // 2)
            C["sGD1"] += 16 * (NHP // 2)
            C["sDD"] += 16 * 11 * NHP

            @blk.sync
            def _(sy):
                sy.wait_ge(sWD, RBD_READY)
                sy.wait_ge(sCB, cb_all)
                sy.wait_ge(sSQ, sq_all)
                sy.wait_ge(sAP, ap_all)
                for ch in range(NB_IL // 4):
                    il0 = ch * 4
                    hp = il0 // (NTQ * NS0)
                    tq = (il0 // NS0) % NTQ
                    s0 = il0 % NS0
                    if ch >= 2:
                        sy.wait_ge(sG, ig(b, 4 * ch - 4))
                    cp0 = 2048 + s0 * 128 - tq * 512 - 512
                    for h in range(2):
                        sy.dma_start(pbb[ch % 2][:, h * 2048:(h + 1) * 2048]
                                     .rearrange("p (q j) -> p q j", q=4, j=512),
                                     bass.AP(rbd, (2 * hp + h) * 4096 + cp0,
                                             [[1, 128], [128, 4], [1, 512]])).then_inc(sPp[ch % 2], 16)

            @blk.tensor
            def _(t):
                for hp in range(NHP):
                    hpg = b * NHP + hp
                    t.wait_ge(sWqp[hp % 2], wq0[hp % 2] + 16 * (hp // 2 + 1))
                    t.wait_ge(sWkp[hp % 2], wk0[hp % 2] + 16 * (hp // 2 + 1))
                    if hp == 0:
                        t.wait_ge(sAP, ap_all)
                    prev_last = ig(b, hp * NTQ * NS0)
                    if prev_last > 0:
                        t.wait_ge(sE, prev_last)
                    for which, wsl in ((0, wks), (1, wqs)):
                        if which == 1:
                            t.wait_ge(sEV, ev0 + 2 * hp + 1)
                        for half in range(2):
                            for dc in range(NDC):
                                for sl in range(2):
                                    mm = t.matmul(scp[half][:, sl * 512:(sl + 1) * 512],
                                                  wsl[hp % 2][:, dc * 128:(dc + 1) * 128],
                                                  xbf[:, dc * T + half * 1024 + sl * 512:
                                                      dc * T + half * 1024 + (sl + 1) * 512],
                                                  start=(dc == 0), stop=(dc == NDC - 1))
                            mm.then_inc(sQK, 1)
                    for tq in range(NTQ):
                        for s0 in range(NS0):
                            il = (hp * NTQ + tq) * NS0 + s0
                            i_g = ig(b, il)
                            if i_g >= 2:
                                t.wait_ge(sE, i_g - 1)
                            if tq == 0 and s0 == 0:
                                t.wait_ge(sEV, ev0 + 2 * hp + 2)
                            t.matmul(scp[il % 2][:, 0:512],
                                     kqp[0:64, s0 * 128:(s0 + 1) * 128],
                                     kqp[0:64, T + tq * 512:T + (tq + 1) * 512],
                                     start=True, stop=False, tile_position=(0, 0))
                            t.matmul(scp[il % 2][:, 512:1024],
                                     kqp[64:128, s0 * 128:(s0 + 1) * 128],
                                     kqp[64:128, T + tq * 512:T + (tq + 1) * 512],
                                     start=True, stop=False, tile_position=(64, 0))
                            if il > 1:
                                emit_av(t, il - 2)
                            t.wait_ge(sG, i_g + 1)
                            t.matmul(scp[il % 2][:, 0:512], i128s[:, :],
                                     Gb[il % 2][:, 0:512], start=False, stop=True)
                            t.matmul(scp[il % 2][:, 512:1024], i128s[:, :],
                                     Gb[il % 2][:, 512:1024],
                                     start=False, stop=True).then_inc(sS, 1)
                emit_av(t, NB_IL - 2)
                emit_av(t, NB_IL - 1)
            for _i in range(2):
                C[f"sP{_i}"] += 32 * (NB_IL // 8)
            C["sQK"] += 4 * NHP

            @blk.scalar
            def _(s):
                den_pending = []

                for hp in range(NHP):
                    s.wait_ge(sQK, qk0 + 4 * hp + 2)
                    s.activation(kqp[:, 0:1024], scp0[:, :], AF.Identity,
                                 bias=bks[:, hp:hp + 1])
                    s.activation(kqp[:, 1024:2048], scp1[:, :], AF.Identity,
                                 bias=bks[:, hp:hp + 1]).then_inc(sEV, 1)
                    s.wait_ge(sQK, qk0 + 4 * hp + 4)
                    s.activation(kqp[:, T:T + 1024], scp0[:, :], AF.Identity,
                                 bias=bq8[:, hp:hp + 1], scale=eighth[:, :])
                    s.activation(kqp[:, T + 1024:2 * T], scp1[:, :], AF.Identity,
                                 bias=bq8[:, hp:hp + 1], scale=eighth[:, :]).then_inc(sEV, 1)
                    for tq in range(NTQ):
                        for s0 in range(NS0):
                            il = (hp * NTQ + tq) * NS0 + s0
                            i_g = ig(b, il)
                            if s0 == 3 and den_pending:
                                hp_e, tq_e, igl = den_pending[0]
                                sl_e = jslot(b, hp_e, tq_e)
                                s.wait_ge(sA, igl + 1)
                                s.activation(ctxu[hp_e % 2][64:128, tq_e * 512:(tq_e + 1) * 512],
                                             cxB[sl_e % 2][64:128, :], AF.Identity)
                            if s0 == 5 and den_pending:
                                hp_e, tq_e, igl = den_pending.pop(0)
                                hpg_e = b * NHP + hp_e
                                sl_e = jslot(b, hp_e, tq_e)
                                if tq_e == 0 and hpg_e > 0:
                                    s.wait_ge(sDD, 16 * (11 * (hpg_e - 1) + 1))
                                s.activation(dstg[32:33, tq_e * 512:(tq_e + 1) * 512],
                                             cxB[sl_e % 2][32:33, :], AF.Identity).then_inc(sVS, 1)
                            s.wait_ge(sS, i_g + 1)
                            if i_g >= 3:
                                s.wait_ge(sA, i_g - 2)
                            s.activation(exb[il % 3][:, :], scp[il % 2][:, :], AF.Exp,
                                         bias=ebias[:, :]).then_inc(sE, 1)
                            if s0 == NS0 - 1:
                                den_pending.append((hp, tq, i_g))
                while den_pending:
                    hp_e, tq_e, igl = den_pending.pop(0)
                    sl_e = jslot(b, hp_e, tq_e)
                    s.wait_ge(sA, igl + 1)
                    s.activation(ctxu[hp_e % 2][64:128, tq_e * 512:(tq_e + 1) * 512],
                                 cxB[sl_e % 2][64:128, :], AF.Identity)
                    s.activation(dstg[32:33, tq_e * 512:(tq_e + 1) * 512],
                                 cxB[sl_e % 2][32:33, :], AF.Identity).then_inc(sVS, 1)
            C["sEV"] += 2 * NHP

            @blk.vector
            def _(v):
                v.wait_ge(sXWp[0], 16 * 16 * b)
                v.wait_ge(sXWp[1], 16 * 16 * b)
                v.wait_ge(sOp[0], 16 * 16 * b)
                v.wait_ge(sOp[1], 16 * 16 * b)
                pending = []

                def emit_evac(vv, step):
                    hp_e, tq_e, igl = pending[0]
                    hpg_e = b * NHP + hp_e
                    sl = jslot(b, hp_e, tq_e)
                    cu = ctxu[hp_e % 2]
                    c0 = tq_e * 512
                    if step == 0:
                        vv.wait_ge(sA, igl + 1)
                        vv.tensor_copy(cu[0:64, c0:c0 + 512], cxA[sl % 2][0:64, :])
                    else:
                        if tq_e == 0 and hpg_e > 0:
                            vv.wait_ge(sDD, 16 * (11 * (hpg_e - 1) + 1))
                        vv.tensor_copy(dstg[64:65, c0:c0 + 512],
                                       cxA[sl % 2][64:65, :]).then_inc(sV, 1)
                        pending.pop(0)

                for hp in range(NHP):
                    hpg = b * NHP + hp
                    for tq in range(NTQ):
                        for s0 in range(NS0):
                            il = (hp * NTQ + tq) * NS0 + s0
                            i_g = ig(b, il)
                            if s0 == 2 and pending:
                                emit_evac(v, 0)
                            if s0 == 4 and pending:
                                emit_evac(v, 1)
                            if tq == 0 and s0 == 7 and hp > 0:
                                v.wait_ge(sDD, 16 * (11 * (hpg - 1) + 2))
                                v.reciprocal(d128r[:, :], d128[:, :]).then_inc(sR, 1)
                            if hp > 0 and tq == 1 and s0 in (2, 6, 10, 14):
                                idx = {2: 0, 6: 1, 10: 2, 14: 3}[s0]
                                hpp = hp - 1
                                v.wait_ge(sDD, 16 * (11 * (hpg - 1) + 11))
                                v.wait_ge(sV, jslot(b, hpp, NTQ - 1) + 1)
                                v.wait_ge(sVS, jslot(b, hpp, NTQ - 1) + 1)
                                v.tensor_mul(
                                    ctxn[:, hpp * T + idx * 512:hpp * T + (idx + 1) * 512],
                                    ctxu[hpp % 2][:, idx * 512:(idx + 1) * 512],
                                    bcps[:, idx * 512:(idx + 1) * 512]).then_inc(sN, 1)
                            ch = il // 4
                            v.wait_ge(sPp[ch % 2], p0[ch % 2] + 32 * (ch // 2 + 1))
                            if il == 0 or (s0 == 0 and tq == 0):
                                v.wait_ge(sGDp[hp % 2], gd0[hp % 2] + 16 * (hp // 2 + 1))
                            if i_g >= 2:
                                v.wait_ge(sS, i_g - 1)
                            rev = bass.AP(pbb[ch % 2].tensor, (il % 4) * 512 + 511,
                                          [[4096, 128], [2048, 2], [-1, 512]])
                            gsl = bass.AP(gbct[hp % 2], tq * 512,
                                          [[4096, 128], [2048, 2], [1, 512]])
                            v.tensor_mul(Gb[il % 2][:, :], rev, gsl).then_inc(sG, 1)
                            if s0 == NS0 - 1:
                                pending.append((hp, tq, i_g))
                while pending:
                    emit_evac(v, 0)
                    emit_evac(v, 1)
                hpgl = b * NHP + NHP - 1
                v.wait_ge(sDD, 16 * (11 * hpgl + 2))
                v.reciprocal(d128r[:, :], d128[:, :]).then_inc(sR, 1)
                v.wait_ge(sDD, 16 * (11 * hpgl + 11))
                for idx in range(4):
                    v.tensor_mul(ctxn[:, (NHP - 1) * T + idx * 512:(NHP - 1) * T + (idx + 1) * 512],
                                 ctxu[(NHP - 1) % 2][:, idx * 512:(idx + 1) * 512],
                                 bcps[:, idx * 512:(idx + 1) * 512]).then_inc(sN, 1)

    def out_phase(b):
        with (
            nc.Block() as blk,
            nc.psum_tensor(f"oup0_{b}", [128, 512], F32) as oup0,
            nc.psum_tensor(f"oup1_{b}", [128, 512], F32) as oup1
        ):
            oup = [oup0, oup1]
            ot0 = C["sOT"]
            oe0 = C["sOE"]
            rv0 = C["sRV"]
            x0 = [C["sX0"], C["sX1"]]
            xw0 = [C["sXW0"], C["sXW1"]]
            wd0 = C["sWD"]

            @blk.gpsimd
            def _(g):
                g.dma_start(wosv, wod[:, :]).then_inc(sWD, 16)
                for j in range(32):
                    tqq, dcO = divmod(j, 8)
                    if j >= 2:
                        g.wait_ge(sRV, rv0 + j - 1)
                    g.dma_start(xrst[j % 2],
                                xd[b, dcO, :, tqq * 512:(tqq + 1) * 512]).then_inc(sXp[j % 2], 16)
            C["sWD"] += 16
            C["sX0"] += 16 * 16
            C["sX1"] += 16 * 16

            @blk.tensor
            def _(t):
                t.wait_ge(sWD, wd0 + 16)
                for j in range(32):
                    tqq, dcO = divmod(j, 8)
                    if j >= 2:
                        t.wait_ge(sOE, oe0 + j - 1)
                    for hp in range(NHP):
                        mm = t.matmul(oup[j % 2][:, :],
                                      wosv[:, (hp * 8 + dcO) * 128:(hp * 8 + dcO + 1) * 128],
                                      ctxn[:, hp * T + tqq * 512:hp * T + (tqq + 1) * 512],
                                      start=(hp == 0), stop=(hp == NHP - 1))
                    mm.then_inc(sOT, 1)
            C["sOT"] += 32

            @blk.scalar
            def _(s):
                for j in range(32):
                    tqq, dcO = divmod(j, 8)
                    s.wait_ge(sOT, ot0 + j + 1)
                    if j >= 2:
                        s.wait_ge(sRV, rv0 + j - 1)
                    s.activation(obuf[j % 2], oup[j % 2][:, :], AF.Identity,
                                 bias=bos[:, dcO:dcO + 1]).then_inc(sOE, 1)
            C["sOE"] += 32

            @blk.vector
            def _(v):
                for j in range(32):
                    v.wait_ge(sOE, oe0 + j + 1)
                    v.wait_ge(sXp[j % 2], x0[j % 2] + 16 * (j // 2 + 1))
                    if j >= 2:
                        v.wait_ge(sXWp[j % 2], xw0[j % 2] + 16 * (j // 2))
                    v.tensor_add(o2buf[j % 2], obuf[j % 2],
                                 xrst[j % 2]).then_inc(sRV, 1)
            C["sRV"] += 32

            @blk.sync
            def _(sy):
                for j in range(32):
                    tqq, dcO = divmod(j, 8)
                    sy.wait_ge(sRV, rv0 + j + 1)
                    sy.dma_start(x1d[b, dcO, :, tqq * 512:(tqq + 1) * 512],
                                 o2buf[j % 2]).then_inc(sXWp[j % 2], 16)
            C["sXW0"] += 16 * 16
            C["sXW1"] += 16 * 16

    def ffn_phase(b):
        with (
            nc.Block() as blk,
            nc.psum_tensor(f"fps0_{b}", [128, 512], F32) as fps0,
            nc.psum_tensor(f"fps1_{b}", [128, 512], F32) as fps1
        ):
            fps = [fps0, fps1]
            ot0 = C["sOT"]
            oe0 = C["sOE"]
            rv0 = C["sRV"]
            x0 = [C["sX0"], C["sX1"]]
            o0 = [C["sO0"], C["sO1"]]
            wd0 = C["sWD"]

            @blk.gpsimd
            def _(g):
                g.dma_start(w8sv, w8d[:, :]).then_inc(sWD, 16)
                for j in range(32):
                    tqq, dcO = divmod(j, 8)
                    if j >= 2:
                        g.wait_ge(sRV, rv0 + j - 1)
                    g.dma_start(xrst[j % 2],
                                x1d[b, dcO, :, tqq * 512:(tqq + 1) * 512]).then_inc(sXp[j % 2], 16)
            C["sWD"] += 16
            C["sX0"] += 16 * 16
            C["sX1"] += 16 * 16

            @blk.tensor
            def _(t):
                t.wait_ge(sWD, wd0 + 16)
                for j in range(32):
                    tqq, dcO = divmod(j, 8)
                    if j >= 2:
                        t.wait_ge(sOE, oe0 + j - 1)
                    for dc in range(NDC):
                        mm = t.matmul(fps[j % 2][:, :],
                                      w8sv[:, dc * 1024 + dcO * 128:dc * 1024 + (dcO + 1) * 128],
                                      z8v[:, dc * T + tqq * 512:dc * T + (tqq + 1) * 512],
                                      start=(dc == 0), stop=(dc == NDC - 1))
                    mm.then_inc(sOT, 1)
            C["sOT"] += 32

            @blk.scalar
            def _(s):
                for j in range(32):
                    tqq, dcO = divmod(j, 8)
                    s.wait_ge(sOT, ot0 + j + 1)
                    if j >= 2:
                        s.wait_ge(sRV, rv0 + j - 1)
                    s.activation(obuf[j % 2], fps[j % 2][:, :], AF.Identity,
                                 bias=bfs[:, dcO:dcO + 1], scale=isc[:, 1:2]).then_inc(sOE, 1)
            C["sOE"] += 32

            @blk.vector
            def _(v):
                for j in range(32):
                    v.wait_ge(sOE, oe0 + j + 1)
                    v.wait_ge(sXp[j % 2], x0[j % 2] + 16 * (j // 2 + 1))
                    if j >= 2:
                        v.wait_ge(sOp[j % 2], o0[j % 2] + 16 * (j // 2))
                    v.tensor_add(o2buf[j % 2], obuf[j % 2],
                                 xrst[j % 2]).then_inc(sRV, 1)
            C["sRV"] += 32

            @blk.sync
            def _(sy):
                for j in range(32):
                    tqq, dcO = divmod(j, 8)
                    sy.wait_ge(sRV, rv0 + j + 1)
                    sy.dma_start(outd[b, dcO, :, tqq * 512:(tqq + 1) * 512],
                                 o2buf[j % 2]).then_inc(sOp[j % 2], 16)
            C["sO0"] += 16 * 16
            C["sO1"] += 16 * 16

    import os
    nph = int(os.environ.get("ATT_NPH", "999"))
    done = [0]

    def _run(f, *a):
        if done[0] < nph:
            f(*a)
            done[0] += 1

    for b in range(B_C):
        _run(ln_phase, b, 0)
        _run(vg_phase, b)
        _run(attn_phase, b)
        _run(out_phase, b)
        _run(ln_phase, b, 1)
        _run(ffn_phase, b)

    if nph >= 6 * B_C:
        with nc.Block() as blk:
            @blk.sync
            def _(sy):
                sy.wait_ge(sXWp[0], 16 * 16 * B_C)
                sy.wait_ge(sXWp[1], 16 * 16 * B_C)
                sy.wait_ge(sOp[0], 16 * 16 * B_C)
                sy.wait_ge(sOp[1], 16 * 16 * B_C)

    return nc


_CACHE = {}


def _consts():
    if "oht" not in _CACHE:
        bt = _bucket_table()
        oh = np.zeros((320, 4096), dtype=BF)
        oh[bt, np.arange(4095)] = 1.0
        _CACHE["oht"] = oh
    oh = _CACHE["oht"]
    i128 = np.ascontiguousarray(np.eye(128, dtype=BF))
    selR = np.zeros((2, 128), dtype=BF)
    selR[1, 0:64] = 1.0
    selR[0, 64:128] = 1.0
    sel232 = np.zeros((2, 32), dtype=BF)
    sel232[0, 0:16] = 1.0
    sel232[1, 16:32] = 1.0
    p82 = np.zeros((8, 2), dtype=BF)
    p82[0:4, 0] = 1.0
    p82[4:8, 1] = 1.0
    return oh, i128, selR, sel232, p82


def _pack_w(w):
    return np.ascontiguousarray(
        np.asarray(w, np.float32).T.reshape(8, 128, 1024).transpose(1, 0, 2).reshape(128, 8192))


def _pack_wo(w):
    return np.ascontiguousarray(
        np.asarray(w, np.float32).T.reshape(8, 128, 8, 128).transpose(1, 0, 2, 3).reshape(128, 8192))


def kernel(L0_attn_hidden_states, L0_attn_q_w, L0_attn_q_b, L0_attn_k_w, L0_attn_k_b,
           L0_attn_v_w, L0_attn_v_b, L0_attn_out_w, L0_attn_out_b, L0_attn_rel_embed,
           L0_attn_gru_const, L0_attn_gru_w, L0_attn_gru_b, L0_ffn_weight_fp8,
           L0_ffn_bias, L0_ffn_input_scale, L0_ffn_weight_scale):
    x = np.asarray(L0_attn_hidden_states, dtype=np.float32)
    oh, i128, selR, sel232, p82 = _consts()

    shared = {
        "wq": _pack_w(L0_attn_q_w),
        "wk": _pack_w(L0_attn_k_w),
        "wv": _pack_w(L0_attn_v_w),
        "wo": _pack_wo(L0_attn_out_w),
        "w8": _pack_w(L0_ffn_weight_fp8),
        "bq": np.ascontiguousarray(np.asarray(L0_attn_q_b, np.float32).reshape(8, 128).T),
        "bk": np.ascontiguousarray(np.asarray(L0_attn_k_b, np.float32).reshape(8, 128).T),
        "bvr": np.ascontiguousarray(np.asarray(L0_attn_v_b, np.float32)[None, :]),
        "bo": np.ascontiguousarray(np.asarray(L0_attn_out_b, np.float32).reshape(8, 128).T),
        "bfn": np.ascontiguousarray(np.asarray(L0_ffn_bias, np.float32).reshape(8, 128).T),
        "rel": np.ascontiguousarray(np.asarray(L0_attn_rel_embed, np.float32)),
        "oht": oh,
        "gw": np.ascontiguousarray(np.asarray(L0_attn_gru_w, np.float32)),
        "gbi": np.ascontiguousarray(np.asarray(L0_attn_gru_b, np.float32).reshape(8, 1)),
        "gci": np.ascontiguousarray(np.asarray(L0_attn_gru_const, np.float32).reshape(16, 1)),
        "scli": np.array([[float(np.asarray(L0_ffn_input_scale).ravel()[0]),
                           float(np.asarray(L0_ffn_weight_scale).ravel()[0])]], np.float32),
        "i128": i128, "selR": selR, "sel232": sel232, "p82": p82,
    }
    xt = np.ascontiguousarray(x.reshape(32, T, 8, 128).transpose(0, 2, 3, 1))

    nc = build_nc()
    in_maps = []
    for c in range(8):
        m = dict(shared)
        m["x"] = np.ascontiguousarray(xt[4 * c:4 * c + 4])
        in_maps.append(m)
    res = run_bass_kernel_spmd(nc, in_maps, core_ids=list(range(8)))
    outs = []
    for c in range(8):
        o = np.asarray(res.results[c]["out"])
        outs.append(o.reshape(4, 1024, T).transpose(0, 2, 1))
    return np.ascontiguousarray(np.concatenate(outs, axis=0), dtype=np.float32)


# version 11
# speedup vs baseline: 1.5271x; 1.5271x over previous
import contextlib

import numpy as np
import ml_dtypes

import concourse.bass as bass
import concourse.mybir as mybir
from concourse.bass_utils import run_bass_kernel_spmd

F32 = mybir.dt.float32
BF16 = mybir.dt.bfloat16
FP8 = mybir.dt.float8e4
U8 = mybir.dt.uint8
AF = mybir.ActivationFunctionType
OP = mybir.AluOpType
BF = ml_dtypes.bfloat16

B_C = 4
T = 2048
NHP = 8
NTQ = 4
TQ = 512
NS0 = 16
NDC = 8
EXPB = -4.0
NB_IL = NHP * NTQ * NS0
GDPB = 16 + 16 * 2 * NHP

NUM_BUCKETS = 320
MAX_DISTANCE = 800


def _bucket_table():
    import math
    d = np.arange(-(T - 1), T)
    nb = NUM_BUCKETS // 2
    rel_b = (d > 0).astype(np.int64) * nb
    rp = np.abs(d)
    me = nb // 2
    small = rp < me
    rl = np.log(np.maximum(rp, 1).astype(np.float32) / me)
    rl = rl / math.log(MAX_DISTANCE / me)
    rl = rl * (nb - me)
    rl = (me + rl).astype(np.int64)
    rl = np.minimum(rl, nb - 1)
    return rel_b + np.where(small, rp, rl)


def build_nc():
    nc = bass.Bass("TRN2", target_bir_lowering=False)

    xd = nc.dram_tensor("x", [B_C, NDC, 128, T], F32, kind="ExternalInput")
    wqd = nc.dram_tensor("wq", [128, 8192], F32, kind="ExternalInput")
    wkd = nc.dram_tensor("wk", [128, 8192], F32, kind="ExternalInput")
    wvd = nc.dram_tensor("wv", [128, 8192], F32, kind="ExternalInput")
    wod = nc.dram_tensor("wo", [128, 8192], F32, kind="ExternalInput")
    w8d = nc.dram_tensor("w8", [128, 8192], F32, kind="ExternalInput")
    bqd = nc.dram_tensor("bq", [128, 8], F32, kind="ExternalInput")
    bkd = nc.dram_tensor("bk", [128, 8], F32, kind="ExternalInput")
    bvrd = nc.dram_tensor("bvr", [1, 1024], F32, kind="ExternalInput")
    bod = nc.dram_tensor("bo", [128, 8], F32, kind="ExternalInput")
    bfd = nc.dram_tensor("bfn", [128, 8], F32, kind="ExternalInput")
    reld = nc.dram_tensor("rel", [320, 16], F32, kind="ExternalInput")
    ohtd = nc.dram_tensor("oht", [320, 4096], BF16, kind="ExternalInput")
    gwd_d = nc.dram_tensor("gw", [8, 64], F32, kind="ExternalInput")
    gbid = nc.dram_tensor("gbi", [8, 1], F32, kind="ExternalInput")
    gcid = nc.dram_tensor("gci", [16, 1], F32, kind="ExternalInput")
    sclid = nc.dram_tensor("scli", [1, 2], F32, kind="ExternalInput")
    i128d = nc.dram_tensor("i128", [128, 128], BF16, kind="ExternalInput")
    selRd = nc.dram_tensor("selR", [2, 128], BF16, kind="ExternalInput")
    sel232d = nc.dram_tensor("sel232", [2, 32], BF16, kind="ExternalInput")
    p82d = nc.dram_tensor("p82", [8, 2], BF16, kind="ExternalInput")

    outd = nc.dram_tensor("out", [B_C, NDC, 128, T], F32, kind="ExternalOutput")

    rbd = nc.dram_tensor("rbd", [16, 4096], BF16, kind="Internal")
    x1d = nc.dram_tensor("x1d", [B_C, NDC, 128, T], F32, kind="Internal")
    statscr = nc.dram_tensor("statscr", [4096], F32, kind="Internal")
    denscr = nc.dram_tensor("denscr", [4096], F32, kind="Internal")
    gatescr = nc.dram_tensor("gatescr", [16, 2048], BF16, kind="Internal")

    es = contextlib.ExitStack()
    nc._kernel_es = es

    def sb(name, shape, dt_):
        return es.enter_context(nc.sbuf_tensor(name, shape, dt_))

    def sem(name):
        return es.enter_context(nc.semaphore(name))

    arA = sb("arA", [128, 49408], U8)
    arB = sb("arB", [128, 32768], U8)
    arC = sb("arC", [128, 32768], U8)

    vT = arA[:, :].bitcast(BF16)
    vTr = vT.rearrange("p (s h c) -> p s h c", s=NS0, h=NHP, c=193)
    wosv = arA[:, 0:16384].bitcast(BF16)
    w8sv = arA[:, 16384:24576].bitcast(FP8)
    z8v = arA[:, 24576:40960].bitcast(FP8)
    xbf = arB[:, :].bitcast(BF16)
    ctxn = arC[:, :].bitcast(BF16)
    mbc = arC[:, 0:8192].bitcast(F32)
    ibc = arC[:, 8192:16384].bitcast(F32)
    gsigA = arC[:, 16384:24576].bitcast(F32)
    gsigB = arC[:, 24576:32768].bitcast(F32)
    wvsv = arC[:, 0:16384].bitcast(BF16)

    xstage_t = [sb(f"xst{i}", [128, 2432], F32) for i in range(2)]
    xstage = [x[:, 0:2048] for x in xstage_t]
    ctxu = [sb(f"ctxu{i}", [128, T], BF16) for i in range(2)]
    kqp = sb("kqp", [128, 2 * T], BF16)
    sqst = kqp[:, 0:2048]
    sq2 = kqp[:, 2048:4096]
    gbct = [sb(f"gbc{i}", [128, 2 * T], BF16) for i in range(2)]
    tstt = [g[:, :].bitcast(F32) for g in gbct]
    wqs = [sb(f"wqs{i}", [128, 1024], BF16) for i in range(2)]
    wks = [sb(f"wks{i}", [128, 1024], BF16) for i in range(2)]
    obuf = [w[:, :].bitcast(F32) for w in wqs]
    xrst = [w[:, :].bitcast(F32) for w in wks]
    pbT = [x[:, :].bitcast(BF16) for x in xstage_t]
    Gb = [sb(f"Gb{i}", [128, 1024], BF16) for i in range(2)]
    exb = [sb(f"exb{i}", [128, 1024], BF16) for i in range(3)]
    o2buf = [g[:, :].bitcast(F32) for g in Gb]
    dstg = sb("dstg", [128, 2048], F32)
    sstv = dstg[0:1, 0:2048]
    d128 = sb("d128", [128, 32], F32)
    d128r = sb("d128r", [128, 32], F32)
    bcps = sb("bcps", [128, 2048], BF16)
    gate16 = sb("gate16", [16, 2048], BF16)
    dstat = sb("dstat", [128, 48], F32)

    bq8 = sb("bq8", [128, 8], F32)
    bqraw = sb("bqraw", [128, 8], F32)
    bks = sb("bks", [128, 8], F32)
    bos = sb("bos", [128, 8], F32)
    bfs = sb("bfs", [128, 8], F32)
    vbr = sb("vbr", [1, 1024], BF16)
    gwdb = sb("gwdb", [8, 128], BF16)
    gbr = sb("gbr", [8, 1], BF16)
    gcs = sb("gcs", [16, 1], F32)
    scl = sb("scl", [1, 2], F32)
    invs_sb = sb("invs", [1, 1], F32)
    sws_sb = sb("sws", [1, 1], F32)
    isc = sb("isc", [128, 2], F32)
    i128s = sb("i128s", [128, 128], BF16)
    selRs = sb("selRs", [2, 128], BF16)
    sel232s = sb("sel232s", [2, 32], BF16)
    p82s = sb("p82s", [8, 2], BF16)
    Tg = sb("Tg", [128, 256], BF16)
    gb2s = sb("gb2s", [2, 1], BF16)
    gb32f = sb("gb32f", [16, 2], F32)
    smalls = sb("smalls", [128, 4], F32)
    onesb = smalls[:, 0:1].bitcast(BF16)[:, 0:1]
    ones1b = sb("ones1b", [1, 128], BF16)
    ones1f = sb("ones1f", [1, 128], F32)
    ebias = smalls[:, 1:2]
    eighth = smalls[:, 2:3]
    dumm = smalls[0:1, 3:4]

    sWD = sem("sWD")
    sGS = sem("sGS")
    sGDp = [sem("sGD0"), sem("sGD1")]
    sXp = [sem("sX0"), sem("sX1")]
    sPp = [sem(f"sP{i}") for i in range(2)]
    sWqp = [sem("sWq0"), sem("sWq1")]
    sWkp = [sem("sWk0"), sem("sWk1")]
    sXWp = [sem("sXW0"), sem("sXW1")]
    sOp = [sem("sO0"), sem("sO1")]
    sSQ = sem("sSQ")
    sCB = sem("sCB")
    sST = sem("sST")
    sSC = sem("sSC")
    sBD = sem("sBD")
    sAP = sem("sAP")
    sQK = sem("sQK")
    sEV = sem("sEV")
    sG = sem("sG")
    sS = sem("sS")
    sE = sem("sE")
    sA = sem("sA")
    sV = sem("sV")
    sDD = sem("sDD")
    sR = sem("sR")
    sB = sem("sB")
    sN = sem("sN")
    sOT = sem("sOT")
    sOE = sem("sOE")
    sRV = sem("sRV")
    sTI = sem("sTI")
    sVI = sem("sVI")
    sGE = sem("sGE")
    sVC = sem("sVC")
    sVS = sem("sVS")

    C = {k: 0 for k in ("sWD sGS sGD0 sGD1 sX0 sX1 sP0 sP1 sWq0 sWq1 sWk0 sWk1 "
                        "sXW0 sXW1 sO0 sO1 sSQ sCB sST sSC sBD sAP sQK sEV sDD "
                        "sOT sOE sRV sGE sVC").split()}

    def ig(b, il):
        return b * NB_IL + il

    def jslot(b, hp, tq):
        return (b * NHP + hp) * NTQ + tq

    ohts = arA[:, 0:24576].bitcast(BF16)
    rbsb = arA[:, 24576:40960].bitcast(BF16)[0:16, 0:4096]
    rels = arC[:, 0:96].bitcast(BF16)
    with nc.Block() as blk:
        @blk.gpsimd
        def _(g):
            for dst, src in (
                (bqraw[:, :], bqd[:, :]), (bks[:, :], bkd[:, :]),
                (bos[:, :], bod[:, :]), (bfs[:, :], bfd[:, :]),
                (vbr[:, :], bvrd[:, :]),
                (rels[0:128, 0:16], reld[0:128, :]),
                (rels[0:128, 16:32], reld[128:256, :]),
                (rels[0:64, 32:48], reld[256:320, :]),
                (ohts[0:128, 0:4096], ohtd[0:128, :]),
                (ohts[0:128, 4096:8192], ohtd[128:256, :]),
                (ohts[0:64, 8192:12288], ohtd[256:320, :]),
                (gwdb[:, 0:64], gwd_d[:, :]), (gwdb[:, 64:128], gwd_d[:, :]),
                (gbr[:, :], gbid[:, :]), (gcs[:, :], gcid[:, :]),
                (scl[:, :], sclid[:, :]), (i128s[:, :], i128d[:, :]),
                (selRs[:, :], selRd[:, :]), (sel232s[:, :], sel232d[:, :]),
                (p82s[:, :], p82d[:, :]),
            ):
                g.dma_start(dst, src).then_inc(sWD, 16)
            g.memset(onesb[:, :], 1.0)
            g.memset(ones1b[:, :], 1.0)
            g.memset(ones1f[:, :], 1.0)
            g.memset(ebias[:, :], EXPB)
            g.memset(eighth[:, :], 0.125)
            g.memset(dumm[:, :], 0.0)
            g.memset(dstg[:, :], 0.0)
            g.memset(Tg[:, :], 0.0).then_inc(sTI, 1)
            g.wait_ge(sVI, 4)
            g.dma_start(rbd[:, :], rbsb[:, :]).then_inc(sWD, 16)

        with (
            nc.psum_tensor("rbps", [16, 1024], F32) as rbps,
            nc.psum_tensor("gw2ps", [128, 2], F32) as gw2ps,
            nc.psum_tensor("gb2ps", [2, 1], F32) as gb2ps,
            nc.psum_tensor("gb32ps", [16, 2], F32) as gb32ps,
            nc.psum_tensor("iscps", [128, 2], F32) as iscps,
        ):
            @blk.tensor
            def _(t):
                t.wait_ge(sWD, 16 * 20)
                t.wait_ge(sTI, 1)
                cnts = [128, 128, 64]
                for n4 in range(4):
                    if n4 >= 1:
                        t.wait_ge(sVI, n4)
                    for c in range(3):
                        for half in range(2):
                            mm = t.matmul(rbps[:, half * 512:(half + 1) * 512],
                                          rels[0:cnts[c], 16 * c:16 * (c + 1)],
                                          ohts[0:cnts[c],
                                               c * 4096 + n4 * 1024 + half * 512:
                                               c * 4096 + n4 * 1024 + (half + 1) * 512],
                                          start=(c == 0), stop=(c == 2))
                    mm.then_inc(sTI, 1)
                t.matmul(gw2ps[:, :], gwdb[:, :], p82s[:, :],
                         start=True, stop=True).then_inc(sTI, 1)
                t.matmul(gb2ps[:, :], p82s[:, :], gbr[:, :],
                         start=True, stop=True).then_inc(sTI, 1)
                t.wait_ge(sVI, 6)
                t.matmul(gb32ps[:, 0:1], sel232s[:, 0:16], gb2s[:, :],
                         start=True, stop=True)
                t.matmul(gb32ps[:, 1:2], sel232s[:, 16:32], gb2s[:, :],
                         start=True, stop=True).then_inc(sTI, 1)
                t.wait_ge(sVI, 7)
                t.matmul(iscps[:, 0:1], ones1f[:, :], invs_sb[:, :], start=True, stop=True)
                t.matmul(iscps[:, 1:2], ones1f[:, :], sws_sb[:, :],
                         start=True, stop=True).then_inc(sTI, 1)

            @blk.vector
            def _(v):
                for n4 in range(4):
                    v.wait_ge(sTI, 2 + n4)
                    v.tensor_copy(rbsb[:, n4 * 1024:(n4 + 1) * 1024],
                                  rbps[:, :]).then_inc(sVI, 1)
                v.wait_ge(sTI, 6)
                for hp in range(NHP):
                    a0 = hp * 16
                    b0 = 128 + hp * 16
                    v.tensor_copy(Tg[0:64, a0 + 2 * hp:a0 + 2 * hp + 1], gw2ps[0:64, 0:1])
                    v.tensor_copy(Tg[64:128, a0 + 2 * hp + 1:a0 + 2 * hp + 2], gw2ps[64:128, 0:1])
                    v.tensor_copy(Tg[0:64, b0 + 2 * hp:b0 + 2 * hp + 1], gw2ps[0:64, 1:2])
                    cp = v.tensor_copy(Tg[64:128, b0 + 2 * hp + 1:b0 + 2 * hp + 2], gw2ps[64:128, 1:2])
                cp.then_inc(sVI, 1)
                v.wait_ge(sTI, 7)
                v.tensor_copy(gb2s[:, :], gb2ps[:, :]).then_inc(sVI, 1)
                v.reciprocal(invs_sb[:, :], scl[0:1, 0:1])
                v.tensor_mul(sws_sb[:, :], scl[0:1, 0:1], scl[0:1, 1:2]).then_inc(sVI, 1)
                v.wait_ge(sTI, 8)
                v.tensor_copy(gb32f[:, :], gb32ps[:, :])
                v.wait_ge(sTI, 9)
                v.tensor_copy(isc[:, :], iscps[:, :])
                v.tensor_scalar(bq8[:, :], bqraw[:, :], 0.125, None,
                                OP.mult).then_inc(sVI, 1)

    C["sWD"] = 16 * 21
    RBD_READY = C["sWD"]

    def ln_phase(b, which):
        src = xd if which == 0 else x1d
        with (
            nc.Block() as blk,
            nc.psum_tensor(f"sxp{b}_{which}", [1, 2048], F32) as sxp,
            nc.psum_tensor(f"sx2p{b}_{which}", [1, 2048], F32) as sx2p
        ):
            x0 = [C["sX0"], C["sX1"]]
            cb0 = C["sCB"]
            sq0 = C["sSQ"]
            st0 = C["sST"]
            sc0 = C["sSC"]
            bd0 = C["sBD"]

            @blk.gpsimd
            def _(g):
                if which == 1:
                    g.wait_ge(sXWp[0], 16 * 16 * (b + 1))
                    g.wait_ge(sXWp[1], 16 * 16 * (b + 1))
                for dc in range(NDC):
                    if dc >= 2:
                        g.wait_ge(sCB, cb0 + dc - 1)
                        g.wait_ge(sSQ, sq0 + dc - 1)
                    g.dma_start(xstage[dc % 2], src[b, dc, :, :]).then_inc(sPq := sXp[dc % 2], 16)
                g.wait_ge(sSC, sc0 + 1)
                g.dma_start(bass.AP(statscr, 0, [[2048, 1], [1, 2048]]), sstv).then_inc(sBD, 16)
                g.wait_ge(sSC, sc0 + 2)
                g.dma_start(bass.AP(statscr, 2048, [[2048, 1], [1, 2048]]), sstv).then_inc(sBD, 16)
                g.wait_ge(sBD, bd0 + 32)
                g.dma_start(bass.AP(dstat, 0, [[48, 128], [1, 16]]),
                            bass.AP(statscr, 0, [[16, 128], [1, 16]])).then_inc(sBD, 16)
                g.dma_start(bass.AP(dstat, 16, [[48, 128], [1, 16]]),
                            bass.AP(statscr, 2048, [[16, 128], [1, 16]])).then_inc(sBD, 16)
                g.wait_ge(sSC, sc0 + 3)
                g.dma_start(bass.AP(statscr, 0, [[16, 128], [1, 16]]),
                            dstat[:, 0:16]).then_inc(sBD, 16)
                g.wait_ge(sSC, sc0 + 4)
                g.dma_start(bass.AP(statscr, 2048, [[16, 128], [1, 16]]),
                            dstat[:, 32:48]).then_inc(sBD, 16)
                g.wait_ge(sBD, bd0 + 96)
                g.dma_start(mbc, bass.AP(statscr, 0, [[0, 128], [1, 2048]])).then_inc(sBD, 16)
                g.dma_start(ibc, bass.AP(statscr, 2048, [[0, 128], [1, 2048]])).then_inc(sBD, 16)
            C["sX0"] += 16 * (NDC // 2)
            C["sX1"] += 16 * (NDC // 2)
            C["sBD"] += 16 * 8

            @blk.vector
            def _(v):
                for dc in range(NDC):
                    v.wait_ge(sXp[dc % 2], x0[dc % 2] + 16 * (dc // 2 + 1))
                    if which == 1 and dc >= 1:
                        v.wait_ge(sST, st0 + dc)
                    tgt = xbf[:, dc * T:(dc + 1) * T] if which == 0 else sqst
                    v.tensor_copy(tgt, xstage[dc % 2]).then_inc(sCB, 1)
                v.wait_ge(sST, st0 + NDC)
                v.wait_ge(sDD, 16 * 11 * NHP * (b + which))
                v.tensor_copy(sstv, sxp[0:1, :]).then_inc(sSC, 1)
                v.wait_ge(sBD, bd0 + 16)
                v.tensor_copy(sstv, sx2p[0:1, :]).then_inc(sSC, 1)
                v.wait_ge(sBD, bd0 + 64)
                v.tensor_scalar(dstat[:, 0:16], dstat[:, 0:16], 1.0 / 1024.0, None,
                                OP.mult).then_inc(sVC, 1)
                v.tensor_scalar(dstat[:, 16:32], dstat[:, 16:32], 1.0 / 1024.0, 1e-5,
                                OP.mult, OP.add).then_inc(sVC, 1)
                v.wait_ge(sVC, C["sVC"] + 2)
                v.tensor_mul(dstat[:, 32:48], dstat[:, 0:16],
                             dstat[:, 0:16]).then_inc(sVC, 1)
                v.wait_ge(sVC, C["sVC"] + 3)
                v.tensor_sub(dstat[:, 16:32], dstat[:, 16:32],
                             dstat[:, 32:48]).then_inc(sVC, 1)
                v.wait_ge(sVC, C["sVC"] + 4)
                v.reciprocal(dstat[:, 32:48], dstat[:, 16:32]).then_inc(sSC, 1)
                v.wait_ge(sSQ, sq0 + NDC + 1)
                if which == 1:
                    v.tensor_scalar(dstat[:, 32:48], dstat[:, 32:48], isc[:, 0:1],
                                    None, OP.mult).then_inc(sSC, 1)
                else:
                    v.tensor_copy(dumm[:, :], dumm[:, :]).then_inc(sSC, 1)
            C["sCB"] += NDC
            C["sSC"] += 4
            C["sVC"] += 4

            @blk.scalar
            def _(s):
                for dc in range(NDC):
                    s.wait_ge(sXp[dc % 2], x0[dc % 2] + 16 * (dc // 2 + 1))
                    if dc >= 1:
                        s.wait_ge(sST, st0 + dc)
                    tgt = sqst if which == 0 else sq2
                    s.activation(tgt, xstage[dc % 2], AF.Square).then_inc(sSQ, 1)
                s.wait_ge(sSC, sc0 + 3)
                s.activation(dstat[:, 32:48], dstat[:, 32:48], AF.Sqrt).then_inc(sSQ, 1)
            C["sSQ"] += NDC + 1

            @blk.tensor
            def _(t):
                for dc in range(NDC):
                    t.wait_ge(sCB, cb0 + dc + 1)
                    t.wait_ge(sSQ, sq0 + dc + 1)
                    xsrc = xbf[:, dc * T:(dc + 1) * T] if which == 0 else sqst
                    qsrc = sqst if which == 0 else sq2
                    for sl in range(4):
                        t.matmul(sxp[0:1, sl * 512:(sl + 1) * 512], onesb[:, :],
                                 xsrc[:, sl * 512:(sl + 1) * 512],
                                 start=(dc == 0), stop=(dc == NDC - 1))
                    for sl in range(4):
                        mm = t.matmul(sx2p[0:1, sl * 512:(sl + 1) * 512], onesb[:, :],
                                      qsrc[:, sl * 512:(sl + 1) * 512],
                                      start=(dc == 0), stop=(dc == NDC - 1))
                    mm.then_inc(sST, 1)
            C["sST"] += NDC

        with nc.Block() as blk:
            x0 = [C["sX0"], C["sX1"]]
            ap0 = C["sAP"]
            bd_all = C["sBD"]

            apc_g = 2 if which == 0 else 3

            @blk.gpsimd
            def _(g):
                for dc in range(NDC):
                    if dc >= 2:
                        g.wait_ge(sAP, ap0 + apc_g * (dc - 2) + 1)
                    g.dma_start(xstage[dc % 2], src[b, dc, :, :]).then_inc(sXp[dc % 2], 16)
            C["sX0"] += 16 * (NDC // 2)
            C["sX1"] += 16 * (NDC // 2)

            @blk.vector
            def _(v):
                v.wait_ge(sBD, bd_all)
                for dc in range(NDC):
                    v.wait_ge(sXp[dc % 2], x0[dc % 2] + 16 * (dc // 2 + 1))
                    if dc >= 1:
                        v.wait_ge(sAP, ap0 + apc_g * (dc - 1) + 2)
                    v.tensor_sub(tstt[0], xstage[dc % 2], mbc).then_inc(sAP, 1)
                    v.wait_ge(sAP, ap0 + apc_g * dc + 1)
                    if which == 0:
                        v.tensor_mul(xbf[:, dc * T:(dc + 1) * T], tstt[0],
                                     ibc).then_inc(sAP, 1)
                    else:
                        v.tensor_mul(tstt[1], tstt[0], ibc).then_inc(sAP, 1)
                        v.wait_ge(sAP, ap0 + apc_g * dc + 2)
                        v.tensor_scalar(z8v[:, dc * T:(dc + 1) * T], tstt[1],
                                        448.0, -448.0, OP.min,
                                        OP.max).then_inc(sAP, 1)
            C["sAP"] += apc_g * NDC

    def vg_phase(b):
        with (
            nc.Block() as blk,
            nc.psum_tensor(f"vps0_{b}", [128, 512], F32) as vps0,
            nc.psum_tensor(f"vps1_{b}", [128, 512], F32) as vps1,
            nc.psum_tensor(f"gapsA_{b}", [16, 1024], F32) as gapsA,
            nc.psum_tensor(f"gapsB_{b}", [16, 1024], F32) as gapsB
        ):
            vps = [vps0, vps1]
            qk0 = C["sQK"]
            ev0 = C["sEV"]
            ge0 = C["sGE"]
            wd0 = C["sWD"]

            @blk.gpsimd
            def _(g):
                g.wait_ge(sWD, RBD_READY)
                g.memset(vT[:, :], 0.0).then_inc(sGE, 1)
                g.dma_start(wvsv, wvd[:, :]).then_inc(sWD, 16)
                g.wait_ge(sGE, ge0 + 3)
                g.dma_start(gatescr[:, :], gate16[:, :]).then_inc(sGS, 16)
            C["sWD"] += 16
            C["sGS"] += 16

            @blk.vector
            def _(v):
                v.wait_ge(sGE, ge0 + 1)
                v.memset(vTr[:, :, :, 64:65], 1.0)
                v.memset(vTr[:, :, :, 97:98], 1.0).then_inc(sGE, 1)
                v.wait_ge(sEV, ev0 + 2)
                v.tensor_scalar(tstt[0][0:16, 0:T], gsigB[0:16, 0:T], gcs[:, :], -1.0,
                                OP.mult, OP.add).then_inc(sVC, 1)
                v.wait_ge(sVC, C["sVC"] + 1)
                v.tensor_mul(tstt[1][0:16, 0:T], tstt[0][0:16, 0:T],
                             gsigA[0:16, 0:T]).then_inc(sVC, 1)
                v.wait_ge(sVC, C["sVC"] + 2)
                v.tensor_scalar(gate16[:, :], tstt[1][0:16, 0:T], 2.0, None,
                                OP.add).then_inc(sGE, 1)
            C["sGE"] += 3
            C["sVC"] += 2

            @blk.tensor
            def _(t):
                t.wait_ge(sWD, RBD_READY)
                for gh in range(2):
                    if gh == 1:
                        t.wait_ge(sEV, ev0 + 1)
                    for sl in range(2):
                        for hp in range(NHP):
                            xsl = xbf[:, hp * T + gh * 1024 + sl * 512:
                                      hp * T + gh * 1024 + (sl + 1) * 512]
                            t.matmul(gapsA[:, sl * 512:(sl + 1) * 512],
                                     Tg[:, hp * 16:(hp + 1) * 16], xsl,
                                     start=(hp == 0), stop=(hp == NHP - 1))
                            mm = t.matmul(gapsB[:, sl * 512:(sl + 1) * 512],
                                          Tg[:, 128 + hp * 16:128 + (hp + 1) * 16], xsl,
                                          start=(hp == 0), stop=(hp == NHP - 1))
                    mm.then_inc(sQK, 1)
                t.wait_ge(sWD, wd0 + 16)
                for tb in range(NS0):
                    for th in range(2):
                        k = 2 * tb + th
                        if k >= 2:
                            t.wait_ge(sEV, ev0 + 2 * k)
                        for dc in range(NDC):
                            t.matmul(vps[k % 2][:, :],
                                     xbf[:, dc * T + tb * 128:dc * T + (tb + 1) * 128],
                                     wvsv[:, dc * 1024 + th * 512:dc * 1024 + (th + 1) * 512],
                                     start=(dc == 0), stop=False)
                        t.matmul(vps[k % 2][:, :], ones1b[:, :],
                                 vbr[0:1, th * 512:(th + 1) * 512],
                                 start=False, stop=True).then_inc(sQK, 1)
            C["sQK"] += 2 + NS0 * 2

            @blk.scalar
            def _(s):
                for gh in range(2):
                    s.wait_ge(sQK, qk0 + gh + 1)
                    s.activation(gsigA[0:16, gh * 1024:(gh + 1) * 1024], gapsA[:, :],
                                 AF.Sigmoid, bias=gb32f[:, 0:1])
                    s.activation(gsigB[0:16, gh * 1024:(gh + 1) * 1024], gapsB[:, :],
                                 AF.Sigmoid, bias=gb32f[:, 1:2]).then_inc(sEV, 1)
                s.wait_ge(sGE, ge0 + 2)
                for tb in range(NS0):
                    for th in range(2):
                        k = 2 * tb + th
                        s.wait_ge(sQK, qk0 + 2 + k + 1)
                        vsrc = vps[k % 2][:, :].rearrange("p (m x d) -> p m x d",
                                                          m=4, x=2, d=64)
                        s.activation(vTr[:, tb, th * 4:(th + 1) * 4, 0:64],
                                     vsrc[:, :, 0, :], AF.Identity)
                        s.activation(vTr[:, tb, th * 4:(th + 1) * 4, 129:193],
                                     vsrc[:, :, 1, :], AF.Identity).then_inc(sEV, 2)
            C["sEV"] += 2 + NS0 * 2 * 2

    def attn_phase(b):
        with (
            nc.Block() as blk,
            nc.psum_tensor(f"scp0_{b}", [128, 1024], F32) as scp0,
            nc.psum_tensor(f"scp1_{b}", [128, 1024], F32) as scp1,
            nc.psum_tensor(f"cxA0_{b}", [65, 512], F32) as cxA0,
            nc.psum_tensor(f"cxB0_{b}", [128, 512], F32) as cxB0,
            nc.psum_tensor(f"cxA1_{b}", [65, 512], F32) as cxA1,
            nc.psum_tensor(f"cxB1_{b}", [128, 512], F32) as cxB1
        ):
            scp = [scp0, scp1]
            cxA = [cxA0, cxA1]
            cxB = [cxB0, cxB1]
            qk0 = C["sQK"]
            ev0 = C["sEV"]
            wq0 = [C["sWq0"], C["sWq1"]]
            wk0 = [C["sWk0"], C["sWk1"]]
            gd0 = [C["sGD0"], C["sGD1"]]
            p0 = [C["sP0"], C["sP1"]]
            ap_all = C["sAP"]
            cb_all = C["sCB"]
            sq_all = C["sSQ"]

            def emit_av(t, il):
                hp = il // (NTQ * NS0)
                tq = (il // NS0) % NTQ
                s0 = il % NS0
                i_g = ig(b, il)
                slot = jslot(b, hp, tq)
                t.wait_ge(sE, i_g + 1)
                if s0 == 0 and slot > 1:
                    t.wait_ge(sV, slot - 1)
                    t.wait_ge(sVS, slot - 1)
                base = s0 * 1544 + hp * 193
                t.matmul(cxA[slot % 2][:, :], vT[:, base:base + 65], exb[il % 3][:, 0:512],
                         start=(s0 == 0), stop=(s0 == NS0 - 1))
                t.matmul(cxB[slot % 2][:, :], vT[:, base + 65:base + 193], exb[il % 3][:, 512:1024],
                         start=(s0 == 0), stop=(s0 == NS0 - 1)).then_inc(sA, 1)

            @blk.gpsimd
            def _(g):
                def load_pair(hp):
                    g.dma_start(gbct[hp % 2][:, :].rearrange("p (h j) -> p h j", h=2, j=2048),
                                bass.AP(gatescr, (2 * hp) * 2048,
                                        [[0, 128], [2048, 2], [1, 2048]])).then_inc(sGDp[hp % 2], 16)
                    g.dma_start(wqs[hp % 2][:, :].rearrange("p (c d) -> p c d", c=8, d=128),
                                bass.AP(wqd, hp * 128,
                                        [[8192, 128], [1024, 8], [1, 128]])).then_inc(sWqp[hp % 2], 16)
                    g.dma_start(wks[hp % 2][:, :].rearrange("p (c d) -> p c d", c=8, d=128),
                                bass.AP(wkd, hp * 128,
                                        [[8192, 128], [1024, 8], [1, 128]])).then_inc(sWkp[hp % 2], 16)

                g.wait_ge(sGS, 16 * (b + 1))
                load_pair(0)
                load_pair(1)
                for hp in range(NHP):
                    hpg = b * NHP + hp
                    if hp + 2 < NHP:
                        nhp = hp + 2
                        g.wait_ge(sG, ig(b, (hp + 1) * NTQ * NS0))
                        g.wait_ge(sQK, qk0 + 4 * (nhp - 1))
                        load_pair(nhp)
                    g.wait_ge(sVS, jslot(b, hp, NTQ - 1) + 1)
                    g.wait_ge(sV, jslot(b, hp, NTQ - 1) + 1)
                    if hpg > 0:
                        g.wait_ge(sDD, 16 * (11 * (hpg - 1) + 11))
                        g.wait_ge(sN, 4 * hpg)
                    g.dma_start(bass.AP(denscr, 0, [[4096, 1], [1, 4096]]),
                                bass.AP(dstg, 32 * 2048, [[32 * 2048, 2], [1, 2048]])).then_inc(sDD, 16)
                    g.wait_ge(sDD, 16 * (11 * hpg + 1))
                    g.dma_start(bass.AP(d128, 0, [[32, 128], [1, 32]]),
                                bass.AP(denscr, 0, [[32, 128], [1, 32]])).then_inc(sDD, 16)
                    g.wait_ge(sR, hpg + 1)
                    g.dma_start(bass.AP(denscr, 0, [[32, 128], [1, 32]]),
                                d128r[:, :]).then_inc(sDD, 16)
                    g.wait_ge(sDD, 16 * (11 * hpg + 3))
                    for idx in range(4):
                        g.dma_start(bcps[0:64, idx * 512:(idx + 1) * 512],
                                    bass.AP(denscr, 2048 + idx * 512, [[0, 64], [1, 512]])).then_inc(sDD, 16)
                        g.dma_start(bcps[64:128, idx * 512:(idx + 1) * 512],
                                    bass.AP(denscr, idx * 512, [[0, 64], [1, 512]])).then_inc(sDD, 16)
            C["sWq0"] += 16 * (NHP // 2)
            C["sWq1"] += 16 * (NHP // 2)
            C["sWk0"] += 16 * (NHP // 2)
            C["sWk1"] += 16 * (NHP // 2)
            C["sGD0"] += 16 * (NHP // 2)
            C["sGD1"] += 16 * (NHP // 2)
            C["sDD"] += 16 * 11 * NHP

            @blk.sync
            def _(sy):
                sy.wait_ge(sWD, RBD_READY)
                sy.wait_ge(sCB, cb_all)
                sy.wait_ge(sSQ, sq_all)
                sy.wait_ge(sAP, ap_all)
                for hp in range(NHP):
                    for tq in range(NTQ):
                        idx = hp * NTQ + tq
                        if idx >= 2:
                            sy.wait_ge(sG, ig(b, (idx - 1) * NS0))
                        for h in range(2):
                            sy.dma_start(pbT[idx % 2][:, h * 2432:(h + 1) * 2432],
                                         bass.AP(rbd, (2 * hp + h) * 4096 + 1536 - tq * 512,
                                                 [[1, 128], [1, 2432]])).then_inc(sPp[idx % 2], 16)

            @blk.tensor
            def _(t):
                for hp in range(NHP):
                    hpg = b * NHP + hp
                    t.wait_ge(sWqp[hp % 2], wq0[hp % 2] + 16 * (hp // 2 + 1))
                    t.wait_ge(sWkp[hp % 2], wk0[hp % 2] + 16 * (hp // 2 + 1))
                    if hp == 0:
                        t.wait_ge(sAP, ap_all)
                    prev_last = ig(b, hp * NTQ * NS0)
                    if prev_last > 0:
                        t.wait_ge(sE, prev_last)
                    for which, wsl in ((0, wks), (1, wqs)):
                        if which == 1:
                            t.wait_ge(sEV, ev0 + 2 * hp + 1)
                        for half in range(2):
                            for dc in range(NDC):
                                for sl in range(2):
                                    mm = t.matmul(scp[half][:, sl * 512:(sl + 1) * 512],
                                                  wsl[hp % 2][:, dc * 128:(dc + 1) * 128],
                                                  xbf[:, dc * T + half * 1024 + sl * 512:
                                                      dc * T + half * 1024 + (sl + 1) * 512],
                                                  start=(dc == 0), stop=(dc == NDC - 1))
                            mm.then_inc(sQK, 1)
                    for tq in range(NTQ):
                        for s0 in range(NS0):
                            il = (hp * NTQ + tq) * NS0 + s0
                            i_g = ig(b, il)
                            if i_g >= 2:
                                t.wait_ge(sE, i_g - 1)
                            if tq == 0 and s0 == 0:
                                t.wait_ge(sEV, ev0 + 2 * hp + 2)
                            t.matmul(scp[il % 2][:, 0:512],
                                     kqp[0:64, s0 * 128:(s0 + 1) * 128],
                                     kqp[0:64, T + tq * 512:T + (tq + 1) * 512],
                                     start=True, stop=False, tile_position=(0, 0))
                            t.matmul(scp[il % 2][:, 512:1024],
                                     kqp[64:128, s0 * 128:(s0 + 1) * 128],
                                     kqp[64:128, T + tq * 512:T + (tq + 1) * 512],
                                     start=True, stop=False, tile_position=(64, 0))
                            if il > 1:
                                emit_av(t, il - 2)
                            t.wait_ge(sG, i_g + 1)
                            t.matmul(scp[il % 2][:, 0:512], i128s[:, :],
                                     Gb[il % 2][:, 0:512], start=False, stop=True)
                            t.matmul(scp[il % 2][:, 512:1024], i128s[:, :],
                                     Gb[il % 2][:, 512:1024],
                                     start=False, stop=True).then_inc(sS, 1)
                emit_av(t, NB_IL - 2)
                emit_av(t, NB_IL - 1)
            for _i in range(2):
                C[f"sP{_i}"] += 32 * (NHP * NTQ // 2)
            C["sQK"] += 4 * NHP

            @blk.scalar
            def _(s):
                den_pending = []

                for hp in range(NHP):
                    s.wait_ge(sQK, qk0 + 4 * hp + 2)
                    s.activation(kqp[:, 0:1024], scp0[:, :], AF.Identity,
                                 bias=bks[:, hp:hp + 1])
                    s.activation(kqp[:, 1024:2048], scp1[:, :], AF.Identity,
                                 bias=bks[:, hp:hp + 1]).then_inc(sEV, 1)
                    s.wait_ge(sQK, qk0 + 4 * hp + 4)
                    s.activation(kqp[:, T:T + 1024], scp0[:, :], AF.Identity,
                                 bias=bq8[:, hp:hp + 1], scale=eighth[:, :])
                    s.activation(kqp[:, T + 1024:2 * T], scp1[:, :], AF.Identity,
                                 bias=bq8[:, hp:hp + 1], scale=eighth[:, :]).then_inc(sEV, 1)
                    for tq in range(NTQ):
                        for s0 in range(NS0):
                            il = (hp * NTQ + tq) * NS0 + s0
                            i_g = ig(b, il)
                            if s0 == 3 and den_pending:
                                hp_e, tq_e, igl = den_pending[0]
                                sl_e = jslot(b, hp_e, tq_e)
                                s.wait_ge(sA, igl + 1)
                                s.activation(ctxu[hp_e % 2][64:128, tq_e * 512:(tq_e + 1) * 512],
                                             cxB[sl_e % 2][64:128, :], AF.Identity)
                            if s0 == 5 and den_pending:
                                hp_e, tq_e, igl = den_pending.pop(0)
                                hpg_e = b * NHP + hp_e
                                sl_e = jslot(b, hp_e, tq_e)
                                if tq_e == 0 and hpg_e > 0:
                                    s.wait_ge(sDD, 16 * (11 * (hpg_e - 1) + 1))
                                s.activation(dstg[32:33, tq_e * 512:(tq_e + 1) * 512],
                                             cxB[sl_e % 2][32:33, :], AF.Identity).then_inc(sVS, 1)
                            s.wait_ge(sS, i_g + 1)
                            if i_g >= 3:
                                s.wait_ge(sA, i_g - 2)
                            s.activation(exb[il % 3][:, :], scp[il % 2][:, :], AF.Exp,
                                         bias=ebias[:, :]).then_inc(sE, 1)
                            if s0 == NS0 - 1:
                                den_pending.append((hp, tq, i_g))
                while den_pending:
                    hp_e, tq_e, igl = den_pending.pop(0)
                    sl_e = jslot(b, hp_e, tq_e)
                    s.wait_ge(sA, igl + 1)
                    s.activation(ctxu[hp_e % 2][64:128, tq_e * 512:(tq_e + 1) * 512],
                                 cxB[sl_e % 2][64:128, :], AF.Identity)
                    s.activation(dstg[32:33, tq_e * 512:(tq_e + 1) * 512],
                                 cxB[sl_e % 2][32:33, :], AF.Identity).then_inc(sVS, 1)
            C["sEV"] += 2 * NHP

            @blk.vector
            def _(v):
                v.wait_ge(sXWp[0], 16 * 16 * b)
                v.wait_ge(sXWp[1], 16 * 16 * b)
                v.wait_ge(sOp[0], 16 * 16 * b)
                v.wait_ge(sOp[1], 16 * 16 * b)
                pending = []

                def emit_evac(vv, step):
                    hp_e, tq_e, igl = pending[0]
                    hpg_e = b * NHP + hp_e
                    sl = jslot(b, hp_e, tq_e)
                    cu = ctxu[hp_e % 2]
                    c0 = tq_e * 512
                    if step == 0:
                        vv.wait_ge(sA, igl + 1)
                        vv.tensor_copy(cu[0:64, c0:c0 + 512], cxA[sl % 2][0:64, :])
                    else:
                        if tq_e == 0 and hpg_e > 0:
                            vv.wait_ge(sDD, 16 * (11 * (hpg_e - 1) + 1))
                        vv.tensor_copy(dstg[64:65, c0:c0 + 512],
                                       cxA[sl % 2][64:65, :]).then_inc(sV, 1)
                        pending.pop(0)

                for hp in range(NHP):
                    hpg = b * NHP + hp
                    for tq in range(NTQ):
                        for s0 in range(NS0):
                            il = (hp * NTQ + tq) * NS0 + s0
                            i_g = ig(b, il)
                            if s0 == 2 and pending:
                                emit_evac(v, 0)
                            if s0 == 4 and pending:
                                emit_evac(v, 1)
                            if tq == 0 and s0 == 7 and hp > 0:
                                v.wait_ge(sDD, 16 * (11 * (hpg - 1) + 2))
                                v.reciprocal(d128r[:, :], d128[:, :]).then_inc(sR, 1)
                            if hp > 0 and tq == 1 and s0 in (2, 6, 10, 14):
                                idx = {2: 0, 6: 1, 10: 2, 14: 3}[s0]
                                hpp = hp - 1
                                v.wait_ge(sDD, 16 * (11 * (hpg - 1) + 11))
                                v.wait_ge(sV, jslot(b, hpp, NTQ - 1) + 1)
                                v.wait_ge(sVS, jslot(b, hpp, NTQ - 1) + 1)
                                v.tensor_mul(
                                    ctxn[:, hpp * T + idx * 512:hpp * T + (idx + 1) * 512],
                                    ctxu[hpp % 2][:, idx * 512:(idx + 1) * 512],
                                    bcps[:, idx * 512:(idx + 1) * 512]).then_inc(sN, 1)
                            idx = hp * NTQ + tq
                            if s0 == 0:
                                v.wait_ge(sPp[idx % 2], p0[idx % 2] + 32 * (idx // 2 + 1))
                            if il == 0 or (s0 == 0 and tq == 0):
                                v.wait_ge(sGDp[hp % 2], gd0[hp % 2] + 16 * (hp // 2 + 1))
                            if i_g >= 2:
                                v.wait_ge(sS, i_g - 1)
                            rev = bass.AP(pbT[idx % 2].tensor, s0 * 128 + 511,
                                          [[4864, 128], [2432, 2], [-1, 512]])
                            gsl = bass.AP(gbct[hp % 2], tq * 512,
                                          [[4096, 128], [2048, 2], [1, 512]])
                            v.tensor_mul(Gb[il % 2][:, :], rev, gsl).then_inc(sG, 1)
                            if s0 == NS0 - 1:
                                pending.append((hp, tq, i_g))
                while pending:
                    emit_evac(v, 0)
                    emit_evac(v, 1)
                hpgl = b * NHP + NHP - 1
                v.wait_ge(sDD, 16 * (11 * hpgl + 2))
                v.reciprocal(d128r[:, :], d128[:, :]).then_inc(sR, 1)
                v.wait_ge(sDD, 16 * (11 * hpgl + 11))
                for idx in range(4):
                    v.tensor_mul(ctxn[:, (NHP - 1) * T + idx * 512:(NHP - 1) * T + (idx + 1) * 512],
                                 ctxu[(NHP - 1) % 2][:, idx * 512:(idx + 1) * 512],
                                 bcps[:, idx * 512:(idx + 1) * 512]).then_inc(sN, 1)

    def out_phase(b):
        with (
            nc.Block() as blk,
            nc.psum_tensor(f"oup0_{b}", [128, 512], F32) as oup0,
            nc.psum_tensor(f"oup1_{b}", [128, 512], F32) as oup1
        ):
            oup = [oup0, oup1]
            ot0 = C["sOT"]
            oe0 = C["sOE"]
            rv0 = C["sRV"]
            x0 = [C["sX0"], C["sX1"]]
            xw0 = [C["sXW0"], C["sXW1"]]
            wd0 = C["sWD"]

            @blk.gpsimd
            def _(g):
                g.dma_start(wosv, wod[:, :]).then_inc(sWD, 16)
                for j in range(32):
                    tqq, dcO = divmod(j, 8)
                    if j >= 2:
                        g.wait_ge(sRV, rv0 + j - 1)
                    g.dma_start(xrst[j % 2],
                                xd[b, dcO, :, tqq * 512:(tqq + 1) * 512]).then_inc(sXp[j % 2], 16)
            C["sWD"] += 16
            C["sX0"] += 16 * 16
            C["sX1"] += 16 * 16

            @blk.tensor
            def _(t):
                t.wait_ge(sWD, wd0 + 16)
                for j in range(32):
                    tqq, dcO = divmod(j, 8)
                    if j >= 2:
                        t.wait_ge(sOE, oe0 + j - 1)
                    for hp in range(NHP):
                        mm = t.matmul(oup[j % 2][:, :],
                                      wosv[:, (hp * 8 + dcO) * 128:(hp * 8 + dcO + 1) * 128],
                                      ctxn[:, hp * T + tqq * 512:hp * T + (tqq + 1) * 512],
                                      start=(hp == 0), stop=(hp == NHP - 1))
                    mm.then_inc(sOT, 1)
            C["sOT"] += 32

            @blk.scalar
            def _(s):
                for j in range(32):
                    tqq, dcO = divmod(j, 8)
                    s.wait_ge(sOT, ot0 + j + 1)
                    if j >= 2:
                        s.wait_ge(sRV, rv0 + j - 1)
                    s.activation(obuf[j % 2], oup[j % 2][:, :], AF.Identity,
                                 bias=bos[:, dcO:dcO + 1]).then_inc(sOE, 1)
            C["sOE"] += 32

            @blk.vector
            def _(v):
                for j in range(32):
                    v.wait_ge(sOE, oe0 + j + 1)
                    v.wait_ge(sXp[j % 2], x0[j % 2] + 16 * (j // 2 + 1))
                    if j >= 2:
                        v.wait_ge(sXWp[j % 2], xw0[j % 2] + 16 * (j // 2))
                    v.tensor_add(o2buf[j % 2], obuf[j % 2],
                                 xrst[j % 2]).then_inc(sRV, 1)
            C["sRV"] += 32

            @blk.sync
            def _(sy):
                for j in range(32):
                    tqq, dcO = divmod(j, 8)
                    sy.wait_ge(sRV, rv0 + j + 1)
                    sy.dma_start(x1d[b, dcO, :, tqq * 512:(tqq + 1) * 512],
                                 o2buf[j % 2]).then_inc(sXWp[j % 2], 16)
            C["sXW0"] += 16 * 16
            C["sXW1"] += 16 * 16

    def ffn_phase(b):
        with (
            nc.Block() as blk,
            nc.psum_tensor(f"fps0_{b}", [128, 512], F32) as fps0,
            nc.psum_tensor(f"fps1_{b}", [128, 512], F32) as fps1
        ):
            fps = [fps0, fps1]
            ot0 = C["sOT"]
            oe0 = C["sOE"]
            rv0 = C["sRV"]
            x0 = [C["sX0"], C["sX1"]]
            o0 = [C["sO0"], C["sO1"]]
            wd0 = C["sWD"]

            @blk.gpsimd
            def _(g):
                g.dma_start(w8sv, w8d[:, :]).then_inc(sWD, 16)
                for j in range(32):
                    tqq, dcO = divmod(j, 8)
                    if j >= 2:
                        g.wait_ge(sRV, rv0 + j - 1)
                    g.dma_start(xrst[j % 2],
                                x1d[b, dcO, :, tqq * 512:(tqq + 1) * 512]).then_inc(sXp[j % 2], 16)
            C["sWD"] += 16
            C["sX0"] += 16 * 16
            C["sX1"] += 16 * 16

            @blk.tensor
            def _(t):
                t.wait_ge(sWD, wd0 + 16)
                for j in range(32):
                    tqq, dcO = divmod(j, 8)
                    if j >= 2:
                        t.wait_ge(sOE, oe0 + j - 1)
                    for dc in range(NDC):
                        mm = t.matmul(fps[j % 2][:, :],
                                      w8sv[:, dc * 1024 + dcO * 128:dc * 1024 + (dcO + 1) * 128],
                                      z8v[:, dc * T + tqq * 512:dc * T + (tqq + 1) * 512],
                                      start=(dc == 0), stop=(dc == NDC - 1))
                    mm.then_inc(sOT, 1)
            C["sOT"] += 32

            @blk.scalar
            def _(s):
                for j in range(32):
                    tqq, dcO = divmod(j, 8)
                    s.wait_ge(sOT, ot0 + j + 1)
                    if j >= 2:
                        s.wait_ge(sRV, rv0 + j - 1)
                    s.activation(obuf[j % 2], fps[j % 2][:, :], AF.Identity,
                                 bias=bfs[:, dcO:dcO + 1], scale=isc[:, 1:2]).then_inc(sOE, 1)
            C["sOE"] += 32

            @blk.vector
            def _(v):
                for j in range(32):
                    v.wait_ge(sOE, oe0 + j + 1)
                    v.wait_ge(sXp[j % 2], x0[j % 2] + 16 * (j // 2 + 1))
                    if j >= 2:
                        v.wait_ge(sOp[j % 2], o0[j % 2] + 16 * (j // 2))
                    v.tensor_add(o2buf[j % 2], obuf[j % 2],
                                 xrst[j % 2]).then_inc(sRV, 1)
            C["sRV"] += 32

            @blk.sync
            def _(sy):
                for j in range(32):
                    tqq, dcO = divmod(j, 8)
                    sy.wait_ge(sRV, rv0 + j + 1)
                    sy.dma_start(outd[b, dcO, :, tqq * 512:(tqq + 1) * 512],
                                 o2buf[j % 2]).then_inc(sOp[j % 2], 16)
            C["sO0"] += 16 * 16
            C["sO1"] += 16 * 16

    import os
    nph = int(os.environ.get("ATT_NPH", "999"))
    done = [0]

    def _run(f, *a):
        if done[0] < nph:
            f(*a)
            done[0] += 1

    for b in range(B_C):
        _run(ln_phase, b, 0)
        _run(vg_phase, b)
        _run(attn_phase, b)
        _run(out_phase, b)
        _run(ln_phase, b, 1)
        _run(ffn_phase, b)

    if nph >= 6 * B_C:
        with nc.Block() as blk:
            @blk.sync
            def _(sy):
                sy.wait_ge(sXWp[0], 16 * 16 * B_C)
                sy.wait_ge(sXWp[1], 16 * 16 * B_C)
                sy.wait_ge(sOp[0], 16 * 16 * B_C)
                sy.wait_ge(sOp[1], 16 * 16 * B_C)

    return nc


_CACHE = {}


def _consts():
    if "oht" not in _CACHE:
        bt = _bucket_table()
        oh = np.zeros((320, 4096), dtype=BF)
        oh[bt, np.arange(4095)] = 1.0
        _CACHE["oht"] = oh
    oh = _CACHE["oht"]
    i128 = np.ascontiguousarray(np.eye(128, dtype=BF))
    selR = np.zeros((2, 128), dtype=BF)
    selR[1, 0:64] = 1.0
    selR[0, 64:128] = 1.0
    sel232 = np.zeros((2, 32), dtype=BF)
    sel232[0, 0:16] = 1.0
    sel232[1, 16:32] = 1.0
    p82 = np.zeros((8, 2), dtype=BF)
    p82[0:4, 0] = 1.0
    p82[4:8, 1] = 1.0
    return oh, i128, selR, sel232, p82


def _pack_w(w):
    return np.ascontiguousarray(
        np.asarray(w, np.float32).T.reshape(8, 128, 1024).transpose(1, 0, 2).reshape(128, 8192))


def _pack_wo(w):
    return np.ascontiguousarray(
        np.asarray(w, np.float32).T.reshape(8, 128, 8, 128).transpose(1, 0, 2, 3).reshape(128, 8192))


def kernel(L0_attn_hidden_states, L0_attn_q_w, L0_attn_q_b, L0_attn_k_w, L0_attn_k_b,
           L0_attn_v_w, L0_attn_v_b, L0_attn_out_w, L0_attn_out_b, L0_attn_rel_embed,
           L0_attn_gru_const, L0_attn_gru_w, L0_attn_gru_b, L0_ffn_weight_fp8,
           L0_ffn_bias, L0_ffn_input_scale, L0_ffn_weight_scale):
    x = np.asarray(L0_attn_hidden_states, dtype=np.float32)
    oh, i128, selR, sel232, p82 = _consts()

    shared = {
        "wq": _pack_w(L0_attn_q_w),
        "wk": _pack_w(L0_attn_k_w),
        "wv": _pack_w(L0_attn_v_w),
        "wo": _pack_wo(L0_attn_out_w),
        "w8": _pack_w(L0_ffn_weight_fp8),
        "bq": np.ascontiguousarray(np.asarray(L0_attn_q_b, np.float32).reshape(8, 128).T),
        "bk": np.ascontiguousarray(np.asarray(L0_attn_k_b, np.float32).reshape(8, 128).T),
        "bvr": np.ascontiguousarray(np.asarray(L0_attn_v_b, np.float32)[None, :]),
        "bo": np.ascontiguousarray(np.asarray(L0_attn_out_b, np.float32).reshape(8, 128).T),
        "bfn": np.ascontiguousarray(np.asarray(L0_ffn_bias, np.float32).reshape(8, 128).T),
        "rel": np.ascontiguousarray(np.asarray(L0_attn_rel_embed, np.float32)),
        "oht": oh,
        "gw": np.ascontiguousarray(np.asarray(L0_attn_gru_w, np.float32)),
        "gbi": np.ascontiguousarray(np.asarray(L0_attn_gru_b, np.float32).reshape(8, 1)),
        "gci": np.ascontiguousarray(np.asarray(L0_attn_gru_const, np.float32).reshape(16, 1)),
        "scli": np.array([[float(np.asarray(L0_ffn_input_scale).ravel()[0]),
                           float(np.asarray(L0_ffn_weight_scale).ravel()[0])]], np.float32),
        "i128": i128, "selR": selR, "sel232": sel232, "p82": p82,
    }
    xt = np.ascontiguousarray(x.reshape(32, T, 8, 128).transpose(0, 2, 3, 1))

    nc = build_nc()
    in_maps = []
    for c in range(8):
        m = dict(shared)
        m["x"] = np.ascontiguousarray(xt[4 * c:4 * c + 4])
        in_maps.append(m)
    res = run_bass_kernel_spmd(nc, in_maps, core_ids=list(range(8)))
    outs = []
    for c in range(8):
        o = np.asarray(res.results[c]["out"])
        outs.append(o.reshape(4, 1024, T).transpose(0, 2, 1))
    return np.ascontiguousarray(np.concatenate(outs, axis=0), dtype=np.float32)


# version 12
# speedup vs baseline: 1.5423x; 1.0100x over previous
import contextlib

import numpy as np
import ml_dtypes

import concourse.bass as bass
import concourse.mybir as mybir
from concourse.bass_utils import run_bass_kernel_spmd

F32 = mybir.dt.float32
BF16 = mybir.dt.bfloat16
FP8 = mybir.dt.float8e4
U8 = mybir.dt.uint8
AF = mybir.ActivationFunctionType
OP = mybir.AluOpType
BF = ml_dtypes.bfloat16

B_C = 4
T = 2048
NHP = 8
NTQ = 4
TQ = 512
NS0 = 16
NDC = 8
EXPB = -4.0
NB_IL = NHP * NTQ * NS0
GDPB = 16 + 16 * 2 * NHP

NUM_BUCKETS = 320
MAX_DISTANCE = 800


def _bucket_table():
    import math
    d = np.arange(-(T - 1), T)
    nb = NUM_BUCKETS // 2
    rel_b = (d > 0).astype(np.int64) * nb
    rp = np.abs(d)
    me = nb // 2
    small = rp < me
    rl = np.log(np.maximum(rp, 1).astype(np.float32) / me)
    rl = rl / math.log(MAX_DISTANCE / me)
    rl = rl * (nb - me)
    rl = (me + rl).astype(np.int64)
    rl = np.minimum(rl, nb - 1)
    return rel_b + np.where(small, rp, rl)


def build_nc():
    nc = bass.Bass("TRN2", target_bir_lowering=False)

    xd = nc.dram_tensor("x", [B_C, NDC, 128, T], F32, kind="ExternalInput")
    wqd = nc.dram_tensor("wq", [128, 8192], F32, kind="ExternalInput")
    wkd = nc.dram_tensor("wk", [128, 8192], F32, kind="ExternalInput")
    wvd = nc.dram_tensor("wv", [128, 8192], F32, kind="ExternalInput")
    wod = nc.dram_tensor("wo", [128, 8192], F32, kind="ExternalInput")
    w8d = nc.dram_tensor("w8", [128, 8192], F32, kind="ExternalInput")
    bqd = nc.dram_tensor("bq", [128, 8], F32, kind="ExternalInput")
    bkd = nc.dram_tensor("bk", [128, 8], F32, kind="ExternalInput")
    bvrd = nc.dram_tensor("bvr", [1, 1024], F32, kind="ExternalInput")
    bod = nc.dram_tensor("bo", [128, 8], F32, kind="ExternalInput")
    bfd = nc.dram_tensor("bfn", [128, 8], F32, kind="ExternalInput")
    reld = nc.dram_tensor("rel", [320, 16], F32, kind="ExternalInput")
    ohtd = nc.dram_tensor("oht", [320, 4096], BF16, kind="ExternalInput")
    gwd_d = nc.dram_tensor("gw", [8, 64], F32, kind="ExternalInput")
    gbid = nc.dram_tensor("gbi", [8, 1], F32, kind="ExternalInput")
    gcid = nc.dram_tensor("gci", [16, 1], F32, kind="ExternalInput")
    sclid = nc.dram_tensor("scli", [1, 2], F32, kind="ExternalInput")
    i128d = nc.dram_tensor("i128", [128, 128], BF16, kind="ExternalInput")
    selRd = nc.dram_tensor("selR", [2, 128], BF16, kind="ExternalInput")
    sel232d = nc.dram_tensor("sel232", [2, 32], BF16, kind="ExternalInput")
    p82d = nc.dram_tensor("p82", [8, 2], BF16, kind="ExternalInput")

    outd = nc.dram_tensor("out", [B_C, NDC, 128, T], F32, kind="ExternalOutput")

    rbd = nc.dram_tensor("rbd", [16, 4096], BF16, kind="Internal")
    x1d = nc.dram_tensor("x1d", [B_C, NDC, 128, T], F32, kind="Internal")
    statscr = nc.dram_tensor("statscr", [4096], F32, kind="Internal")
    denscr = nc.dram_tensor("denscr", [4096], F32, kind="Internal")
    gatescr = nc.dram_tensor("gatescr", [16, 2048], BF16, kind="Internal")

    es = contextlib.ExitStack()
    nc._kernel_es = es

    def sb(name, shape, dt_):
        return es.enter_context(nc.sbuf_tensor(name, shape, dt_))

    def sem(name):
        return es.enter_context(nc.semaphore(name))

    arA = sb("arA", [128, 49408], U8)
    arB = sb("arB", [128, 32768], U8)
    arC = sb("arC", [128, 32768], U8)

    vT = arA[:, :].bitcast(BF16)
    vTr = vT.rearrange("p (s h c) -> p s h c", s=NS0, h=NHP, c=193)
    wosv = arA[:, 0:16384].bitcast(BF16)
    w8sv = arA[:, 16384:24576].bitcast(FP8)
    z8v = arA[:, 24576:40960].bitcast(FP8)
    xbf = arB[:, :].bitcast(BF16)
    ctxn = arC[:, :].bitcast(BF16)
    mbc = arC[:, 0:8192].bitcast(F32)
    ibc = arC[:, 8192:16384].bitcast(F32)
    gsigA = arC[:, 16384:24576].bitcast(F32)
    gsigB = arC[:, 24576:32768].bitcast(F32)
    wvsv = arC[:, 0:16384].bitcast(BF16)

    xstage_t = [sb(f"xst{i}", [128, 2432], F32) for i in range(2)]
    xstage = [x[:, 0:2048] for x in xstage_t]
    ctxu = [sb(f"ctxu{i}", [128, T], BF16) for i in range(2)]
    kqp = sb("kqp", [128, 2 * T], BF16)
    sqst = kqp[:, 0:2048]
    sq2 = kqp[:, 2048:4096]
    gbct = [sb(f"gbc{i}", [128, 2 * T], BF16) for i in range(2)]
    tstt = [g[:, :].bitcast(F32) for g in gbct]
    wqs = [sb(f"wqs{i}", [128, 1024], BF16) for i in range(2)]
    wks = [sb(f"wks{i}", [128, 1024], BF16) for i in range(2)]
    obuf = [w[:, :].bitcast(F32) for w in wqs]
    xrst = [w[:, :].bitcast(F32) for w in wks]
    pbT = [x[:, :].bitcast(BF16) for x in xstage_t]
    Gb = [sb(f"Gb{i}", [128, 1024], BF16) for i in range(2)]
    exb = [sb(f"exb{i}", [128, 1024], BF16) for i in range(3)]
    o2buf = [g[:, :].bitcast(F32) for g in Gb]
    dstg = sb("dstg", [128, 2048], F32)
    sstv = dstg[0:1, 0:2048]
    d128 = sb("d128", [128, 32], F32)
    d128r = sb("d128r", [128, 32], F32)
    bcps = sb("bcps", [128, 2048], BF16)
    gate16 = sb("gate16", [16, 2048], BF16)
    dstat = sb("dstat", [128, 48], F32)

    bq8 = sb("bq8", [128, 8], F32)
    bqraw = sb("bqraw", [128, 8], F32)
    bks = sb("bks", [128, 8], F32)
    bos = sb("bos", [128, 8], F32)
    bfs = sb("bfs", [128, 8], F32)
    vbr = sb("vbr", [1, 1024], BF16)
    gwdb = sb("gwdb", [8, 128], BF16)
    gbr = sb("gbr", [8, 1], BF16)
    gcs = sb("gcs", [16, 1], F32)
    scl = sb("scl", [1, 2], F32)
    invs_sb = sb("invs", [1, 1], F32)
    sws_sb = sb("sws", [1, 1], F32)
    isc = sb("isc", [128, 2], F32)
    i128s = sb("i128s", [128, 128], BF16)
    selRs = sb("selRs", [2, 128], BF16)
    sel232s = sb("sel232s", [2, 32], BF16)
    p82s = sb("p82s", [8, 2], BF16)
    Tg = sb("Tg", [128, 256], BF16)
    gb2s = sb("gb2s", [2, 1], BF16)
    gb32f = sb("gb32f", [16, 2], F32)
    smalls = sb("smalls", [128, 4], F32)
    onesb = smalls[:, 0:1].bitcast(BF16)[:, 0:1]
    ones1b = sb("ones1b", [1, 128], BF16)
    ones1f = sb("ones1f", [1, 128], F32)
    ebias = smalls[:, 1:2]
    eighth = smalls[:, 2:3]
    dumm = smalls[0:1, 3:4]

    sWD = sem("sWD")
    sGS = sem("sGS")
    sGDp = [sem("sGD0"), sem("sGD1")]
    sXp = [sem("sX0"), sem("sX1")]
    sPp = [sem(f"sP{i}") for i in range(2)]
    sWqp = [sem("sWq0"), sem("sWq1")]
    sWkp = [sem("sWk0"), sem("sWk1")]
    sXWp = [sem("sXW0"), sem("sXW1")]
    sOp = [sem("sO0"), sem("sO1")]
    sSQ = sem("sSQ")
    sCB = sem("sCB")
    sST = sem("sST")
    sSC = sem("sSC")
    sBD = sem("sBD")
    sAP = sem("sAP")
    sQK = sem("sQK")
    sEV = sem("sEV")
    sG = sem("sG")
    sS = sem("sS")
    sE = sem("sE")
    sA = sem("sA")
    sV = sem("sV")
    sDD = sem("sDD")
    sR = sem("sR")
    sB = sem("sB")
    sN = sem("sN")
    sOT = sem("sOT")
    sOE = sem("sOE")
    sRV = sem("sRV")
    sTI = sem("sTI")
    sVI = sem("sVI")
    sGE = sem("sGE")
    sVC = sem("sVC")
    sVS = sem("sVS")
    sXs = [sem("sXs0"), sem("sXs1")]
    sBs = sem("sBs")

    C = {k: 0 for k in ("sWD sGS sGD0 sGD1 sX0 sX1 sP0 sP1 sWq0 sWq1 sWk0 sWk1 "
                        "sXW0 sXW1 sO0 sO1 sSQ sCB sST sSC sBD sAP sQK sEV sDD "
                        "sOT sOE sRV sGE sVC sXs0 sXs1 sBs").split()}

    def ig(b, il):
        return b * NB_IL + il

    def jslot(b, hp, tq):
        return (b * NHP + hp) * NTQ + tq

    ohts = arA[:, 0:24576].bitcast(BF16)
    rbsb = arA[:, 24576:40960].bitcast(BF16)[0:16, 0:4096]
    rels = arC[:, 0:96].bitcast(BF16)
    with nc.Block() as blk:
        @blk.gpsimd
        def _(g):
            for dst, src in (
                (bqraw[:, :], bqd[:, :]), (bks[:, :], bkd[:, :]),
                (bos[:, :], bod[:, :]), (bfs[:, :], bfd[:, :]),
                (vbr[:, :], bvrd[:, :]),
                (rels[0:128, 0:16], reld[0:128, :]),
                (rels[0:128, 16:32], reld[128:256, :]),
                (rels[0:64, 32:48], reld[256:320, :]),
                (ohts[0:128, 0:4096], ohtd[0:128, :]),
                (ohts[0:128, 4096:8192], ohtd[128:256, :]),
                (ohts[0:64, 8192:12288], ohtd[256:320, :]),
                (gwdb[:, 0:64], gwd_d[:, :]), (gwdb[:, 64:128], gwd_d[:, :]),
                (gbr[:, :], gbid[:, :]), (gcs[:, :], gcid[:, :]),
                (scl[:, :], sclid[:, :]), (i128s[:, :], i128d[:, :]),
                (selRs[:, :], selRd[:, :]), (sel232s[:, :], sel232d[:, :]),
                (p82s[:, :], p82d[:, :]),
            ):
                g.dma_start(dst, src).then_inc(sWD, 16)
            g.memset(onesb[:, :], 1.0)
            g.memset(ones1b[:, :], 1.0)
            g.memset(ones1f[:, :], 1.0)
            g.memset(ebias[:, :], EXPB)
            g.memset(eighth[:, :], 0.125)
            g.memset(dumm[:, :], 0.0)
            g.memset(dstg[:, :], 0.0)
            g.memset(Tg[:, :], 0.0).then_inc(sTI, 1)
            g.wait_ge(sVI, 4)
            g.dma_start(rbd[:, :], rbsb[:, :]).then_inc(sWD, 16)

        with (
            nc.psum_tensor("rbps", [16, 1024], F32) as rbps,
            nc.psum_tensor("gw2ps", [128, 2], F32) as gw2ps,
            nc.psum_tensor("gb2ps", [2, 1], F32) as gb2ps,
            nc.psum_tensor("gb32ps", [16, 2], F32) as gb32ps,
            nc.psum_tensor("iscps", [128, 2], F32) as iscps,
        ):
            @blk.tensor
            def _(t):
                t.wait_ge(sWD, 16 * 20)
                t.wait_ge(sTI, 1)
                cnts = [128, 128, 64]
                for n4 in range(4):
                    if n4 >= 1:
                        t.wait_ge(sVI, n4)
                    for c in range(3):
                        for half in range(2):
                            mm = t.matmul(rbps[:, half * 512:(half + 1) * 512],
                                          rels[0:cnts[c], 16 * c:16 * (c + 1)],
                                          ohts[0:cnts[c],
                                               c * 4096 + n4 * 1024 + half * 512:
                                               c * 4096 + n4 * 1024 + (half + 1) * 512],
                                          start=(c == 0), stop=(c == 2))
                    mm.then_inc(sTI, 1)
                t.matmul(gw2ps[:, :], gwdb[:, :], p82s[:, :],
                         start=True, stop=True).then_inc(sTI, 1)
                t.matmul(gb2ps[:, :], p82s[:, :], gbr[:, :],
                         start=True, stop=True).then_inc(sTI, 1)
                t.wait_ge(sVI, 6)
                t.matmul(gb32ps[:, 0:1], sel232s[:, 0:16], gb2s[:, :],
                         start=True, stop=True)
                t.matmul(gb32ps[:, 1:2], sel232s[:, 16:32], gb2s[:, :],
                         start=True, stop=True).then_inc(sTI, 1)
                t.wait_ge(sVI, 7)
                t.matmul(iscps[:, 0:1], ones1f[:, :], invs_sb[:, :], start=True, stop=True)
                t.matmul(iscps[:, 1:2], ones1f[:, :], sws_sb[:, :],
                         start=True, stop=True).then_inc(sTI, 1)

            @blk.vector
            def _(v):
                for n4 in range(4):
                    v.wait_ge(sTI, 2 + n4)
                    v.tensor_copy(rbsb[:, n4 * 1024:(n4 + 1) * 1024],
                                  rbps[:, :]).then_inc(sVI, 1)
                v.wait_ge(sTI, 6)
                for hp in range(NHP):
                    a0 = hp * 16
                    b0 = 128 + hp * 16
                    v.tensor_copy(Tg[0:64, a0 + 2 * hp:a0 + 2 * hp + 1], gw2ps[0:64, 0:1])
                    v.tensor_copy(Tg[64:128, a0 + 2 * hp + 1:a0 + 2 * hp + 2], gw2ps[64:128, 0:1])
                    v.tensor_copy(Tg[0:64, b0 + 2 * hp:b0 + 2 * hp + 1], gw2ps[0:64, 1:2])
                    cp = v.tensor_copy(Tg[64:128, b0 + 2 * hp + 1:b0 + 2 * hp + 2], gw2ps[64:128, 1:2])
                cp.then_inc(sVI, 1)
                v.wait_ge(sTI, 7)
                v.tensor_copy(gb2s[:, :], gb2ps[:, :]).then_inc(sVI, 1)
                v.reciprocal(invs_sb[:, :], scl[0:1, 0:1])
                v.tensor_mul(sws_sb[:, :], scl[0:1, 0:1], scl[0:1, 1:2]).then_inc(sVI, 1)
                v.wait_ge(sTI, 8)
                v.tensor_copy(gb32f[:, :], gb32ps[:, :])
                v.wait_ge(sTI, 9)
                v.tensor_copy(isc[:, :], iscps[:, :])
                v.tensor_scalar(bq8[:, :], bqraw[:, :], 0.125, None,
                                OP.mult).then_inc(sVI, 1)

    C["sWD"] = 16 * 21
    RBD_READY = C["sWD"]

    def ln_phase(b, which):
        src = xd if which == 0 else x1d
        with (
            nc.Block() as blk,
            nc.psum_tensor(f"sxp{b}_{which}", [1, 2048], F32) as sxp,
            nc.psum_tensor(f"sx2p{b}_{which}", [1, 2048], F32) as sx2p
        ):
            x0 = [C["sX0"], C["sX1"]]
            cb0 = C["sCB"]
            sq0 = C["sSQ"]
            st0 = C["sST"]
            sc0 = C["sSC"]
            bd0 = C["sBD"]
            bsF = C["sBs"]

            @blk.gpsimd
            def _(g):
                if which == 1:
                    g.wait_ge(sXWp[0], 16 * 16 * (b + 1))
                    g.wait_ge(sXWp[1], 16 * 16 * (b + 1))
                for dc in range(NDC):
                    if dc >= 2:
                        g.wait_ge(sCB, cb0 + dc - 1)
                        g.wait_ge(sSQ, sq0 + dc - 1)
                    g.dma_start(xstage[dc % 2], src[b, dc, :, :]).then_inc(sPq := sXp[dc % 2], 16)
                g.wait_ge(sBD, bd0)
                g.wait_ge(sBs, bsF)
                g.wait_ge(sSC, sc0 + 1)
                g.dma_start(bass.AP(statscr, 0, [[2048, 1], [1, 2048]]), sstv).then_inc(sBD, 16)
                g.wait_ge(sSC, sc0 + 2)
                g.dma_start(bass.AP(statscr, 2048, [[2048, 1], [1, 2048]]), sstv).then_inc(sBD, 16)
                g.wait_ge(sBD, bd0 + 32)
                g.dma_start(bass.AP(dstat, 0, [[48, 128], [1, 16]]),
                            bass.AP(statscr, 0, [[16, 128], [1, 16]])).then_inc(sBD, 16)
                g.dma_start(bass.AP(dstat, 16, [[48, 128], [1, 16]]),
                            bass.AP(statscr, 2048, [[16, 128], [1, 16]])).then_inc(sBD, 16)
                g.wait_ge(sSC, sc0 + 3)
                g.dma_start(bass.AP(statscr, 0, [[16, 128], [1, 16]]),
                            dstat[:, 0:16]).then_inc(sBD, 16)
                g.wait_ge(sSC, sc0 + 4)
                g.dma_start(bass.AP(statscr, 2048, [[16, 128], [1, 16]]),
                            dstat[:, 32:48]).then_inc(sBD, 16)
                g.wait_ge(sBD, bd0 + 96)
                g.dma_start(mbc, bass.AP(statscr, 0, [[0, 128], [1, 2048]])).then_inc(sBD, 16)
                g.dma_start(ibc, bass.AP(statscr, 2048, [[0, 128], [1, 2048]])).then_inc(sBD, 16)
            C["sX0"] += 16 * (NDC // 2)
            C["sX1"] += 16 * (NDC // 2)
            C["sBD"] += 16 * 8

            @blk.vector
            def _(v):
                for dc in range(NDC):
                    v.wait_ge(sXp[dc % 2], x0[dc % 2] + 16 * (dc // 2 + 1))
                    if which == 1 and dc >= 1:
                        v.wait_ge(sST, st0 + dc)
                    tgt = xbf[:, dc * T:(dc + 1) * T] if which == 0 else sqst
                    v.tensor_copy(tgt, xstage[dc % 2]).then_inc(sCB, 1)
                v.wait_ge(sST, st0 + NDC)
                v.wait_ge(sDD, 16 * 11 * NHP * (b + which))
                v.tensor_copy(sstv, sxp[0:1, :]).then_inc(sSC, 1)
                v.wait_ge(sBD, bd0 + 16)
                v.tensor_copy(sstv, sx2p[0:1, :]).then_inc(sSC, 1)
                v.wait_ge(sBD, bd0 + 64)
                v.tensor_scalar(dstat[:, 0:16], dstat[:, 0:16], 1.0 / 1024.0, None,
                                OP.mult).then_inc(sVC, 1)
                v.tensor_scalar(dstat[:, 16:32], dstat[:, 16:32], 1.0 / 1024.0, 1e-5,
                                OP.mult, OP.add).then_inc(sVC, 1)
                v.wait_ge(sVC, C["sVC"] + 2)
                v.tensor_mul(dstat[:, 32:48], dstat[:, 0:16],
                             dstat[:, 0:16]).then_inc(sVC, 1)
                v.wait_ge(sVC, C["sVC"] + 3)
                v.tensor_sub(dstat[:, 16:32], dstat[:, 16:32],
                             dstat[:, 32:48]).then_inc(sVC, 1)
                v.wait_ge(sVC, C["sVC"] + 4)
                v.reciprocal(dstat[:, 32:48], dstat[:, 16:32]).then_inc(sSC, 1)
                v.wait_ge(sSQ, sq0 + NDC + 1)
                if which == 1:
                    v.tensor_scalar(dstat[:, 32:48], dstat[:, 32:48], isc[:, 0:1],
                                    None, OP.mult).then_inc(sSC, 1)
                else:
                    v.tensor_copy(dumm[:, :], dumm[:, :]).then_inc(sSC, 1)
            C["sCB"] += NDC
            C["sSC"] += 4
            C["sVC"] += 4

            @blk.scalar
            def _(s):
                for dc in range(NDC):
                    s.wait_ge(sXp[dc % 2], x0[dc % 2] + 16 * (dc // 2 + 1))
                    if dc >= 1:
                        s.wait_ge(sST, st0 + dc)
                    tgt = sqst if which == 0 else sq2
                    s.activation(tgt, xstage[dc % 2], AF.Square).then_inc(sSQ, 1)
                s.wait_ge(sSC, sc0 + 3)
                s.activation(dstat[:, 32:48], dstat[:, 32:48], AF.Sqrt).then_inc(sSQ, 1)
            C["sSQ"] += NDC + 1

            @blk.tensor
            def _(t):
                for dc in range(NDC):
                    t.wait_ge(sCB, cb0 + dc + 1)
                    t.wait_ge(sSQ, sq0 + dc + 1)
                    xsrc = xbf[:, dc * T:(dc + 1) * T] if which == 0 else sqst
                    qsrc = sqst if which == 0 else sq2
                    for sl in range(4):
                        t.matmul(sxp[0:1, sl * 512:(sl + 1) * 512], onesb[:, :],
                                 xsrc[:, sl * 512:(sl + 1) * 512],
                                 start=(dc == 0), stop=(dc == NDC - 1))
                    for sl in range(4):
                        mm = t.matmul(sx2p[0:1, sl * 512:(sl + 1) * 512], onesb[:, :],
                                      qsrc[:, sl * 512:(sl + 1) * 512],
                                      start=(dc == 0), stop=(dc == NDC - 1))
                    mm.then_inc(sST, 1)
            C["sST"] += NDC

        with nc.Block() as blk:
            x0 = [C["sX0"], C["sX1"]]
            ap0 = C["sAP"]
            bd_all = C["sBD"]

            apc_g = 2 if which == 0 else 3

            @blk.gpsimd
            def _(g):
                for dc in range(NDC):
                    if dc >= 2:
                        g.wait_ge(sAP, ap0 + apc_g * (dc - 2) + 1)
                    g.dma_start(xstage[dc % 2], src[b, dc, :, :]).then_inc(sXp[dc % 2], 16)
            C["sX0"] += 16 * (NDC // 2)
            C["sX1"] += 16 * (NDC // 2)

            @blk.vector
            def _(v):
                v.wait_ge(sBD, bd_all)
                for dc in range(NDC):
                    v.wait_ge(sXp[dc % 2], x0[dc % 2] + 16 * (dc // 2 + 1))
                    if dc >= 1:
                        v.wait_ge(sAP, ap0 + apc_g * (dc - 1) + 2)
                    v.tensor_sub(tstt[0], xstage[dc % 2], mbc).then_inc(sAP, 1)
                    v.wait_ge(sAP, ap0 + apc_g * dc + 1)
                    if which == 0:
                        v.tensor_mul(xbf[:, dc * T:(dc + 1) * T], tstt[0],
                                     ibc).then_inc(sAP, 1)
                    else:
                        v.tensor_mul(tstt[1], tstt[0], ibc).then_inc(sAP, 1)
                        v.wait_ge(sAP, ap0 + apc_g * dc + 2)
                        v.tensor_scalar(z8v[:, dc * T:(dc + 1) * T], tstt[1],
                                        448.0, -448.0, OP.min,
                                        OP.max).then_inc(sAP, 1)
            C["sAP"] += apc_g * NDC

    ln0_shadow = {}

    def ln0_pre(b):
        with (
            nc.Block() as blk,
            nc.psum_tensor(f"sxpS{b}", [1, 2048], F32) as sxp,
            nc.psum_tensor(f"sx2pS{b}", [1, 2048], F32) as sx2p
        ):
            xs0 = [C["sXs0"], C["sXs1"]]
            cb0 = C["sCB"]
            sq0 = C["sSQ"]
            st0 = C["sST"]
            sc0 = C["sSC"]
            bd0 = C["sBD"]
            vc0 = C["sVC"]

            bs0 = C["sBs"]
            ln0_shadow[b] = bs0

            @blk.sync
            def _(sy):
                sy.wait_ge(sG, NB_IL * b)
                sy.wait_ge(sBD, bd0)
                sy.wait_ge(sBs, bs0)
                for dc in range(0, NDC, 2):
                    if dc >= 2:
                        sy.wait_ge(sCB, cb0 + dc - 1)
                        sy.wait_ge(sSQ, sq0 + dc - 1)
                    sy.dma_start(xstage[0], xd[b, dc, :, :]).then_inc(sXs[0], 16)
                sy.wait_ge(sSC, sc0 + 1)
                sy.dma_start(bass.AP(statscr, 0, [[2048, 1], [1, 2048]]), sstv).then_inc(sBs, 16)
                sy.wait_ge(sSC, sc0 + 2)
                sy.dma_start(bass.AP(statscr, 2048, [[2048, 1], [1, 2048]]), sstv).then_inc(sBs, 16)
                sy.wait_ge(sBs, bs0 + 32)
                sy.dma_start(bass.AP(dstat, 0, [[48, 128], [1, 16]]),
                             bass.AP(statscr, 0, [[16, 128], [1, 16]])).then_inc(sBs, 16)
                sy.dma_start(bass.AP(dstat, 16, [[48, 128], [1, 16]]),
                             bass.AP(statscr, 2048, [[16, 128], [1, 16]])).then_inc(sBs, 16)
                sy.wait_ge(sSC, sc0 + 3)
                sy.dma_start(bass.AP(statscr, 0, [[16, 128], [1, 16]]),
                             dstat[:, 0:16]).then_inc(sBs, 16)
                sy.wait_ge(sSC, sc0 + 4)
                sy.dma_start(bass.AP(statscr, 2048, [[16, 128], [1, 16]]),
                             dstat[:, 32:48]).then_inc(sBs, 16)
            C["sXs0"] += 16 * (NDC // 2)
            C["sBs"] += 16 * 6

            @blk.scalar
            def _(s):
                s.dma_start(xstage[1], xd[b, 1, :, :]).then_inc(sXs[1], 16)
                for dc in range(NDC):
                    s.wait_ge(sXs[dc % 2], xs0[dc % 2] + 16 * (dc // 2 + 1))
                    if dc >= 1:
                        s.wait_ge(sST, st0 + dc)
                    s.activation(sqst, xstage[dc % 2], AF.Square).then_inc(sSQ, 1)
                    if dc % 2 == 1 and dc + 2 < NDC:
                        s.wait_ge(sCB, cb0 + dc + 1)
                        s.wait_ge(sSQ, sq0 + dc + 1)
                        s.dma_start(xstage[1], xd[b, dc + 2, :, :]).then_inc(sXs[1], 16)
                s.wait_ge(sSC, sc0 + 3)
                s.activation(dstat[:, 32:48], dstat[:, 32:48], AF.Sqrt).then_inc(sSQ, 1)
            C["sSQ"] += NDC + 1
            C["sXs1"] += 16 * (NDC // 2)

            @blk.vector
            def _(v):
                for dc in range(NDC):
                    v.wait_ge(sXs[dc % 2], xs0[dc % 2] + 16 * (dc // 2 + 1))
                    v.tensor_copy(xbf[:, dc * T:(dc + 1) * T],
                                  xstage[dc % 2]).then_inc(sCB, 1)
                v.wait_ge(sST, st0 + NDC)
                v.wait_ge(sDD, 16 * 11 * NHP * b)
                v.tensor_copy(sstv, sxp[0:1, :]).then_inc(sSC, 1)
                v.wait_ge(sBs, bs0 + 16)
                v.tensor_copy(sstv, sx2p[0:1, :]).then_inc(sSC, 1)
                v.wait_ge(sBs, bs0 + 64)
                v.tensor_scalar(dstat[:, 0:16], dstat[:, 0:16], 1.0 / 1024.0, None,
                                OP.mult).then_inc(sVC, 1)
                v.tensor_scalar(dstat[:, 16:32], dstat[:, 16:32], 1.0 / 1024.0, 1e-5,
                                OP.mult, OP.add).then_inc(sVC, 1)
                v.wait_ge(sVC, vc0 + 2)
                v.tensor_mul(dstat[:, 32:48], dstat[:, 0:16],
                             dstat[:, 0:16]).then_inc(sVC, 1)
                v.wait_ge(sVC, vc0 + 3)
                v.tensor_sub(dstat[:, 16:32], dstat[:, 16:32],
                             dstat[:, 32:48]).then_inc(sVC, 1)
                v.wait_ge(sVC, vc0 + 4)
                v.reciprocal(dstat[:, 32:48], dstat[:, 16:32]).then_inc(sSC, 1)
                v.wait_ge(sSQ, sq0 + NDC + 1)
                v.tensor_copy(dumm[:, :], dumm[:, :]).then_inc(sSC, 1)
            C["sCB"] += NDC
            C["sSC"] += 4
            C["sVC"] += 4

            @blk.tensor
            def _(t):
                for dc in range(NDC):
                    t.wait_ge(sCB, cb0 + dc + 1)
                    t.wait_ge(sSQ, sq0 + dc + 1)
                    for sl in range(4):
                        t.matmul(sxp[0:1, sl * 512:(sl + 1) * 512], onesb[:, :],
                                 xbf[:, dc * T + sl * 512:dc * T + (sl + 1) * 512],
                                 start=(dc == 0), stop=(dc == NDC - 1))
                    for sl in range(4):
                        mm = t.matmul(sx2p[0:1, sl * 512:(sl + 1) * 512], onesb[:, :],
                                      sqst[:, sl * 512:(sl + 1) * 512],
                                      start=(dc == 0), stop=(dc == NDC - 1))
                    mm.then_inc(sST, 1)
            C["sST"] += NDC

    def ln0_post(b):
        with nc.Block() as blk:
            bs0 = ln0_shadow.pop(b)
            xs0 = [C["sXs0"], C["sXs1"]]
            ap0 = C["sAP"]
            ot0 = C["sOT"]

            @blk.sync
            def _(sy):
                sy.wait_ge(sOT, ot0)
                sy.wait_ge(sBs, bs0 + 96)
                sy.dma_start(mbc, bass.AP(statscr, 0, [[0, 128], [1, 2048]])).then_inc(sBs, 16)
                sy.dma_start(ibc, bass.AP(statscr, 2048, [[0, 128], [1, 2048]])).then_inc(sBs, 16)
                for dc in range(NDC):
                    if dc >= 2:
                        sy.wait_ge(sAP, ap0 + 2 * (dc - 2) + 2)
                    sy.dma_start(xstage[dc % 2], xd[b, dc, :, :]).then_inc(sXs[dc % 2], 16)
            C["sBs"] += 16 * 2
            C["sXs0"] += 16 * (NDC // 2)
            C["sXs1"] += 16 * (NDC // 2)

            @blk.vector
            def _(v):
                v.wait_ge(sBs, bs0 + 128)
                for dc in range(NDC):
                    v.wait_ge(sXs[dc % 2], xs0[dc % 2] + 16 * (dc // 2 + 1))
                    if dc >= 1:
                        v.wait_ge(sAP, ap0 + 2 * (dc - 1) + 2)
                    v.tensor_sub(tstt[0], xstage[dc % 2], mbc).then_inc(sAP, 1)
                    v.wait_ge(sAP, ap0 + 2 * dc + 1)
                    v.tensor_mul(xbf[:, dc * T:(dc + 1) * T], tstt[0],
                                 ibc).then_inc(sAP, 1)
            C["sAP"] += 2 * NDC

    def vg_phase(b):
        with (
            nc.Block() as blk,
            nc.psum_tensor(f"vps0_{b}", [128, 512], F32) as vps0,
            nc.psum_tensor(f"vps1_{b}", [128, 512], F32) as vps1,
            nc.psum_tensor(f"gapsA_{b}", [16, 1024], F32) as gapsA,
            nc.psum_tensor(f"gapsB_{b}", [16, 1024], F32) as gapsB
        ):
            vps = [vps0, vps1]
            qk0 = C["sQK"]
            ev0 = C["sEV"]
            ge0 = C["sGE"]
            wd0 = C["sWD"]
            ap_vg = C["sAP"]

            @blk.gpsimd
            def _(g):
                g.wait_ge(sWD, RBD_READY)
                g.wait_ge(sAP, ap_vg)
                if b == 0:
                    g.memset(vT[:, :], 0.0).then_inc(sGE, 1)
                else:
                    g.nop().then_inc(sGE, 1)
                g.dma_start(wvsv, wvd[:, :]).then_inc(sWD, 16)
                g.wait_ge(sGE, ge0 + 3)
                g.dma_start(gatescr[:, :], gate16[:, :]).then_inc(sGS, 16)
            C["sWD"] += 16
            C["sGS"] += 16

            @blk.vector
            def _(v):
                v.wait_ge(sGE, ge0 + 1)
                v.memset(vTr[:, :, :, 64:65], 1.0)
                v.memset(vTr[:, :, :, 97:98], 1.0).then_inc(sGE, 1)
                v.wait_ge(sEV, ev0 + 2)
                v.tensor_scalar(tstt[0][0:16, 0:T], gsigB[0:16, 0:T], gcs[:, :], -1.0,
                                OP.mult, OP.add).then_inc(sVC, 1)
                v.wait_ge(sVC, C["sVC"] + 1)
                v.tensor_mul(tstt[1][0:16, 0:T], tstt[0][0:16, 0:T],
                             gsigA[0:16, 0:T]).then_inc(sVC, 1)
                v.wait_ge(sVC, C["sVC"] + 2)
                v.tensor_scalar(gate16[:, :], tstt[1][0:16, 0:T], 2.0, None,
                                OP.add).then_inc(sGE, 1)
            C["sGE"] += 3
            C["sVC"] += 2

            @blk.tensor
            def _(t):
                t.wait_ge(sWD, RBD_READY)
                t.wait_ge(sAP, ap_vg)
                for gh in range(2):
                    if gh == 1:
                        t.wait_ge(sEV, ev0 + 1)
                    for sl in range(2):
                        for hp in range(NHP):
                            xsl = xbf[:, hp * T + gh * 1024 + sl * 512:
                                      hp * T + gh * 1024 + (sl + 1) * 512]
                            t.matmul(gapsA[:, sl * 512:(sl + 1) * 512],
                                     Tg[:, hp * 16:(hp + 1) * 16], xsl,
                                     start=(hp == 0), stop=(hp == NHP - 1))
                            mm = t.matmul(gapsB[:, sl * 512:(sl + 1) * 512],
                                          Tg[:, 128 + hp * 16:128 + (hp + 1) * 16], xsl,
                                          start=(hp == 0), stop=(hp == NHP - 1))
                    mm.then_inc(sQK, 1)
                t.wait_ge(sWD, wd0 + 16)
                for tb in range(NS0):
                    for th in range(2):
                        k = 2 * tb + th
                        if k >= 2:
                            t.wait_ge(sEV, ev0 + 2 * k)
                        for dc in range(NDC):
                            t.matmul(vps[k % 2][:, :],
                                     xbf[:, dc * T + tb * 128:dc * T + (tb + 1) * 128],
                                     wvsv[:, dc * 1024 + th * 512:dc * 1024 + (th + 1) * 512],
                                     start=(dc == 0), stop=False)
                        t.matmul(vps[k % 2][:, :], ones1b[:, :],
                                 vbr[0:1, th * 512:(th + 1) * 512],
                                 start=False, stop=True).then_inc(sQK, 1)
            C["sQK"] += 2 + NS0 * 2

            @blk.scalar
            def _(s):
                for gh in range(2):
                    s.wait_ge(sQK, qk0 + gh + 1)
                    s.activation(gsigA[0:16, gh * 1024:(gh + 1) * 1024], gapsA[:, :],
                                 AF.Sigmoid, bias=gb32f[:, 0:1])
                    s.activation(gsigB[0:16, gh * 1024:(gh + 1) * 1024], gapsB[:, :],
                                 AF.Sigmoid, bias=gb32f[:, 1:2]).then_inc(sEV, 1)
                s.wait_ge(sGE, ge0 + 2)
                for tb in range(NS0):
                    for th in range(2):
                        k = 2 * tb + th
                        s.wait_ge(sQK, qk0 + 2 + k + 1)
                        vsrc = vps[k % 2][:, :].rearrange("p (m x d) -> p m x d",
                                                          m=4, x=2, d=64)
                        s.activation(vTr[:, tb, th * 4:(th + 1) * 4, 0:64],
                                     vsrc[:, :, 0, :], AF.Identity)
                        s.activation(vTr[:, tb, th * 4:(th + 1) * 4, 129:193],
                                     vsrc[:, :, 1, :], AF.Identity).then_inc(sEV, 2)
            C["sEV"] += 2 + NS0 * 2 * 2

    def attn_phase(b):
        with (
            nc.Block() as blk,
            nc.psum_tensor(f"scp0_{b}", [128, 1024], F32) as scp0,
            nc.psum_tensor(f"scp1_{b}", [128, 1024], F32) as scp1,
            nc.psum_tensor(f"cxA0_{b}", [65, 512], F32) as cxA0,
            nc.psum_tensor(f"cxB0_{b}", [128, 512], F32) as cxB0,
            nc.psum_tensor(f"cxA1_{b}", [65, 512], F32) as cxA1,
            nc.psum_tensor(f"cxB1_{b}", [128, 512], F32) as cxB1
        ):
            scp = [scp0, scp1]
            cxA = [cxA0, cxA1]
            cxB = [cxB0, cxB1]
            qk0 = C["sQK"]
            ev0 = C["sEV"]
            wq0 = [C["sWq0"], C["sWq1"]]
            wk0 = [C["sWk0"], C["sWk1"]]
            gd0 = [C["sGD0"], C["sGD1"]]
            p0 = [C["sP0"], C["sP1"]]
            ap_all = C["sAP"]
            cb_all = C["sCB"]
            sq_all = C["sSQ"]

            def emit_av(t, il):
                hp = il // (NTQ * NS0)
                tq = (il // NS0) % NTQ
                s0 = il % NS0
                i_g = ig(b, il)
                slot = jslot(b, hp, tq)
                t.wait_ge(sE, i_g + 1)
                if s0 == 0 and slot > 1:
                    t.wait_ge(sV, slot - 1)
                    t.wait_ge(sVS, slot - 1)
                base = s0 * 1544 + hp * 193
                t.matmul(cxA[slot % 2][:, :], vT[:, base:base + 65], exb[il % 3][:, 0:512],
                         start=(s0 == 0), stop=(s0 == NS0 - 1))
                t.matmul(cxB[slot % 2][:, :], vT[:, base + 65:base + 193], exb[il % 3][:, 512:1024],
                         start=(s0 == 0), stop=(s0 == NS0 - 1)).then_inc(sA, 1)

            @blk.gpsimd
            def _(g):
                def load_pair(hp):
                    g.dma_start(gbct[hp % 2][:, :].rearrange("p (h j) -> p h j", h=2, j=2048),
                                bass.AP(gatescr, (2 * hp) * 2048,
                                        [[0, 128], [2048, 2], [1, 2048]])).then_inc(sGDp[hp % 2], 16)
                    g.dma_start(wqs[hp % 2][:, :].rearrange("p (c d) -> p c d", c=8, d=128),
                                bass.AP(wqd, hp * 128,
                                        [[8192, 128], [1024, 8], [1, 128]])).then_inc(sWqp[hp % 2], 16)
                    g.dma_start(wks[hp % 2][:, :].rearrange("p (c d) -> p c d", c=8, d=128),
                                bass.AP(wkd, hp * 128,
                                        [[8192, 128], [1024, 8], [1, 128]])).then_inc(sWkp[hp % 2], 16)

                g.wait_ge(sGS, 16 * (b + 1))
                load_pair(0)
                load_pair(1)
                for hp in range(NHP):
                    hpg = b * NHP + hp
                    if hp + 2 < NHP:
                        nhp = hp + 2
                        g.wait_ge(sG, ig(b, (hp + 1) * NTQ * NS0))
                        g.wait_ge(sQK, qk0 + 4 * (nhp - 1))
                        load_pair(nhp)
                    g.wait_ge(sVS, jslot(b, hp, NTQ - 1) + 1)
                    g.wait_ge(sV, jslot(b, hp, NTQ - 1) + 1)
                    if hpg > 0:
                        g.wait_ge(sDD, 16 * (11 * (hpg - 1) + 11))
                        g.wait_ge(sN, 4 * hpg)
                    g.dma_start(bass.AP(denscr, 0, [[4096, 1], [1, 4096]]),
                                bass.AP(dstg, 32 * 2048, [[32 * 2048, 2], [1, 2048]])).then_inc(sDD, 16)
                    g.wait_ge(sDD, 16 * (11 * hpg + 1))
                    g.dma_start(bass.AP(d128, 0, [[32, 128], [1, 32]]),
                                bass.AP(denscr, 0, [[32, 128], [1, 32]])).then_inc(sDD, 16)
                    g.wait_ge(sR, hpg + 1)
                    g.dma_start(bass.AP(denscr, 0, [[32, 128], [1, 32]]),
                                d128r[:, :]).then_inc(sDD, 16)
                    g.wait_ge(sDD, 16 * (11 * hpg + 3))
                    for idx in range(4):
                        g.dma_start(bcps[0:64, idx * 512:(idx + 1) * 512],
                                    bass.AP(denscr, 2048 + idx * 512, [[0, 64], [1, 512]])).then_inc(sDD, 16)
                        g.dma_start(bcps[64:128, idx * 512:(idx + 1) * 512],
                                    bass.AP(denscr, idx * 512, [[0, 64], [1, 512]])).then_inc(sDD, 16)
            C["sWq0"] += 16 * (NHP // 2)
            C["sWq1"] += 16 * (NHP // 2)
            C["sWk0"] += 16 * (NHP // 2)
            C["sWk1"] += 16 * (NHP // 2)
            C["sGD0"] += 16 * (NHP // 2)
            C["sGD1"] += 16 * (NHP // 2)
            C["sDD"] += 16 * 11 * NHP

            @blk.sync
            def _(sy):
                sy.wait_ge(sWD, RBD_READY)
                sy.wait_ge(sCB, cb_all)
                sy.wait_ge(sSQ, sq_all)
                sy.wait_ge(sAP, ap_all)
                for hp in range(NHP):
                    for tq in range(NTQ):
                        idx = hp * NTQ + tq
                        if idx >= 2:
                            sy.wait_ge(sG, ig(b, (idx - 1) * NS0))
                        for h in range(2):
                            sy.dma_start(pbT[idx % 2][:, h * 2432:(h + 1) * 2432],
                                         bass.AP(rbd, (2 * hp + h) * 4096 + 1536 - tq * 512,
                                                 [[1, 128], [1, 2432]])).then_inc(sPp[idx % 2], 16)

            @blk.tensor
            def _(t):
                for hp in range(NHP):
                    hpg = b * NHP + hp
                    t.wait_ge(sWqp[hp % 2], wq0[hp % 2] + 16 * (hp // 2 + 1))
                    t.wait_ge(sWkp[hp % 2], wk0[hp % 2] + 16 * (hp // 2 + 1))
                    if hp == 0:
                        t.wait_ge(sAP, ap_all)
                    prev_last = ig(b, hp * NTQ * NS0)
                    if prev_last > 0:
                        t.wait_ge(sE, prev_last)
                    for which, wsl in ((0, wks), (1, wqs)):
                        if which == 1:
                            t.wait_ge(sEV, ev0 + 2 * hp + 1)
                        for half in range(2):
                            for dc in range(NDC):
                                for sl in range(2):
                                    mm = t.matmul(scp[half][:, sl * 512:(sl + 1) * 512],
                                                  wsl[hp % 2][:, dc * 128:(dc + 1) * 128],
                                                  xbf[:, dc * T + half * 1024 + sl * 512:
                                                      dc * T + half * 1024 + (sl + 1) * 512],
                                                  start=(dc == 0), stop=(dc == NDC - 1))
                            mm.then_inc(sQK, 1)
                    for tq in range(NTQ):
                        for s0 in range(NS0):
                            il = (hp * NTQ + tq) * NS0 + s0
                            i_g = ig(b, il)
                            if i_g >= 2:
                                t.wait_ge(sE, i_g - 1)
                            if tq == 0 and s0 == 0:
                                t.wait_ge(sEV, ev0 + 2 * hp + 2)
                            t.matmul(scp[il % 2][:, 0:512],
                                     kqp[0:64, s0 * 128:(s0 + 1) * 128],
                                     kqp[0:64, T + tq * 512:T + (tq + 1) * 512],
                                     start=True, stop=False, tile_position=(0, 0))
                            t.matmul(scp[il % 2][:, 512:1024],
                                     kqp[64:128, s0 * 128:(s0 + 1) * 128],
                                     kqp[64:128, T + tq * 512:T + (tq + 1) * 512],
                                     start=True, stop=False, tile_position=(64, 0))
                            if il > 1:
                                emit_av(t, il - 2)
                            t.wait_ge(sG, i_g + 1)
                            t.matmul(scp[il % 2][:, 0:512], i128s[:, :],
                                     Gb[il % 2][:, 0:512], start=False, stop=True)
                            t.matmul(scp[il % 2][:, 512:1024], i128s[:, :],
                                     Gb[il % 2][:, 512:1024],
                                     start=False, stop=True).then_inc(sS, 1)
                emit_av(t, NB_IL - 2)
                emit_av(t, NB_IL - 1)
            for _i in range(2):
                C[f"sP{_i}"] += 32 * (NHP * NTQ // 2)
            C["sQK"] += 4 * NHP

            @blk.scalar
            def _(s):
                den_pending = []

                for hp in range(NHP):
                    s.wait_ge(sQK, qk0 + 4 * hp + 2)
                    s.activation(kqp[:, 0:1024], scp0[:, :], AF.Identity,
                                 bias=bks[:, hp:hp + 1])
                    s.activation(kqp[:, 1024:2048], scp1[:, :], AF.Identity,
                                 bias=bks[:, hp:hp + 1]).then_inc(sEV, 1)
                    s.wait_ge(sQK, qk0 + 4 * hp + 4)
                    s.activation(kqp[:, T:T + 1024], scp0[:, :], AF.Identity,
                                 bias=bq8[:, hp:hp + 1], scale=eighth[:, :])
                    s.activation(kqp[:, T + 1024:2 * T], scp1[:, :], AF.Identity,
                                 bias=bq8[:, hp:hp + 1], scale=eighth[:, :]).then_inc(sEV, 1)
                    for tq in range(NTQ):
                        for s0 in range(NS0):
                            il = (hp * NTQ + tq) * NS0 + s0
                            i_g = ig(b, il)
                            if s0 == 3 and den_pending:
                                hp_e, tq_e, igl = den_pending[0]
                                sl_e = jslot(b, hp_e, tq_e)
                                s.wait_ge(sA, igl + 1)
                                s.activation(ctxu[hp_e % 2][64:128, tq_e * 512:(tq_e + 1) * 512],
                                             cxB[sl_e % 2][64:128, :], AF.Identity)
                            if s0 == 5 and den_pending:
                                hp_e, tq_e, igl = den_pending.pop(0)
                                hpg_e = b * NHP + hp_e
                                sl_e = jslot(b, hp_e, tq_e)
                                if tq_e == 0 and hpg_e > 0:
                                    s.wait_ge(sDD, 16 * (11 * (hpg_e - 1) + 1))
                                s.activation(dstg[32:33, tq_e * 512:(tq_e + 1) * 512],
                                             cxB[sl_e % 2][32:33, :], AF.Identity).then_inc(sVS, 1)
                            s.wait_ge(sS, i_g + 1)
                            if i_g >= 3:
                                s.wait_ge(sA, i_g - 2)
                            s.activation(exb[il % 3][:, :], scp[il % 2][:, :], AF.Exp,
                                         bias=ebias[:, :]).then_inc(sE, 1)
                            if s0 == NS0 - 1:
                                den_pending.append((hp, tq, i_g))
                while den_pending:
                    hp_e, tq_e, igl = den_pending.pop(0)
                    sl_e = jslot(b, hp_e, tq_e)
                    s.wait_ge(sA, igl + 1)
                    s.activation(ctxu[hp_e % 2][64:128, tq_e * 512:(tq_e + 1) * 512],
                                 cxB[sl_e % 2][64:128, :], AF.Identity)
                    s.activation(dstg[32:33, tq_e * 512:(tq_e + 1) * 512],
                                 cxB[sl_e % 2][32:33, :], AF.Identity).then_inc(sVS, 1)
            C["sEV"] += 2 * NHP

            @blk.vector
            def _(v):
                v.wait_ge(sXWp[0], 16 * 16 * b)
                v.wait_ge(sXWp[1], 16 * 16 * b)
                v.wait_ge(sOp[0], 16 * 16 * b)
                v.wait_ge(sOp[1], 16 * 16 * b)
                pending = []

                def emit_evac(vv, step):
                    hp_e, tq_e, igl = pending[0]
                    hpg_e = b * NHP + hp_e
                    sl = jslot(b, hp_e, tq_e)
                    cu = ctxu[hp_e % 2]
                    c0 = tq_e * 512
                    if step == 0:
                        vv.wait_ge(sA, igl + 1)
                        vv.tensor_copy(cu[0:64, c0:c0 + 512], cxA[sl % 2][0:64, :])
                    else:
                        if tq_e == 0 and hpg_e > 0:
                            vv.wait_ge(sDD, 16 * (11 * (hpg_e - 1) + 1))
                        vv.tensor_copy(dstg[64:65, c0:c0 + 512],
                                       cxA[sl % 2][64:65, :]).then_inc(sV, 1)
                        pending.pop(0)

                for hp in range(NHP):
                    hpg = b * NHP + hp
                    for tq in range(NTQ):
                        for s0 in range(NS0):
                            il = (hp * NTQ + tq) * NS0 + s0
                            i_g = ig(b, il)
                            if s0 == 2 and pending:
                                emit_evac(v, 0)
                            if s0 == 4 and pending:
                                emit_evac(v, 1)
                            if tq == 0 and s0 == 7 and hp > 0:
                                v.wait_ge(sDD, 16 * (11 * (hpg - 1) + 2))
                                v.reciprocal(d128r[:, :], d128[:, :]).then_inc(sR, 1)
                            if hp > 0 and tq == 1 and s0 in (2, 6, 10, 14):
                                idx = {2: 0, 6: 1, 10: 2, 14: 3}[s0]
                                hpp = hp - 1
                                v.wait_ge(sDD, 16 * (11 * (hpg - 1) + 11))
                                v.wait_ge(sV, jslot(b, hpp, NTQ - 1) + 1)
                                v.wait_ge(sVS, jslot(b, hpp, NTQ - 1) + 1)
                                v.tensor_mul(
                                    ctxn[:, hpp * T + idx * 512:hpp * T + (idx + 1) * 512],
                                    ctxu[hpp % 2][:, idx * 512:(idx + 1) * 512],
                                    bcps[:, idx * 512:(idx + 1) * 512]).then_inc(sN, 1)
                            idx = hp * NTQ + tq
                            if s0 == 0:
                                v.wait_ge(sPp[idx % 2], p0[idx % 2] + 32 * (idx // 2 + 1))
                            if il == 0 or (s0 == 0 and tq == 0):
                                v.wait_ge(sGDp[hp % 2], gd0[hp % 2] + 16 * (hp // 2 + 1))
                            if i_g >= 2:
                                v.wait_ge(sS, i_g - 1)
                            rev = bass.AP(pbT[idx % 2].tensor, s0 * 128 + 511,
                                          [[4864, 128], [2432, 2], [-1, 512]])
                            gsl = bass.AP(gbct[hp % 2], tq * 512,
                                          [[4096, 128], [2048, 2], [1, 512]])
                            v.tensor_mul(Gb[il % 2][:, :], rev, gsl).then_inc(sG, 1)
                            if s0 == NS0 - 1:
                                pending.append((hp, tq, i_g))
                while pending:
                    emit_evac(v, 0)
                    emit_evac(v, 1)
                hpgl = b * NHP + NHP - 1
                v.wait_ge(sDD, 16 * (11 * hpgl + 2))
                v.reciprocal(d128r[:, :], d128[:, :]).then_inc(sR, 1)
                v.wait_ge(sDD, 16 * (11 * hpgl + 11))
                for idx in range(4):
                    v.tensor_mul(ctxn[:, (NHP - 1) * T + idx * 512:(NHP - 1) * T + (idx + 1) * 512],
                                 ctxu[(NHP - 1) % 2][:, idx * 512:(idx + 1) * 512],
                                 bcps[:, idx * 512:(idx + 1) * 512]).then_inc(sN, 1)

    def out_phase(b):
        with (
            nc.Block() as blk,
            nc.psum_tensor(f"oup0_{b}", [128, 512], F32) as oup0,
            nc.psum_tensor(f"oup1_{b}", [128, 512], F32) as oup1
        ):
            oup = [oup0, oup1]
            ot0 = C["sOT"]
            oe0 = C["sOE"]
            rv0 = C["sRV"]
            x0 = [C["sX0"], C["sX1"]]
            xw0 = [C["sXW0"], C["sXW1"]]
            wd0 = C["sWD"]
            sc_all = C["sSC"]

            @blk.gpsimd
            def _(g):
                g.dma_start(wosv, wod[:, :]).then_inc(sWD, 16)
                for j in range(32):
                    tqq, dcO = divmod(j, 8)
                    if j >= 2:
                        g.wait_ge(sRV, rv0 + j - 1)
                    g.dma_start(xrst[j % 2],
                                xd[b, dcO, :, tqq * 512:(tqq + 1) * 512]).then_inc(sXp[j % 2], 16)
            C["sWD"] += 16
            C["sX0"] += 16 * 16
            C["sX1"] += 16 * 16

            @blk.tensor
            def _(t):
                t.wait_ge(sWD, wd0 + 16)
                t.wait_ge(sSC, sc_all)
                for j in range(32):
                    tqq, dcO = divmod(j, 8)
                    if j >= 2:
                        t.wait_ge(sOE, oe0 + j - 1)
                    for hp in range(NHP):
                        mm = t.matmul(oup[j % 2][:, :],
                                      wosv[:, (hp * 8 + dcO) * 128:(hp * 8 + dcO + 1) * 128],
                                      ctxn[:, hp * T + tqq * 512:hp * T + (tqq + 1) * 512],
                                      start=(hp == 0), stop=(hp == NHP - 1))
                    mm.then_inc(sOT, 1)
            C["sOT"] += 32

            @blk.scalar
            def _(s):
                for j in range(32):
                    tqq, dcO = divmod(j, 8)
                    s.wait_ge(sOT, ot0 + j + 1)
                    if j >= 2:
                        s.wait_ge(sRV, rv0 + j - 1)
                    s.activation(obuf[j % 2], oup[j % 2][:, :], AF.Identity,
                                 bias=bos[:, dcO:dcO + 1]).then_inc(sOE, 1)
            C["sOE"] += 32

            @blk.vector
            def _(v):
                for j in range(32):
                    v.wait_ge(sOE, oe0 + j + 1)
                    v.wait_ge(sXp[j % 2], x0[j % 2] + 16 * (j // 2 + 1))
                    if j >= 2:
                        v.wait_ge(sXWp[j % 2], xw0[j % 2] + 16 * (j // 2))
                    v.tensor_add(o2buf[j % 2], obuf[j % 2],
                                 xrst[j % 2]).then_inc(sRV, 1)
            C["sRV"] += 32

            @blk.sync
            def _(sy):
                for j in range(32):
                    tqq, dcO = divmod(j, 8)
                    sy.wait_ge(sRV, rv0 + j + 1)
                    sy.dma_start(x1d[b, dcO, :, tqq * 512:(tqq + 1) * 512],
                                 o2buf[j % 2]).then_inc(sXWp[j % 2], 16)
            C["sXW0"] += 16 * 16
            C["sXW1"] += 16 * 16

    def ffn_phase(b):
        with (
            nc.Block() as blk,
            nc.psum_tensor(f"fps0_{b}", [128, 512], F32) as fps0,
            nc.psum_tensor(f"fps1_{b}", [128, 512], F32) as fps1
        ):
            fps = [fps0, fps1]
            ot0 = C["sOT"]
            oe0 = C["sOE"]
            rv0 = C["sRV"]
            x0 = [C["sX0"], C["sX1"]]
            o0 = [C["sO0"], C["sO1"]]
            wd0 = C["sWD"]
            ap_f = C["sAP"]

            @blk.gpsimd
            def _(g):
                g.dma_start(w8sv, w8d[:, :]).then_inc(sWD, 16)
                for j in range(32):
                    tqq, dcO = divmod(j, 8)
                    if j >= 2:
                        g.wait_ge(sRV, rv0 + j - 1)
                    g.dma_start(xrst[j % 2],
                                x1d[b, dcO, :, tqq * 512:(tqq + 1) * 512]).then_inc(sXp[j % 2], 16)
            C["sWD"] += 16
            C["sX0"] += 16 * 16
            C["sX1"] += 16 * 16

            @blk.tensor
            def _(t):
                t.wait_ge(sWD, wd0 + 16)
                t.wait_ge(sAP, ap_f)
                for j in range(32):
                    tqq, dcO = divmod(j, 8)
                    if j >= 2:
                        t.wait_ge(sOE, oe0 + j - 1)
                    for dc in range(NDC):
                        mm = t.matmul(fps[j % 2][:, :],
                                      w8sv[:, dc * 1024 + dcO * 128:dc * 1024 + (dcO + 1) * 128],
                                      z8v[:, dc * T + tqq * 512:dc * T + (tqq + 1) * 512],
                                      start=(dc == 0), stop=(dc == NDC - 1))
                    mm.then_inc(sOT, 1)
            C["sOT"] += 32

            @blk.scalar
            def _(s):
                for j in range(32):
                    tqq, dcO = divmod(j, 8)
                    s.wait_ge(sOT, ot0 + j + 1)
                    if j >= 2:
                        s.wait_ge(sRV, rv0 + j - 1)
                    s.activation(obuf[j % 2], fps[j % 2][:, :], AF.Identity,
                                 bias=bfs[:, dcO:dcO + 1], scale=isc[:, 1:2]).then_inc(sOE, 1)
            C["sOE"] += 32

            @blk.vector
            def _(v):
                for j in range(32):
                    v.wait_ge(sOE, oe0 + j + 1)
                    v.wait_ge(sXp[j % 2], x0[j % 2] + 16 * (j // 2 + 1))
                    if j >= 2:
                        v.wait_ge(sOp[j % 2], o0[j % 2] + 16 * (j // 2))
                    v.tensor_add(o2buf[j % 2], obuf[j % 2],
                                 xrst[j % 2]).then_inc(sRV, 1)
            C["sRV"] += 32

            @blk.sync
            def _(sy):
                for j in range(32):
                    tqq, dcO = divmod(j, 8)
                    sy.wait_ge(sRV, rv0 + j + 1)
                    sy.dma_start(outd[b, dcO, :, tqq * 512:(tqq + 1) * 512],
                                 o2buf[j % 2]).then_inc(sOp[j % 2], 16)
            C["sO0"] += 16 * 16
            C["sO1"] += 16 * 16

    import os
    nph = int(os.environ.get("ATT_NPH", "999"))
    done = [0]

    def _run(f, *a):
        if done[0] < nph:
            f(*a)
            done[0] += 1

    for b in range(B_C):
        if b == 0:
            _run(ln_phase, 0, 0)
        _run(vg_phase, b)
        _run(attn_phase, b)
        if b + 1 < B_C:
            _run(ln0_pre, b + 1)
        _run(out_phase, b)
        if b + 1 < B_C:
            _run(ln0_post, b + 1)
        _run(ln_phase, b, 1)
        _run(ffn_phase, b)

    if nph >= 6 * B_C + 3:
        with nc.Block() as blk:
            @blk.sync
            def _(sy):
                sy.wait_ge(sXWp[0], 16 * 16 * B_C)
                sy.wait_ge(sXWp[1], 16 * 16 * B_C)
                sy.wait_ge(sOp[0], 16 * 16 * B_C)
                sy.wait_ge(sOp[1], 16 * 16 * B_C)

    return nc


_CACHE = {}


def _consts():
    if "oht" not in _CACHE:
        bt = _bucket_table()
        oh = np.zeros((320, 4096), dtype=BF)
        oh[bt, np.arange(4095)] = 1.0
        _CACHE["oht"] = oh
    oh = _CACHE["oht"]
    i128 = np.ascontiguousarray(np.eye(128, dtype=BF))
    selR = np.zeros((2, 128), dtype=BF)
    selR[1, 0:64] = 1.0
    selR[0, 64:128] = 1.0
    sel232 = np.zeros((2, 32), dtype=BF)
    sel232[0, 0:16] = 1.0
    sel232[1, 16:32] = 1.0
    p82 = np.zeros((8, 2), dtype=BF)
    p82[0:4, 0] = 1.0
    p82[4:8, 1] = 1.0
    return oh, i128, selR, sel232, p82


def _pack_w(w):
    return np.ascontiguousarray(
        np.asarray(w, np.float32).T.reshape(8, 128, 1024).transpose(1, 0, 2).reshape(128, 8192))


def _pack_wo(w):
    return np.ascontiguousarray(
        np.asarray(w, np.float32).T.reshape(8, 128, 8, 128).transpose(1, 0, 2, 3).reshape(128, 8192))


def kernel(L0_attn_hidden_states, L0_attn_q_w, L0_attn_q_b, L0_attn_k_w, L0_attn_k_b,
           L0_attn_v_w, L0_attn_v_b, L0_attn_out_w, L0_attn_out_b, L0_attn_rel_embed,
           L0_attn_gru_const, L0_attn_gru_w, L0_attn_gru_b, L0_ffn_weight_fp8,
           L0_ffn_bias, L0_ffn_input_scale, L0_ffn_weight_scale):
    x = np.asarray(L0_attn_hidden_states, dtype=np.float32)
    oh, i128, selR, sel232, p82 = _consts()

    shared = {
        "wq": _pack_w(L0_attn_q_w),
        "wk": _pack_w(L0_attn_k_w),
        "wv": _pack_w(L0_attn_v_w),
        "wo": _pack_wo(L0_attn_out_w),
        "w8": _pack_w(L0_ffn_weight_fp8),
        "bq": np.ascontiguousarray(np.asarray(L0_attn_q_b, np.float32).reshape(8, 128).T),
        "bk": np.ascontiguousarray(np.asarray(L0_attn_k_b, np.float32).reshape(8, 128).T),
        "bvr": np.ascontiguousarray(np.asarray(L0_attn_v_b, np.float32)[None, :]),
        "bo": np.ascontiguousarray(np.asarray(L0_attn_out_b, np.float32).reshape(8, 128).T),
        "bfn": np.ascontiguousarray(np.asarray(L0_ffn_bias, np.float32).reshape(8, 128).T),
        "rel": np.ascontiguousarray(np.asarray(L0_attn_rel_embed, np.float32)),
        "oht": oh,
        "gw": np.ascontiguousarray(np.asarray(L0_attn_gru_w, np.float32)),
        "gbi": np.ascontiguousarray(np.asarray(L0_attn_gru_b, np.float32).reshape(8, 1)),
        "gci": np.ascontiguousarray(np.asarray(L0_attn_gru_const, np.float32).reshape(16, 1)),
        "scli": np.array([[float(np.asarray(L0_ffn_input_scale).ravel()[0]),
                           float(np.asarray(L0_ffn_weight_scale).ravel()[0])]], np.float32),
        "i128": i128, "selR": selR, "sel232": sel232, "p82": p82,
    }
    xt = np.ascontiguousarray(x.reshape(32, T, 8, 128).transpose(0, 2, 3, 1))

    nc = build_nc()
    in_maps = []
    for c in range(8):
        m = dict(shared)
        m["x"] = np.ascontiguousarray(xt[4 * c:4 * c + 4])
        in_maps.append(m)
    res = run_bass_kernel_spmd(nc, in_maps, core_ids=list(range(8)))
    outs = []
    for c in range(8):
        o = np.asarray(res.results[c]["out"])
        outs.append(o.reshape(4, 1024, T).transpose(0, 2, 1))
    return np.ascontiguousarray(np.concatenate(outs, axis=0), dtype=np.float32)


# version 13
# speedup vs baseline: 1.5541x; 1.0077x over previous
import contextlib

import numpy as np
import ml_dtypes

import concourse.bass as bass
import concourse.mybir as mybir
from concourse.bass_utils import run_bass_kernel_spmd

F32 = mybir.dt.float32
BF16 = mybir.dt.bfloat16
FP8 = mybir.dt.float8e4
U8 = mybir.dt.uint8
AF = mybir.ActivationFunctionType
OP = mybir.AluOpType
BF = ml_dtypes.bfloat16

B_C = 4
T = 2048
NHP = 8
NTQ = 4
TQ = 512
NS0 = 16
NDC = 8
EXPB = -4.0
NB_IL = NHP * NTQ * NS0
GDPB = 16 + 16 * 2 * NHP

NUM_BUCKETS = 320
MAX_DISTANCE = 800


def _bucket_table():
    import math
    d = np.arange(-(T - 1), T)
    nb = NUM_BUCKETS // 2
    rel_b = (d > 0).astype(np.int64) * nb
    rp = np.abs(d)
    me = nb // 2
    small = rp < me
    rl = np.log(np.maximum(rp, 1).astype(np.float32) / me)
    rl = rl / math.log(MAX_DISTANCE / me)
    rl = rl * (nb - me)
    rl = (me + rl).astype(np.int64)
    rl = np.minimum(rl, nb - 1)
    return rel_b + np.where(small, rp, rl)


def build_nc():
    nc = bass.Bass("TRN2", target_bir_lowering=False)

    xd = nc.dram_tensor("x", [B_C, NDC, 128, T], F32, kind="ExternalInput")
    wqd = nc.dram_tensor("wq", [128, 8192], F32, kind="ExternalInput")
    wkd = nc.dram_tensor("wk", [128, 8192], F32, kind="ExternalInput")
    wvd = nc.dram_tensor("wv", [128, 8192], F32, kind="ExternalInput")
    wod = nc.dram_tensor("wo", [128, 8192], F32, kind="ExternalInput")
    w8d = nc.dram_tensor("w8", [128, 8192], F32, kind="ExternalInput")
    bqd = nc.dram_tensor("bq", [128, 8], F32, kind="ExternalInput")
    bkd = nc.dram_tensor("bk", [128, 8], F32, kind="ExternalInput")
    bvrd = nc.dram_tensor("bvr", [1, 1024], F32, kind="ExternalInput")
    bod = nc.dram_tensor("bo", [128, 8], F32, kind="ExternalInput")
    bfd = nc.dram_tensor("bfn", [128, 8], F32, kind="ExternalInput")
    reld = nc.dram_tensor("rel", [320, 16], F32, kind="ExternalInput")
    ohtd = nc.dram_tensor("oht", [320, 4096], BF16, kind="ExternalInput")
    gwd_d = nc.dram_tensor("gw", [8, 64], F32, kind="ExternalInput")
    gbid = nc.dram_tensor("gbi", [8, 1], F32, kind="ExternalInput")
    gcid = nc.dram_tensor("gci", [16, 1], F32, kind="ExternalInput")
    sclid = nc.dram_tensor("scli", [1, 2], F32, kind="ExternalInput")
    i128d = nc.dram_tensor("i128", [128, 128], BF16, kind="ExternalInput")
    selRd = nc.dram_tensor("selR", [2, 128], BF16, kind="ExternalInput")
    sel232d = nc.dram_tensor("sel232", [2, 32], BF16, kind="ExternalInput")
    p82d = nc.dram_tensor("p82", [8, 2], BF16, kind="ExternalInput")

    outd = nc.dram_tensor("out", [B_C, NDC, 128, T], F32, kind="ExternalOutput")

    rbd = nc.dram_tensor("rbd", [16, 4096], BF16, kind="Internal")
    x1d = nc.dram_tensor("x1d", [B_C, NDC, 128, T], F32, kind="Internal")
    statscr = nc.dram_tensor("statscr", [4096], F32, kind="Internal")
    denscr = nc.dram_tensor("denscr", [4096], F32, kind="Internal")
    gatescr = nc.dram_tensor("gatescr", [16, 2048], BF16, kind="Internal")

    es = contextlib.ExitStack()
    nc._kernel_es = es

    def sb(name, shape, dt_):
        return es.enter_context(nc.sbuf_tensor(name, shape, dt_))

    def sem(name):
        return es.enter_context(nc.semaphore(name))

    arA = sb("arA", [128, 49408], U8)
    arB = sb("arB", [128, 32768], U8)
    arC = sb("arC", [128, 32768], U8)

    vT = arA[:, :].bitcast(BF16)
    vTr = vT.rearrange("p (s h c) -> p s h c", s=NS0, h=NHP, c=193)
    wosv = arA[:, 0:16384].bitcast(BF16)
    w8sv = arA[:, 16384:24576].bitcast(FP8)
    z8v = arA[:, 24576:40960].bitcast(FP8)
    xbf = arB[:, :].bitcast(BF16)
    ctxn = arC[:, :].bitcast(BF16)
    mbc = arC[:, 0:8192].bitcast(F32)
    ibc = arC[:, 8192:16384].bitcast(F32)
    gsigA = arC[:, 16384:24576].bitcast(F32)
    gsigB = arC[:, 24576:32768].bitcast(F32)
    wvsv = arC[:, 16384:32768].bitcast(BF16)

    xstage_t = [sb(f"xst{i}", [128, 2432], F32) for i in range(2)]
    xstage = [x[:, 0:2048] for x in xstage_t]
    ctxu = [sb(f"ctxu{i}", [128, T], BF16) for i in range(2)]
    kqp = sb("kqp", [128, 2 * T], BF16)
    sqst = kqp[:, 0:2048]
    sq2 = kqp[:, 2048:4096]
    gbct = [sb(f"gbc{i}", [128, 2 * T], BF16) for i in range(2)]
    tstt = [g[:, :].bitcast(F32) for g in gbct]
    wqs = [sb(f"wqs{i}", [128, 1024], BF16) for i in range(2)]
    wks = [sb(f"wks{i}", [128, 1024], BF16) for i in range(2)]
    obuf = [w[:, :].bitcast(F32) for w in wqs]
    xrst = [w[:, :].bitcast(F32) for w in wks]
    pbT = [x[:, :].bitcast(BF16) for x in xstage_t]
    Gb = [sb(f"Gb{i}", [128, 1024], BF16) for i in range(2)]
    exb = [sb(f"exb{i}", [128, 1024], BF16) for i in range(3)]
    o2buf = [g[:, :].bitcast(F32) for g in Gb]
    dstg = sb("dstg", [128, 2048], F32)
    sstv = dstg[0:1, 0:2048]
    d128 = sb("d128", [128, 32], F32)
    d128r = sb("d128r", [128, 32], F32)
    bcps = sb("bcps", [128, 2048], BF16)
    gate16 = sb("gate16", [16, 2048], BF16)
    dstat = sb("dstat", [128, 48], F32)

    bq8 = sb("bq8", [128, 8], F32)
    bqraw = sb("bqraw", [128, 8], F32)
    bks = sb("bks", [128, 8], F32)
    bos = sb("bos", [128, 8], F32)
    bfs = sb("bfs", [128, 8], F32)
    vbr = sb("vbr", [1, 1024], BF16)
    gwdb = sb("gwdb", [8, 128], BF16)
    gbr = sb("gbr", [8, 1], BF16)
    gcs = sb("gcs", [16, 1], F32)
    scl = sb("scl", [1, 2], F32)
    invs_sb = sb("invs", [1, 1], F32)
    sws_sb = sb("sws", [1, 1], F32)
    isc = sb("isc", [128, 2], F32)
    i128s = sb("i128s", [128, 128], BF16)
    selRs = sb("selRs", [2, 128], BF16)
    sel232s = sb("sel232s", [2, 32], BF16)
    p82s = sb("p82s", [8, 2], BF16)
    Tg = sb("Tg", [128, 256], BF16)
    gb2s = sb("gb2s", [2, 1], BF16)
    gb32f = sb("gb32f", [16, 2], F32)
    smalls = sb("smalls", [128, 4], F32)
    onesb = smalls[:, 0:1].bitcast(BF16)[:, 0:1]
    ones1b = sb("ones1b", [1, 128], BF16)
    ones1f = sb("ones1f", [1, 128], F32)
    ebias = smalls[:, 1:2]
    eighth = smalls[:, 2:3]
    dumm = smalls[0:1, 3:4]

    sWD = sem("sWD")
    sGS = sem("sGS")
    sGDp = [sem("sGD0"), sem("sGD1")]
    sXp = [sem("sX0"), sem("sX1")]
    sPp = [sem(f"sP{i}") for i in range(2)]
    sWqp = [sem("sWq0"), sem("sWq1")]
    sWkp = [sem("sWk0"), sem("sWk1")]
    sXWp = [sem("sXW0"), sem("sXW1")]
    sOp = [sem("sO0"), sem("sO1")]
    sSQ = sem("sSQ")
    sCB = sem("sCB")
    sST = sem("sST")
    sSC = sem("sSC")
    sBD = sem("sBD")
    sAP = sem("sAP")
    sQK = sem("sQK")
    sEV = sem("sEV")
    sG = sem("sG")
    sS = sem("sS")
    sE = sem("sE")
    sA = sem("sA")
    sV = sem("sV")
    sDD = sem("sDD")
    sR = sem("sR")
    sB = sem("sB")
    sN = sem("sN")
    sOT = sem("sOT")
    sOE = sem("sOE")
    sRV = sem("sRV")
    sTI = sem("sTI")
    sVI = sem("sVI")
    sGE = sem("sGE")
    sVC = sem("sVC")
    sVS = sem("sVS")
    sXs = [sem("sXs0"), sem("sXs1")]
    sBs = sem("sBs")

    C = {k: 0 for k in ("sWD sGS sGD0 sGD1 sX0 sX1 sP0 sP1 sWq0 sWq1 sWk0 sWk1 "
                        "sXW0 sXW1 sO0 sO1 sSQ sCB sST sSC sBD sAP sQK sEV sDD "
                        "sOT sOE sRV sGE sVC sXs0 sXs1 sBs sWv").split()}

    def ig(b, il):
        return b * NB_IL + il

    def jslot(b, hp, tq):
        return (b * NHP + hp) * NTQ + tq

    ohts = arA[:, 0:24576].bitcast(BF16)
    rbsb = arA[:, 24576:40960].bitcast(BF16)[0:16, 0:4096]
    rels = arC[:, 0:96].bitcast(BF16)
    with nc.Block() as blk:
        @blk.gpsimd
        def _(g):
            for dst, src in (
                (bqraw[:, :], bqd[:, :]), (bks[:, :], bkd[:, :]),
                (bos[:, :], bod[:, :]), (bfs[:, :], bfd[:, :]),
                (vbr[:, :], bvrd[:, :]),
                (rels[0:128, 0:16], reld[0:128, :]),
                (rels[0:128, 16:32], reld[128:256, :]),
                (rels[0:64, 32:48], reld[256:320, :]),
                (ohts[0:128, 0:4096], ohtd[0:128, :]),
                (ohts[0:128, 4096:8192], ohtd[128:256, :]),
                (ohts[0:64, 8192:12288], ohtd[256:320, :]),
                (gwdb[:, 0:64], gwd_d[:, :]), (gwdb[:, 64:128], gwd_d[:, :]),
                (gbr[:, :], gbid[:, :]), (gcs[:, :], gcid[:, :]),
                (scl[:, :], sclid[:, :]), (i128s[:, :], i128d[:, :]),
                (selRs[:, :], selRd[:, :]), (sel232s[:, :], sel232d[:, :]),
                (p82s[:, :], p82d[:, :]),
            ):
                g.dma_start(dst, src).then_inc(sWD, 16)
            g.memset(onesb[:, :], 1.0)
            g.memset(ones1b[:, :], 1.0)
            g.memset(ones1f[:, :], 1.0)
            g.memset(ebias[:, :], EXPB)
            g.memset(eighth[:, :], 0.125)
            g.memset(dumm[:, :], 0.0)
            g.memset(dstg[:, :], 0.0)
            g.memset(Tg[:, :], 0.0).then_inc(sTI, 1)
            g.wait_ge(sVI, 4)
            g.dma_start(rbd[:, :], rbsb[:, :]).then_inc(sWD, 16)

        with (
            nc.psum_tensor("rbps", [16, 1024], F32) as rbps,
            nc.psum_tensor("gw2ps", [128, 2], F32) as gw2ps,
            nc.psum_tensor("gb2ps", [2, 1], F32) as gb2ps,
            nc.psum_tensor("gb32ps", [16, 2], F32) as gb32ps,
            nc.psum_tensor("iscps", [128, 2], F32) as iscps,
        ):
            @blk.tensor
            def _(t):
                t.wait_ge(sWD, 16 * 20)
                t.wait_ge(sTI, 1)
                cnts = [128, 128, 64]
                for n4 in range(4):
                    if n4 >= 1:
                        t.wait_ge(sVI, n4)
                    for c in range(3):
                        for half in range(2):
                            mm = t.matmul(rbps[:, half * 512:(half + 1) * 512],
                                          rels[0:cnts[c], 16 * c:16 * (c + 1)],
                                          ohts[0:cnts[c],
                                               c * 4096 + n4 * 1024 + half * 512:
                                               c * 4096 + n4 * 1024 + (half + 1) * 512],
                                          start=(c == 0), stop=(c == 2))
                    mm.then_inc(sTI, 1)
                t.matmul(gw2ps[:, :], gwdb[:, :], p82s[:, :],
                         start=True, stop=True).then_inc(sTI, 1)
                t.matmul(gb2ps[:, :], p82s[:, :], gbr[:, :],
                         start=True, stop=True).then_inc(sTI, 1)
                t.wait_ge(sVI, 6)
                t.matmul(gb32ps[:, 0:1], sel232s[:, 0:16], gb2s[:, :],
                         start=True, stop=True)
                t.matmul(gb32ps[:, 1:2], sel232s[:, 16:32], gb2s[:, :],
                         start=True, stop=True).then_inc(sTI, 1)
                t.wait_ge(sVI, 7)
                t.matmul(iscps[:, 0:1], ones1f[:, :], invs_sb[:, :], start=True, stop=True)
                t.matmul(iscps[:, 1:2], ones1f[:, :], sws_sb[:, :],
                         start=True, stop=True).then_inc(sTI, 1)

            @blk.vector
            def _(v):
                for n4 in range(4):
                    v.wait_ge(sTI, 2 + n4)
                    v.tensor_copy(rbsb[:, n4 * 1024:(n4 + 1) * 1024],
                                  rbps[:, :]).then_inc(sVI, 1)
                v.wait_ge(sTI, 6)
                for hp in range(NHP):
                    a0 = hp * 16
                    b0 = 128 + hp * 16
                    v.tensor_copy(Tg[0:64, a0 + 2 * hp:a0 + 2 * hp + 1], gw2ps[0:64, 0:1])
                    v.tensor_copy(Tg[64:128, a0 + 2 * hp + 1:a0 + 2 * hp + 2], gw2ps[64:128, 0:1])
                    v.tensor_copy(Tg[0:64, b0 + 2 * hp:b0 + 2 * hp + 1], gw2ps[0:64, 1:2])
                    cp = v.tensor_copy(Tg[64:128, b0 + 2 * hp + 1:b0 + 2 * hp + 2], gw2ps[64:128, 1:2])
                cp.then_inc(sVI, 1)
                v.wait_ge(sTI, 7)
                v.tensor_copy(gb2s[:, :], gb2ps[:, :]).then_inc(sVI, 1)
                v.reciprocal(invs_sb[:, :], scl[0:1, 0:1])
                v.tensor_mul(sws_sb[:, :], scl[0:1, 0:1], scl[0:1, 1:2]).then_inc(sVI, 1)
                v.wait_ge(sTI, 8)
                v.tensor_copy(gb32f[:, :], gb32ps[:, :])
                v.wait_ge(sTI, 9)
                v.tensor_copy(isc[:, :], iscps[:, :])
                v.tensor_scalar(bq8[:, :], bqraw[:, :], 0.125, None,
                                OP.mult).then_inc(sVI, 1)

    C["sWD"] = 16 * 21
    RBD_READY = C["sWD"]

    def ln_phase(b, which):
        src = xd if which == 0 else x1d
        with (
            nc.Block() as blk,
            nc.psum_tensor(f"sxp{b}_{which}", [1, 2048], F32) as sxp,
            nc.psum_tensor(f"sx2p{b}_{which}", [1, 2048], F32) as sx2p
        ):
            x0 = [C["sX0"], C["sX1"]]
            cb0 = C["sCB"]
            sq0 = C["sSQ"]
            st0 = C["sST"]
            sc0 = C["sSC"]
            bd0 = C["sBD"]
            bsF = C["sBs"]
            evL = C["sEV"]

            @blk.gpsimd
            def _(g):
                if which == 1:
                    g.wait_ge(sXWp[0], 16 * 16 * (b + 1))
                    g.wait_ge(sXWp[1], 16 * 16 * (b + 1))
                for dc in range(NDC):
                    if dc >= 2:
                        g.wait_ge(sCB, cb0 + dc - 1)
                        g.wait_ge(sSQ, sq0 + dc - 1)
                    g.dma_start(xstage[dc % 2], src[b, dc, :, :]).then_inc(sPq := sXp[dc % 2], 16)
            C["sX0"] += 16 * (NDC // 2)
            C["sX1"] += 16 * (NDC // 2)

            @blk.sync
            def _(sy):
                sy.wait_ge(sBD, bd0)
                sy.wait_ge(sBs, bsF)
                sy.wait_ge(sSC, sc0 + 1)
                sy.dma_start(bass.AP(statscr, 0, [[2048, 1], [1, 2048]]), sstv).then_inc(sBD, 16)
                sy.wait_ge(sSC, sc0 + 2)
                sy.dma_start(bass.AP(statscr, 2048, [[2048, 1], [1, 2048]]), sstv).then_inc(sBD, 16)
                sy.wait_ge(sBD, bd0 + 32)
                sy.dma_start(bass.AP(dstat, 0, [[48, 128], [1, 16]]),
                             bass.AP(statscr, 0, [[16, 128], [1, 16]])).then_inc(sBD, 16)
                sy.dma_start(bass.AP(dstat, 16, [[48, 128], [1, 16]]),
                             bass.AP(statscr, 2048, [[16, 128], [1, 16]])).then_inc(sBD, 16)
                sy.wait_ge(sSC, sc0 + 3)
                sy.dma_start(bass.AP(statscr, 0, [[16, 128], [1, 16]]),
                             dstat[:, 0:16]).then_inc(sBD, 16)
                sy.wait_ge(sSC, sc0 + 4)
                sy.dma_start(bass.AP(statscr, 2048, [[16, 128], [1, 16]]),
                             dstat[:, 32:48]).then_inc(sBD, 16)
                sy.wait_ge(sBD, bd0 + 96)
                sy.dma_start(mbc, bass.AP(statscr, 0, [[0, 128], [1, 2048]])).then_inc(sBD, 16)
                sy.dma_start(ibc, bass.AP(statscr, 2048, [[0, 128], [1, 2048]])).then_inc(sBD, 16)
            C["sBD"] += 16 * 8

            @blk.vector
            def _(v):
                for dc in range(NDC):
                    v.wait_ge(sXp[dc % 2], x0[dc % 2] + 16 * (dc // 2 + 1))
                    if which == 1 and dc >= 1:
                        v.wait_ge(sST, st0 + dc)
                    tgt = xbf[:, dc * T:(dc + 1) * T] if which == 0 else sqst
                    v.tensor_copy(tgt, xstage[dc % 2]).then_inc(sCB, 1)
                v.wait_ge(sST, st0 + NDC)
                v.wait_ge(sDD, 16 * 11 * NHP * (b + which))
                v.tensor_copy(sstv, sxp[0:1, :]).then_inc(sSC, 1)
                v.wait_ge(sBD, bd0 + 16)
                v.tensor_copy(sstv, sx2p[0:1, :]).then_inc(sSC, 1)
                v.wait_ge(sBD, bd0 + 64)
                v.tensor_scalar(dstat[:, 0:16], dstat[:, 0:16], 1.0 / 1024.0, None,
                                OP.mult).then_inc(sVC, 1)
                v.tensor_scalar(dstat[:, 16:32], dstat[:, 16:32], 1.0 / 1024.0, 1e-5,
                                OP.mult, OP.add).then_inc(sVC, 1)
                v.wait_ge(sVC, C["sVC"] + 2)
                v.tensor_mul(dstat[:, 32:48], dstat[:, 0:16],
                             dstat[:, 0:16]).then_inc(sVC, 1)
                v.wait_ge(sVC, C["sVC"] + 3)
                v.tensor_sub(dstat[:, 16:32], dstat[:, 16:32],
                             dstat[:, 32:48]).then_inc(sVC, 1)
                v.wait_ge(sVC, C["sVC"] + 4)
                v.reciprocal(dstat[:, 32:48], dstat[:, 16:32]).then_inc(sSC, 1)
                v.wait_ge(sSQ, sq0 + NDC + 1)
                if which == 1:
                    v.tensor_scalar(dstat[:, 32:48], dstat[:, 32:48], isc[:, 0:1],
                                    None, OP.mult).then_inc(sSC, 1)
                else:
                    v.tensor_copy(dumm[:, :], dumm[:, :]).then_inc(sSC, 1)
            C["sCB"] += NDC
            C["sSC"] += 4
            C["sVC"] += 4

            @blk.scalar
            def _(s):
                for dc in range(NDC):
                    s.wait_ge(sXp[dc % 2], x0[dc % 2] + 16 * (dc // 2 + 1))
                    if dc >= 1:
                        s.wait_ge(sST, st0 + dc)
                    tgt = sqst if which == 0 else sq2
                    s.activation(tgt, xstage[dc % 2], AF.Square).then_inc(sSQ, 1)
                s.wait_ge(sSC, sc0 + 3)
                s.activation(dstat[:, 32:48], dstat[:, 32:48], AF.Sqrt).then_inc(sSQ, 1)
            C["sSQ"] += NDC + 1

            @blk.tensor
            def _(t):
                t.wait_ge(sEV, evL)
                for dc in range(NDC):
                    t.wait_ge(sCB, cb0 + dc + 1)
                    t.wait_ge(sSQ, sq0 + dc + 1)
                    xsrc = xbf[:, dc * T:(dc + 1) * T] if which == 0 else sqst
                    qsrc = sqst if which == 0 else sq2
                    for sl in range(4):
                        t.matmul(sxp[0:1, sl * 512:(sl + 1) * 512], onesb[:, :],
                                 xsrc[:, sl * 512:(sl + 1) * 512],
                                 start=(dc == 0), stop=(dc == NDC - 1))
                    for sl in range(4):
                        mm = t.matmul(sx2p[0:1, sl * 512:(sl + 1) * 512], onesb[:, :],
                                      qsrc[:, sl * 512:(sl + 1) * 512],
                                      start=(dc == 0), stop=(dc == NDC - 1))
                    mm.then_inc(sST, 1)
            C["sST"] += NDC

        with nc.Block() as blk:
            x0 = [C["sX0"], C["sX1"]]
            ap0 = C["sAP"]
            bd_all = C["sBD"]

            apc_g = 2 if which == 0 else 3

            @blk.gpsimd
            def _(g):
                for dc in range(NDC):
                    if dc >= 2:
                        g.wait_ge(sAP, ap0 + apc_g * (dc - 2) + 1)
                    g.dma_start(xstage[dc % 2], src[b, dc, :, :]).then_inc(sXp[dc % 2], 16)
            C["sX0"] += 16 * (NDC // 2)
            C["sX1"] += 16 * (NDC // 2)

            @blk.vector
            def _(v):
                v.wait_ge(sBD, bd_all)
                for dc in range(NDC):
                    v.wait_ge(sXp[dc % 2], x0[dc % 2] + 16 * (dc // 2 + 1))
                    if dc >= 1:
                        v.wait_ge(sAP, ap0 + apc_g * (dc - 1) + 2)
                    v.tensor_sub(tstt[0], xstage[dc % 2], mbc).then_inc(sAP, 1)
                    v.wait_ge(sAP, ap0 + apc_g * dc + 1)
                    if which == 0:
                        v.tensor_mul(xbf[:, dc * T:(dc + 1) * T], tstt[0],
                                     ibc).then_inc(sAP, 1)
                    else:
                        v.tensor_mul(tstt[1], tstt[0], ibc).then_inc(sAP, 1)
                        v.wait_ge(sAP, ap0 + apc_g * dc + 2)
                        v.tensor_scalar(z8v[:, dc * T:(dc + 1) * T], tstt[1],
                                        448.0, -448.0, OP.min,
                                        OP.max).then_inc(sAP, 1)
            C["sAP"] += apc_g * NDC

    ln0_shadow = {}

    def ln0_pre(b):
        with (
            nc.Block() as blk,
            nc.psum_tensor(f"sxpS{b}", [1, 2048], F32) as sxp,
            nc.psum_tensor(f"sx2pS{b}", [1, 2048], F32) as sx2p
        ):
            xs0 = [C["sXs0"], C["sXs1"]]
            cb0 = C["sCB"]
            sq0 = C["sSQ"]
            st0 = C["sST"]
            sc0 = C["sSC"]
            bd0 = C["sBD"]
            vc0 = C["sVC"]

            bs0 = C["sBs"]
            ln0_shadow[b] = bs0

            @blk.sync
            def _(sy):
                sy.wait_ge(sG, NB_IL * b)
                sy.wait_ge(sBD, bd0)
                sy.wait_ge(sBs, bs0)
                for dc in range(0, NDC, 2):
                    if dc >= 2:
                        sy.wait_ge(sCB, cb0 + dc - 1)
                        sy.wait_ge(sSQ, sq0 + dc - 1)
                    sy.dma_start(xstage[0], xd[b, dc, :, :]).then_inc(sXs[0], 16)
                sy.wait_ge(sSC, sc0 + 1)
                sy.dma_start(bass.AP(statscr, 0, [[2048, 1], [1, 2048]]), sstv).then_inc(sBs, 16)
                sy.wait_ge(sSC, sc0 + 2)
                sy.dma_start(bass.AP(statscr, 2048, [[2048, 1], [1, 2048]]), sstv).then_inc(sBs, 16)
                sy.wait_ge(sBs, bs0 + 32)
                sy.dma_start(bass.AP(dstat, 0, [[48, 128], [1, 16]]),
                             bass.AP(statscr, 0, [[16, 128], [1, 16]])).then_inc(sBs, 16)
                sy.dma_start(bass.AP(dstat, 16, [[48, 128], [1, 16]]),
                             bass.AP(statscr, 2048, [[16, 128], [1, 16]])).then_inc(sBs, 16)
                sy.wait_ge(sSC, sc0 + 3)
                sy.dma_start(bass.AP(statscr, 0, [[16, 128], [1, 16]]),
                             dstat[:, 0:16]).then_inc(sBs, 16)
                sy.wait_ge(sSC, sc0 + 4)
                sy.dma_start(bass.AP(statscr, 2048, [[16, 128], [1, 16]]),
                             dstat[:, 32:48]).then_inc(sBs, 16)
            C["sXs0"] += 16 * (NDC // 2)
            C["sBs"] += 16 * 6

            @blk.scalar
            def _(s):
                s.dma_start(xstage[1], xd[b, 1, :, :]).then_inc(sXs[1], 16)
                for dc in range(NDC):
                    s.wait_ge(sXs[dc % 2], xs0[dc % 2] + 16 * (dc // 2 + 1))
                    if dc >= 1:
                        s.wait_ge(sST, st0 + dc)
                    s.activation(sqst, xstage[dc % 2], AF.Square).then_inc(sSQ, 1)
                    if dc % 2 == 1 and dc + 2 < NDC:
                        s.wait_ge(sCB, cb0 + dc + 1)
                        s.wait_ge(sSQ, sq0 + dc + 1)
                        s.dma_start(xstage[1], xd[b, dc + 2, :, :]).then_inc(sXs[1], 16)
                s.wait_ge(sSC, sc0 + 3)
                s.activation(dstat[:, 32:48], dstat[:, 32:48], AF.Sqrt).then_inc(sSQ, 1)
            C["sSQ"] += NDC + 1
            C["sXs1"] += 16 * (NDC // 2)

            @blk.vector
            def _(v):
                for dc in range(NDC):
                    v.wait_ge(sXs[dc % 2], xs0[dc % 2] + 16 * (dc // 2 + 1))
                    v.tensor_copy(xbf[:, dc * T:(dc + 1) * T],
                                  xstage[dc % 2]).then_inc(sCB, 1)
                v.wait_ge(sST, st0 + NDC)
                v.wait_ge(sDD, 16 * 11 * NHP * b)
                v.tensor_copy(sstv, sxp[0:1, :]).then_inc(sSC, 1)
                v.wait_ge(sBs, bs0 + 16)
                v.tensor_copy(sstv, sx2p[0:1, :]).then_inc(sSC, 1)
                v.wait_ge(sBs, bs0 + 64)
                v.tensor_scalar(dstat[:, 0:16], dstat[:, 0:16], 1.0 / 1024.0, None,
                                OP.mult).then_inc(sVC, 1)
                v.tensor_scalar(dstat[:, 16:32], dstat[:, 16:32], 1.0 / 1024.0, 1e-5,
                                OP.mult, OP.add).then_inc(sVC, 1)
                v.wait_ge(sVC, vc0 + 2)
                v.tensor_mul(dstat[:, 32:48], dstat[:, 0:16],
                             dstat[:, 0:16]).then_inc(sVC, 1)
                v.wait_ge(sVC, vc0 + 3)
                v.tensor_sub(dstat[:, 16:32], dstat[:, 16:32],
                             dstat[:, 32:48]).then_inc(sVC, 1)
                v.wait_ge(sVC, vc0 + 4)
                v.reciprocal(dstat[:, 32:48], dstat[:, 16:32]).then_inc(sSC, 1)
                v.wait_ge(sSQ, sq0 + NDC + 1)
                v.tensor_copy(dumm[:, :], dumm[:, :]).then_inc(sSC, 1)
            C["sCB"] += NDC
            C["sSC"] += 4
            C["sVC"] += 4

            @blk.tensor
            def _(t):
                for dc in range(NDC):
                    t.wait_ge(sCB, cb0 + dc + 1)
                    t.wait_ge(sSQ, sq0 + dc + 1)
                    for sl in range(4):
                        t.matmul(sxp[0:1, sl * 512:(sl + 1) * 512], onesb[:, :],
                                 xbf[:, dc * T + sl * 512:dc * T + (sl + 1) * 512],
                                 start=(dc == 0), stop=(dc == NDC - 1))
                    for sl in range(4):
                        mm = t.matmul(sx2p[0:1, sl * 512:(sl + 1) * 512], onesb[:, :],
                                      sqst[:, sl * 512:(sl + 1) * 512],
                                      start=(dc == 0), stop=(dc == NDC - 1))
                    mm.then_inc(sST, 1)
            C["sST"] += NDC

    def ln0_post(b):
        with nc.Block() as blk:
            bs0 = ln0_shadow.pop(b)
            xs0 = [C["sXs0"], C["sXs1"]]
            ap0 = C["sAP"]
            ot0 = C["sOT"]

            @blk.sync
            def _(sy):
                sy.wait_ge(sOT, ot0)
                sy.wait_ge(sBs, bs0 + 96)
                sy.dma_start(mbc, bass.AP(statscr, 0, [[0, 128], [1, 2048]])).then_inc(sBs, 16)
                sy.dma_start(ibc, bass.AP(statscr, 2048, [[0, 128], [1, 2048]])).then_inc(sBs, 16)
                for dc in range(NDC):
                    if dc >= 2:
                        sy.wait_ge(sAP, ap0 + 2 * (dc - 2) + 2)
                    sy.dma_start(xstage[dc % 2], xd[b, dc, :, :]).then_inc(sXs[dc % 2], 16)
            C["sBs"] += 16 * 2
            C["sXs0"] += 16 * (NDC // 2)
            C["sXs1"] += 16 * (NDC // 2)

            @blk.vector
            def _(v):
                v.wait_ge(sBs, bs0 + 128)
                for dc in range(NDC):
                    v.wait_ge(sXs[dc % 2], xs0[dc % 2] + 16 * (dc // 2 + 1))
                    if dc >= 1:
                        v.wait_ge(sAP, ap0 + 2 * (dc - 1) + 2)
                    v.tensor_sub(tstt[0], xstage[dc % 2], mbc).then_inc(sAP, 1)
                    v.wait_ge(sAP, ap0 + 2 * dc + 1)
                    v.tensor_mul(xbf[:, dc * T:(dc + 1) * T], tstt[0],
                                 ibc).then_inc(sAP, 1)
            C["sAP"] += 2 * NDC

    KS1 = [(tb, th) for tb in (0, 1, 2, 3, 4, 14, 15) for th in range(2)]
    KS2 = [(tb, th) for tb in range(5, 14) for th in range(2)]
    sWv = sem("sWv")

    def vga_phase(b):
        with (
            nc.Block() as blk,
            nc.psum_tensor(f"gapsA_{b}", [16, 1024], F32) as gapsA,
            nc.psum_tensor(f"gapsB_{b}", [16, 1024], F32) as gapsB
        ):
            qk0 = C["sQK"]
            ev0 = C["sEV"]
            ge0 = C["sGE"]
            ap_vg = C["sAP"]
            oe0 = C["sOE"]
            vc0 = C["sVC"]

            @blk.tensor
            def _(t):
                t.wait_ge(sWD, RBD_READY)
                t.wait_ge(sAP, ap_vg)
                t.wait_ge(sOE, oe0)
                for gh in range(2):
                    if gh == 1:
                        t.wait_ge(sEV, ev0 + 1)
                    for sl in range(2):
                        for hp in range(NHP):
                            xsl = xbf[:, hp * T + gh * 1024 + sl * 512:
                                      hp * T + gh * 1024 + (sl + 1) * 512]
                            t.matmul(gapsA[:, sl * 512:(sl + 1) * 512],
                                     Tg[:, hp * 16:(hp + 1) * 16], xsl,
                                     start=(hp == 0), stop=(hp == NHP - 1))
                            mm = t.matmul(gapsB[:, sl * 512:(sl + 1) * 512],
                                          Tg[:, 128 + hp * 16:128 + (hp + 1) * 16], xsl,
                                          start=(hp == 0), stop=(hp == NHP - 1))
                    mm.then_inc(sQK, 1)
            C["sQK"] += 2

            @blk.scalar
            def _(s):
                for gh in range(2):
                    s.wait_ge(sQK, qk0 + gh + 1)
                    s.activation(gsigA[0:16, gh * 1024:(gh + 1) * 1024], gapsA[:, :],
                                 AF.Sigmoid, bias=gb32f[:, 0:1])
                    s.activation(gsigB[0:16, gh * 1024:(gh + 1) * 1024], gapsB[:, :],
                                 AF.Sigmoid, bias=gb32f[:, 1:2]).then_inc(sEV, 1)
                s.wait_ge(sGE, ge0 + 1)
                s.dma_start(gatescr[:, :], gate16[:, :]).then_inc(sGS, 16)
            C["sEV"] += 2

            @blk.vector
            def _(v):
                v.wait_ge(sEV, ev0 + 2)
                v.tensor_scalar(tstt[0][0:16, 0:T], gsigB[0:16, 0:T], gcs[:, :], -1.0,
                                OP.mult, OP.add).then_inc(sVC, 1)
                v.wait_ge(sVC, vc0 + 1)
                v.tensor_mul(tstt[1][0:16, 0:T], tstt[0][0:16, 0:T],
                             gsigA[0:16, 0:T]).then_inc(sVC, 1)
                v.wait_ge(sVC, vc0 + 2)
                v.tensor_scalar(gate16[:, :], tstt[1][0:16, 0:T], 2.0, None,
                                OP.add).then_inc(sGE, 1)
            C["sGE"] += 1
            C["sVC"] += 2
            C["sGS"] += 16

    def _vproj(blk, b, ks, vps, qk0, ev0, ge_need, wv_need):
        @blk.tensor
        def _(t):
            t.wait_ge(sWv, wv_need)
            for ki, (tb, th) in enumerate(ks):
                if ki >= 2:
                    t.wait_ge(sEV, ev0 + 2 * ki - 2)
                for dc in range(NDC):
                    t.matmul(vps[ki % 2][:, :],
                             xbf[:, dc * T + tb * 128:dc * T + (tb + 1) * 128],
                             wvsv[:, dc * 1024 + th * 512:dc * 1024 + (th + 1) * 512],
                             start=(dc == 0), stop=False)
                t.matmul(vps[ki % 2][:, :], ones1b[:, :],
                         vbr[0:1, th * 512:(th + 1) * 512],
                         start=False, stop=True).then_inc(sQK, 1)
        C["sQK"] += len(ks)

        @blk.scalar
        def _(s):
            s.wait_ge(sGE, ge_need)
            for ki, (tb, th) in enumerate(ks):
                s.wait_ge(sQK, qk0 + ki + 1)
                vsrc = vps[ki % 2][:, :].rearrange("p (m x d) -> p m x d",
                                                   m=4, x=2, d=64)
                s.activation(vTr[:, tb, th * 4:(th + 1) * 4, 0:64],
                             vsrc[:, :, 0, :], AF.Identity)
                s.activation(vTr[:, tb, th * 4:(th + 1) * 4, 129:193],
                             vsrc[:, :, 1, :], AF.Identity).then_inc(sEV, 2)
        C["sEV"] += 2 * len(ks)

    def vgb1_phase(b):
        with (
            nc.Block() as blk,
            nc.psum_tensor(f"vps0_{b}", [128, 512], F32) as vps0,
            nc.psum_tensor(f"vps1_{b}", [128, 512], F32) as vps1
        ):
            qk0 = C["sQK"]
            ev0 = C["sEV"]
            ge0 = C["sGE"]
            wv0 = C["sWv"]

            @blk.gpsimd
            def _(g):
                g.wait_ge(sWD, RBD_READY)
                if b == 0:
                    g.memset(vT[:, :], 0.0).then_inc(sGE, 1)
                else:
                    g.nop().then_inc(sGE, 1)
                g.wait_ge(sGE, ge0)
                g.dma_start(wvsv, wvd[:, :]).then_inc(sWv, 16)
            C["sWv"] += 16

            @blk.vector
            def _(v):
                v.wait_ge(sGE, ge0 + 1)
                v.memset(vTr[:, 0:5, :, 64:65], 1.0)
                v.memset(vTr[:, 0:5, :, 97:98], 1.0)
                v.memset(vTr[:, 14:16, :, 64:65], 1.0)
                v.memset(vTr[:, 14:16, :, 97:98], 1.0).then_inc(sGE, 1)
            C["sGE"] += 2

            _vproj(blk, b, KS1, [vps0, vps1], qk0, ev0, ge0 + 2, wv0 + 16)

    def vgb2_phase(b):
        with (
            nc.Block() as blk,
            nc.psum_tensor(f"vps0b_{b}", [128, 512], F32) as vps0,
            nc.psum_tensor(f"vps1b_{b}", [128, 512], F32) as vps1
        ):
            qk0 = C["sQK"]
            ev0 = C["sEV"]
            ge0 = C["sGE"]
            oe0 = C["sOE"]
            wv_need = C["sWv"]

            @blk.vector
            def _(v):
                v.wait_ge(sGE, ge0)
                v.memset(vTr[:, 5:14, :, 64:65], 1.0)
                v.memset(vTr[:, 5:14, :, 97:98], 1.0).then_inc(sGE, 1)
            C["sGE"] += 1

            @blk.tensor
            def _(t):
                t.wait_ge(sOE, oe0)
                t.wait_ge(sGE, ge0 + 1)
            _vproj(blk, b, KS2, [vps0, vps1], qk0, ev0, ge0 + 1, wv_need)

    def attn_phase(b):
        with (
            nc.Block() as blk,
            nc.psum_tensor(f"scp0_{b}", [128, 1024], F32) as scp0,
            nc.psum_tensor(f"scp1_{b}", [128, 1024], F32) as scp1,
            nc.psum_tensor(f"cxA0_{b}", [65, 512], F32) as cxA0,
            nc.psum_tensor(f"cxB0_{b}", [128, 512], F32) as cxB0,
            nc.psum_tensor(f"cxA1_{b}", [65, 512], F32) as cxA1,
            nc.psum_tensor(f"cxB1_{b}", [128, 512], F32) as cxB1
        ):
            scp = [scp0, scp1]
            cxA = [cxA0, cxA1]
            cxB = [cxB0, cxB1]
            qk0 = C["sQK"]
            ev0 = C["sEV"]
            wq0 = [C["sWq0"], C["sWq1"]]
            wk0 = [C["sWk0"], C["sWk1"]]
            gd0 = [C["sGD0"], C["sGD1"]]
            p0 = [C["sP0"], C["sP1"]]
            ap_all = C["sAP"]
            cb_all = C["sCB"]
            sq_all = C["sSQ"]

            def emit_av(t, il):
                hp = il // (NTQ * NS0)
                tq = (il // NS0) % NTQ
                s0 = il % NS0
                i_g = ig(b, il)
                slot = jslot(b, hp, tq)
                t.wait_ge(sE, i_g + 1)
                if s0 == 0 and slot > 1:
                    t.wait_ge(sV, slot - 1)
                    t.wait_ge(sVS, slot - 1)
                base = s0 * 1544 + hp * 193
                t.matmul(cxA[slot % 2][:, :], vT[:, base:base + 65], exb[il % 3][:, 0:512],
                         start=(s0 == 0), stop=(s0 == NS0 - 1))
                t.matmul(cxB[slot % 2][:, :], vT[:, base + 65:base + 193], exb[il % 3][:, 512:1024],
                         start=(s0 == 0), stop=(s0 == NS0 - 1)).then_inc(sA, 1)

            @blk.gpsimd
            def _(g):
                def load_pair(hp):
                    g.dma_start(gbct[hp % 2][:, :].rearrange("p (h j) -> p h j", h=2, j=2048),
                                bass.AP(gatescr, (2 * hp) * 2048,
                                        [[0, 128], [2048, 2], [1, 2048]])).then_inc(sGDp[hp % 2], 16)
                    g.dma_start(wqs[hp % 2][:, :].rearrange("p (c d) -> p c d", c=8, d=128),
                                bass.AP(wqd, hp * 128,
                                        [[8192, 128], [1024, 8], [1, 128]])).then_inc(sWqp[hp % 2], 16)
                    g.dma_start(wks[hp % 2][:, :].rearrange("p (c d) -> p c d", c=8, d=128),
                                bass.AP(wkd, hp * 128,
                                        [[8192, 128], [1024, 8], [1, 128]])).then_inc(sWkp[hp % 2], 16)

                g.wait_ge(sGS, 16 * (b + 1))
                load_pair(0)
                load_pair(1)
                for hp in range(NHP):
                    hpg = b * NHP + hp
                    if hp + 2 < NHP:
                        nhp = hp + 2
                        g.wait_ge(sG, ig(b, (hp + 1) * NTQ * NS0))
                        g.wait_ge(sQK, qk0 + 4 * (nhp - 1))
                        load_pair(nhp)
                    g.wait_ge(sVS, jslot(b, hp, NTQ - 1) + 1)
                    g.wait_ge(sV, jslot(b, hp, NTQ - 1) + 1)
                    if hpg > 0:
                        g.wait_ge(sDD, 16 * (11 * (hpg - 1) + 11))
                        g.wait_ge(sN, 4 * hpg)
                    g.dma_start(bass.AP(denscr, 0, [[4096, 1], [1, 4096]]),
                                bass.AP(dstg, 32 * 2048, [[32 * 2048, 2], [1, 2048]])).then_inc(sDD, 16)
                    g.wait_ge(sDD, 16 * (11 * hpg + 1))
                    g.dma_start(bass.AP(d128, 0, [[32, 128], [1, 32]]),
                                bass.AP(denscr, 0, [[32, 128], [1, 32]])).then_inc(sDD, 16)
                    g.wait_ge(sR, hpg + 1)
                    g.dma_start(bass.AP(denscr, 0, [[32, 128], [1, 32]]),
                                d128r[:, :]).then_inc(sDD, 16)
                    g.wait_ge(sDD, 16 * (11 * hpg + 3))
                    for idx in range(4):
                        g.dma_start(bcps[0:64, idx * 512:(idx + 1) * 512],
                                    bass.AP(denscr, 2048 + idx * 512, [[0, 64], [1, 512]])).then_inc(sDD, 16)
                        g.dma_start(bcps[64:128, idx * 512:(idx + 1) * 512],
                                    bass.AP(denscr, idx * 512, [[0, 64], [1, 512]])).then_inc(sDD, 16)
            C["sWq0"] += 16 * (NHP // 2)
            C["sWq1"] += 16 * (NHP // 2)
            C["sWk0"] += 16 * (NHP // 2)
            C["sWk1"] += 16 * (NHP // 2)
            C["sGD0"] += 16 * (NHP // 2)
            C["sGD1"] += 16 * (NHP // 2)
            C["sDD"] += 16 * 11 * NHP

            @blk.sync
            def _(sy):
                sy.wait_ge(sWD, RBD_READY)
                sy.wait_ge(sCB, cb_all)
                sy.wait_ge(sSQ, sq_all)
                sy.wait_ge(sAP, ap_all)
                for hp in range(NHP):
                    for tq in range(NTQ):
                        idx = hp * NTQ + tq
                        if idx >= 2:
                            sy.wait_ge(sG, ig(b, (idx - 1) * NS0))
                        for h in range(2):
                            sy.dma_start(pbT[idx % 2][:, h * 2432:(h + 1) * 2432],
                                         bass.AP(rbd, (2 * hp + h) * 4096 + 1536 - tq * 512,
                                                 [[1, 128], [1, 2432]])).then_inc(sPp[idx % 2], 16)

            @blk.tensor
            def _(t):
                t.wait_ge(sEV, ev0)
                for hp in range(NHP):
                    hpg = b * NHP + hp
                    t.wait_ge(sWqp[hp % 2], wq0[hp % 2] + 16 * (hp // 2 + 1))
                    t.wait_ge(sWkp[hp % 2], wk0[hp % 2] + 16 * (hp // 2 + 1))
                    if hp == 0:
                        t.wait_ge(sAP, ap_all)
                    prev_last = ig(b, hp * NTQ * NS0)
                    if prev_last > 0:
                        t.wait_ge(sE, prev_last)
                    for which, wsl in ((0, wks), (1, wqs)):
                        for half in range(2):
                            if which == 1:
                                t.wait_ge(sEV, ev0 + 4 * hp + 1 + half)
                            for dc in range(NDC):
                                for sl in range(2):
                                    mm = t.matmul(scp[half][:, sl * 512:(sl + 1) * 512],
                                                  wsl[hp % 2][:, dc * 128:(dc + 1) * 128],
                                                  xbf[:, dc * T + half * 1024 + sl * 512:
                                                      dc * T + half * 1024 + (sl + 1) * 512],
                                                  start=(dc == 0), stop=(dc == NDC - 1))
                            mm.then_inc(sQK, 1)
                    for tq in range(NTQ):
                        for s0 in range(NS0):
                            il = (hp * NTQ + tq) * NS0 + s0
                            i_g = ig(b, il)
                            if i_g >= 2:
                                t.wait_ge(sE, i_g - 1)
                            if tq == 0 and s0 == 0:
                                t.wait_ge(sEV, ev0 + 4 * hp + 3)
                            if tq == 0 and s0 == 1:
                                t.wait_ge(sEV, ev0 + 4 * hp + 4)
                            t.matmul(scp[il % 2][:, 0:512],
                                     kqp[0:64, s0 * 128:(s0 + 1) * 128],
                                     kqp[0:64, T + tq * 512:T + (tq + 1) * 512],
                                     start=True, stop=False, tile_position=(0, 0))
                            t.matmul(scp[il % 2][:, 512:1024],
                                     kqp[64:128, s0 * 128:(s0 + 1) * 128],
                                     kqp[64:128, T + tq * 512:T + (tq + 1) * 512],
                                     start=True, stop=False, tile_position=(64, 0))
                            if il > 1:
                                emit_av(t, il - 2)
                            t.wait_ge(sG, i_g + 1)
                            t.matmul(scp[il % 2][:, 0:512], i128s[:, :],
                                     Gb[il % 2][:, 0:512], start=False, stop=True)
                            t.matmul(scp[il % 2][:, 512:1024], i128s[:, :],
                                     Gb[il % 2][:, 512:1024],
                                     start=False, stop=True).then_inc(sS, 1)
                emit_av(t, NB_IL - 2)
                emit_av(t, NB_IL - 1)
            for _i in range(2):
                C[f"sP{_i}"] += 32 * (NHP * NTQ // 2)
            C["sQK"] += 4 * NHP

            @blk.scalar
            def _(s):
                den_pending = []

                for hp in range(NHP):
                    s.wait_ge(sQK, qk0 + 4 * hp + 1)
                    s.activation(kqp[:, 0:1024], scp0[:, :], AF.Identity,
                                 bias=bks[:, hp:hp + 1]).then_inc(sEV, 1)
                    s.wait_ge(sQK, qk0 + 4 * hp + 2)
                    s.activation(kqp[:, 1024:2048], scp1[:, :], AF.Identity,
                                 bias=bks[:, hp:hp + 1]).then_inc(sEV, 1)
                    s.wait_ge(sQK, qk0 + 4 * hp + 3)
                    s.activation(kqp[:, T:T + 1024], scp0[:, :], AF.Identity,
                                 bias=bq8[:, hp:hp + 1], scale=eighth[:, :]).then_inc(sEV, 1)
                    s.wait_ge(sQK, qk0 + 4 * hp + 4)
                    s.activation(kqp[:, T + 1024:2 * T], scp1[:, :], AF.Identity,
                                 bias=bq8[:, hp:hp + 1], scale=eighth[:, :]).then_inc(sEV, 1)
                    for tq in range(NTQ):
                        for s0 in range(NS0):
                            il = (hp * NTQ + tq) * NS0 + s0
                            i_g = ig(b, il)
                            if s0 == 3 and den_pending:
                                hp_e, tq_e, igl = den_pending[0]
                                sl_e = jslot(b, hp_e, tq_e)
                                s.wait_ge(sA, igl + 1)
                                s.activation(ctxu[hp_e % 2][64:128, tq_e * 512:(tq_e + 1) * 512],
                                             cxB[sl_e % 2][64:128, :], AF.Identity)
                            if s0 == 5 and den_pending:
                                hp_e, tq_e, igl = den_pending.pop(0)
                                hpg_e = b * NHP + hp_e
                                sl_e = jslot(b, hp_e, tq_e)
                                if tq_e == 0 and hpg_e > 0:
                                    s.wait_ge(sDD, 16 * (11 * (hpg_e - 1) + 1))
                                s.activation(dstg[32:33, tq_e * 512:(tq_e + 1) * 512],
                                             cxB[sl_e % 2][32:33, :], AF.Identity).then_inc(sVS, 1)
                            s.wait_ge(sS, i_g + 1)
                            if i_g >= 3:
                                s.wait_ge(sA, i_g - 2)
                            s.activation(exb[il % 3][:, :], scp[il % 2][:, :], AF.Exp,
                                         bias=ebias[:, :]).then_inc(sE, 1)
                            if s0 == NS0 - 1:
                                den_pending.append((hp, tq, i_g))
                while den_pending:
                    hp_e, tq_e, igl = den_pending.pop(0)
                    sl_e = jslot(b, hp_e, tq_e)
                    s.wait_ge(sA, igl + 1)
                    s.activation(ctxu[hp_e % 2][64:128, tq_e * 512:(tq_e + 1) * 512],
                                 cxB[sl_e % 2][64:128, :], AF.Identity)
                    s.activation(dstg[32:33, tq_e * 512:(tq_e + 1) * 512],
                                 cxB[sl_e % 2][32:33, :], AF.Identity).then_inc(sVS, 1)
            C["sEV"] += 4 * NHP

            @blk.vector
            def _(v):
                v.wait_ge(sXWp[0], 16 * 16 * b)
                v.wait_ge(sXWp[1], 16 * 16 * b)
                v.wait_ge(sOp[0], 16 * 16 * b)
                v.wait_ge(sOp[1], 16 * 16 * b)
                pending = []

                def emit_evac(vv, step):
                    hp_e, tq_e, igl = pending[0]
                    hpg_e = b * NHP + hp_e
                    sl = jslot(b, hp_e, tq_e)
                    cu = ctxu[hp_e % 2]
                    c0 = tq_e * 512
                    if step == 0:
                        vv.wait_ge(sA, igl + 1)
                        vv.tensor_copy(cu[0:64, c0:c0 + 512], cxA[sl % 2][0:64, :])
                    else:
                        if tq_e == 0 and hpg_e > 0:
                            vv.wait_ge(sDD, 16 * (11 * (hpg_e - 1) + 1))
                        vv.tensor_copy(dstg[64:65, c0:c0 + 512],
                                       cxA[sl % 2][64:65, :]).then_inc(sV, 1)
                        pending.pop(0)

                for hp in range(NHP):
                    hpg = b * NHP + hp
                    for tq in range(NTQ):
                        for s0 in range(NS0):
                            il = (hp * NTQ + tq) * NS0 + s0
                            i_g = ig(b, il)
                            if s0 == 2 and pending:
                                emit_evac(v, 0)
                            if s0 == 4 and pending:
                                emit_evac(v, 1)
                            if tq == 0 and s0 == 7 and hp > 0:
                                v.wait_ge(sDD, 16 * (11 * (hpg - 1) + 2))
                                v.reciprocal(d128r[:, :], d128[:, :]).then_inc(sR, 1)
                            if hp > 0 and tq == 1 and s0 in (2, 6, 10, 14):
                                idx = {2: 0, 6: 1, 10: 2, 14: 3}[s0]
                                hpp = hp - 1
                                v.wait_ge(sDD, 16 * (11 * (hpg - 1) + 11))
                                v.wait_ge(sV, jslot(b, hpp, NTQ - 1) + 1)
                                v.wait_ge(sVS, jslot(b, hpp, NTQ - 1) + 1)
                                v.tensor_mul(
                                    ctxn[:, hpp * T + idx * 512:hpp * T + (idx + 1) * 512],
                                    ctxu[hpp % 2][:, idx * 512:(idx + 1) * 512],
                                    bcps[:, idx * 512:(idx + 1) * 512]).then_inc(sN, 1)
                            idx = hp * NTQ + tq
                            if s0 == 0:
                                v.wait_ge(sPp[idx % 2], p0[idx % 2] + 32 * (idx // 2 + 1))
                            if il == 0 or (s0 == 0 and tq == 0):
                                v.wait_ge(sGDp[hp % 2], gd0[hp % 2] + 16 * (hp // 2 + 1))
                            if i_g >= 2:
                                v.wait_ge(sS, i_g - 1)
                            rev = bass.AP(pbT[idx % 2].tensor, s0 * 128 + 511,
                                          [[4864, 128], [2432, 2], [-1, 512]])
                            gsl = bass.AP(gbct[hp % 2], tq * 512,
                                          [[4096, 128], [2048, 2], [1, 512]])
                            v.tensor_mul(Gb[il % 2][:, :], rev, gsl).then_inc(sG, 1)
                            if s0 == NS0 - 1:
                                pending.append((hp, tq, i_g))
                while pending:
                    emit_evac(v, 0)
                    emit_evac(v, 1)
                hpgl = b * NHP + NHP - 1
                v.wait_ge(sDD, 16 * (11 * hpgl + 2))
                v.reciprocal(d128r[:, :], d128[:, :]).then_inc(sR, 1)
                v.wait_ge(sDD, 16 * (11 * hpgl + 11))
                for idx in range(4):
                    v.tensor_mul(ctxn[:, (NHP - 1) * T + idx * 512:(NHP - 1) * T + (idx + 1) * 512],
                                 ctxu[(NHP - 1) % 2][:, idx * 512:(idx + 1) * 512],
                                 bcps[:, idx * 512:(idx + 1) * 512]).then_inc(sN, 1)

    def out_phase(b):
        with (
            nc.Block() as blk,
            nc.psum_tensor(f"oup0_{b}", [128, 512], F32) as oup0,
            nc.psum_tensor(f"oup1_{b}", [128, 512], F32) as oup1
        ):
            oup = [oup0, oup1]
            ot0 = C["sOT"]
            oe0 = C["sOE"]
            rv0 = C["sRV"]
            x0 = [C["sX0"], C["sX1"]]
            xw0 = [C["sXW0"], C["sXW1"]]
            wd0 = C["sWD"]
            sc_all = C["sSC"]

            @blk.gpsimd
            def _(g):
                g.dma_start(wosv, wod[:, :]).then_inc(sWD, 16)
                for j in range(32):
                    tqq, dcO = divmod(j, 8)
                    if j >= 2:
                        g.wait_ge(sRV, rv0 + j - 1)
                    g.dma_start(xrst[j % 2],
                                xd[b, dcO, :, tqq * 512:(tqq + 1) * 512]).then_inc(sXp[j % 2], 16)
            C["sWD"] += 16
            C["sX0"] += 16 * 16
            C["sX1"] += 16 * 16

            @blk.tensor
            def _(t):
                t.wait_ge(sWD, wd0 + 16)
                t.wait_ge(sSC, sc_all)
                for j in range(32):
                    tqq, dcO = divmod(j, 8)
                    if j >= 2:
                        t.wait_ge(sOE, oe0 + j - 1)
                    for hp in range(NHP):
                        mm = t.matmul(oup[j % 2][:, :],
                                      wosv[:, (hp * 8 + dcO) * 128:(hp * 8 + dcO + 1) * 128],
                                      ctxn[:, hp * T + tqq * 512:hp * T + (tqq + 1) * 512],
                                      start=(hp == 0), stop=(hp == NHP - 1))
                    mm.then_inc(sOT, 1)
            C["sOT"] += 32

            @blk.scalar
            def _(s):
                for j in range(32):
                    tqq, dcO = divmod(j, 8)
                    s.wait_ge(sOT, ot0 + j + 1)
                    if j >= 2:
                        s.wait_ge(sRV, rv0 + j - 1)
                    s.activation(obuf[j % 2], oup[j % 2][:, :], AF.Identity,
                                 bias=bos[:, dcO:dcO + 1]).then_inc(sOE, 1)
            C["sOE"] += 32

            @blk.vector
            def _(v):
                for j in range(32):
                    v.wait_ge(sOE, oe0 + j + 1)
                    v.wait_ge(sXp[j % 2], x0[j % 2] + 16 * (j // 2 + 1))
                    if j >= 2:
                        v.wait_ge(sXWp[j % 2], xw0[j % 2] + 16 * (j // 2))
                    v.tensor_add(o2buf[j % 2], obuf[j % 2],
                                 xrst[j % 2]).then_inc(sRV, 1)
            C["sRV"] += 32

            @blk.sync
            def _(sy):
                for j in range(32):
                    tqq, dcO = divmod(j, 8)
                    sy.wait_ge(sRV, rv0 + j + 1)
                    sy.dma_start(x1d[b, dcO, :, tqq * 512:(tqq + 1) * 512],
                                 o2buf[j % 2]).then_inc(sXWp[j % 2], 16)
            C["sXW0"] += 16 * 16
            C["sXW1"] += 16 * 16

    def ffn_phase(b):
        with (
            nc.Block() as blk,
            nc.psum_tensor(f"fps0_{b}", [128, 512], F32) as fps0,
            nc.psum_tensor(f"fps1_{b}", [128, 512], F32) as fps1
        ):
            fps = [fps0, fps1]
            ot0 = C["sOT"]
            oe0 = C["sOE"]
            rv0 = C["sRV"]
            x0 = [C["sX0"], C["sX1"]]
            o0 = [C["sO0"], C["sO1"]]
            wd0 = C["sWD"]
            ap_f = C["sAP"]

            @blk.gpsimd
            def _(g):
                g.dma_start(w8sv, w8d[:, :]).then_inc(sWD, 16)
                for j in range(32):
                    tqq, dcO = divmod(j, 8)
                    if j >= 2:
                        g.wait_ge(sRV, rv0 + j - 1)
                    g.dma_start(xrst[j % 2],
                                x1d[b, dcO, :, tqq * 512:(tqq + 1) * 512]).then_inc(sXp[j % 2], 16)
            C["sWD"] += 16
            C["sX0"] += 16 * 16
            C["sX1"] += 16 * 16

            @blk.tensor
            def _(t):
                t.wait_ge(sWD, wd0 + 16)
                t.wait_ge(sAP, ap_f)
                for j in range(32):
                    tqq, dcO = divmod(j, 8)
                    if j >= 2:
                        t.wait_ge(sOE, oe0 + j - 1)
                    for dc in range(NDC):
                        mm = t.matmul(fps[j % 2][:, :],
                                      w8sv[:, dc * 1024 + dcO * 128:dc * 1024 + (dcO + 1) * 128],
                                      z8v[:, dc * T + tqq * 512:dc * T + (tqq + 1) * 512],
                                      start=(dc == 0), stop=(dc == NDC - 1))
                    mm.then_inc(sOT, 1)
            C["sOT"] += 32

            @blk.scalar
            def _(s):
                for j in range(32):
                    tqq, dcO = divmod(j, 8)
                    s.wait_ge(sOT, ot0 + j + 1)
                    if j >= 2:
                        s.wait_ge(sRV, rv0 + j - 1)
                    s.activation(obuf[j % 2], fps[j % 2][:, :], AF.Identity,
                                 bias=bfs[:, dcO:dcO + 1], scale=isc[:, 1:2]).then_inc(sOE, 1)
            C["sOE"] += 32

            @blk.vector
            def _(v):
                for j in range(32):
                    v.wait_ge(sOE, oe0 + j + 1)
                    v.wait_ge(sXp[j % 2], x0[j % 2] + 16 * (j // 2 + 1))
                    if j >= 2:
                        v.wait_ge(sOp[j % 2], o0[j % 2] + 16 * (j // 2))
                    v.tensor_add(o2buf[j % 2], obuf[j % 2],
                                 xrst[j % 2]).then_inc(sRV, 1)
            C["sRV"] += 32

            @blk.sync
            def _(sy):
                for j in range(32):
                    tqq, dcO = divmod(j, 8)
                    sy.wait_ge(sRV, rv0 + j + 1)
                    sy.dma_start(outd[b, dcO, :, tqq * 512:(tqq + 1) * 512],
                                 o2buf[j % 2]).then_inc(sOp[j % 2], 16)
            C["sO0"] += 16 * 16
            C["sO1"] += 16 * 16

    import os
    nph = int(os.environ.get("ATT_NPH", "999"))
    done = [0]

    def _run(f, *a):
        if done[0] < nph:
            f(*a)
            done[0] += 1

    for b in range(B_C):
        if b == 0:
            _run(ln_phase, 0, 0)
            _run(vga_phase, 0)
            _run(vgb1_phase, 0)
        _run(vgb2_phase, b)
        _run(attn_phase, b)
        if b + 1 < B_C:
            _run(ln0_pre, b + 1)
        _run(out_phase, b)
        if b + 1 < B_C:
            _run(ln0_post, b + 1)
            _run(vga_phase, b + 1)
        _run(ln_phase, b, 1)
        if b + 1 < B_C:
            _run(vgb1_phase, b + 1)
        _run(ffn_phase, b)

    if nph >= 35:
        with nc.Block() as blk:
            @blk.sync
            def _(sy):
                sy.wait_ge(sXWp[0], 16 * 16 * B_C)
                sy.wait_ge(sXWp[1], 16 * 16 * B_C)
                sy.wait_ge(sOp[0], 16 * 16 * B_C)
                sy.wait_ge(sOp[1], 16 * 16 * B_C)

    return nc


_CACHE = {}


def _consts():
    if "oht" not in _CACHE:
        bt = _bucket_table()
        oh = np.zeros((320, 4096), dtype=BF)
        oh[bt, np.arange(4095)] = 1.0
        _CACHE["oht"] = oh
    oh = _CACHE["oht"]
    i128 = np.ascontiguousarray(np.eye(128, dtype=BF))
    selR = np.zeros((2, 128), dtype=BF)
    selR[1, 0:64] = 1.0
    selR[0, 64:128] = 1.0
    sel232 = np.zeros((2, 32), dtype=BF)
    sel232[0, 0:16] = 1.0
    sel232[1, 16:32] = 1.0
    p82 = np.zeros((8, 2), dtype=BF)
    p82[0:4, 0] = 1.0
    p82[4:8, 1] = 1.0
    return oh, i128, selR, sel232, p82


def _pack_w(w):
    return np.ascontiguousarray(
        np.asarray(w, np.float32).T.reshape(8, 128, 1024).transpose(1, 0, 2).reshape(128, 8192))


def _pack_wo(w):
    return np.ascontiguousarray(
        np.asarray(w, np.float32).T.reshape(8, 128, 8, 128).transpose(1, 0, 2, 3).reshape(128, 8192))


def kernel(L0_attn_hidden_states, L0_attn_q_w, L0_attn_q_b, L0_attn_k_w, L0_attn_k_b,
           L0_attn_v_w, L0_attn_v_b, L0_attn_out_w, L0_attn_out_b, L0_attn_rel_embed,
           L0_attn_gru_const, L0_attn_gru_w, L0_attn_gru_b, L0_ffn_weight_fp8,
           L0_ffn_bias, L0_ffn_input_scale, L0_ffn_weight_scale):
    x = np.asarray(L0_attn_hidden_states, dtype=np.float32)
    oh, i128, selR, sel232, p82 = _consts()

    shared = {
        "wq": _pack_w(L0_attn_q_w),
        "wk": _pack_w(L0_attn_k_w),
        "wv": _pack_w(L0_attn_v_w),
        "wo": _pack_wo(L0_attn_out_w),
        "w8": _pack_w(L0_ffn_weight_fp8),
        "bq": np.ascontiguousarray(np.asarray(L0_attn_q_b, np.float32).reshape(8, 128).T),
        "bk": np.ascontiguousarray(np.asarray(L0_attn_k_b, np.float32).reshape(8, 128).T),
        "bvr": np.ascontiguousarray(np.asarray(L0_attn_v_b, np.float32)[None, :]),
        "bo": np.ascontiguousarray(np.asarray(L0_attn_out_b, np.float32).reshape(8, 128).T),
        "bfn": np.ascontiguousarray(np.asarray(L0_ffn_bias, np.float32).reshape(8, 128).T),
        "rel": np.ascontiguousarray(np.asarray(L0_attn_rel_embed, np.float32)),
        "oht": oh,
        "gw": np.ascontiguousarray(np.asarray(L0_attn_gru_w, np.float32)),
        "gbi": np.ascontiguousarray(np.asarray(L0_attn_gru_b, np.float32).reshape(8, 1)),
        "gci": np.ascontiguousarray(np.asarray(L0_attn_gru_const, np.float32).reshape(16, 1)),
        "scli": np.array([[float(np.asarray(L0_ffn_input_scale).ravel()[0]),
                           float(np.asarray(L0_ffn_weight_scale).ravel()[0])]], np.float32),
        "i128": i128, "selR": selR, "sel232": sel232, "p82": p82,
    }
    xt = np.ascontiguousarray(x.reshape(32, T, 8, 128).transpose(0, 2, 3, 1))

    nc = build_nc()
    in_maps = []
    for c in range(8):
        m = dict(shared)
        m["x"] = np.ascontiguousarray(xt[4 * c:4 * c + 4])
        in_maps.append(m)
    res = run_bass_kernel_spmd(nc, in_maps, core_ids=list(range(8)))
    outs = []
    for c in range(8):
        o = np.asarray(res.results[c]["out"])
        outs.append(o.reshape(4, 1024, T).transpose(0, 2, 1))
    return np.ascontiguousarray(np.concatenate(outs, axis=0), dtype=np.float32)


# version 14
# speedup vs baseline: 1.5603x; 1.0040x over previous
import contextlib

import numpy as np
import ml_dtypes

import concourse.bass as bass
import concourse.mybir as mybir
from concourse.bass_utils import run_bass_kernel_spmd

F32 = mybir.dt.float32
BF16 = mybir.dt.bfloat16
FP8 = mybir.dt.float8e4
U8 = mybir.dt.uint8
AF = mybir.ActivationFunctionType
OP = mybir.AluOpType
BF = ml_dtypes.bfloat16

B_C = 4
T = 2048
NHP = 8
NTQ = 4
TQ = 512
NS0 = 16
NDC = 8
EXPB = -4.0
NB_IL = NHP * NTQ * NS0
GDPB = 16 + 16 * 2 * NHP

NUM_BUCKETS = 320
MAX_DISTANCE = 800


def _bucket_table():
    import math
    d = np.arange(-(T - 1), T)
    nb = NUM_BUCKETS // 2
    rel_b = (d > 0).astype(np.int64) * nb
    rp = np.abs(d)
    me = nb // 2
    small = rp < me
    rl = np.log(np.maximum(rp, 1).astype(np.float32) / me)
    rl = rl / math.log(MAX_DISTANCE / me)
    rl = rl * (nb - me)
    rl = (me + rl).astype(np.int64)
    rl = np.minimum(rl, nb - 1)
    return rel_b + np.where(small, rp, rl)


def build_nc():
    nc = bass.Bass("TRN2", target_bir_lowering=False)

    xd = nc.dram_tensor("x", [B_C, NDC, 128, T], F32, kind="ExternalInput")
    wqd = nc.dram_tensor("wq", [128, 8192], F32, kind="ExternalInput")
    wkd = nc.dram_tensor("wk", [128, 8192], F32, kind="ExternalInput")
    wvd = nc.dram_tensor("wv", [128, 8192], F32, kind="ExternalInput")
    wod = nc.dram_tensor("wo", [128, 8192], F32, kind="ExternalInput")
    w8d = nc.dram_tensor("w8", [128, 8192], F32, kind="ExternalInput")
    bqd = nc.dram_tensor("bq", [128, 8], F32, kind="ExternalInput")
    bkd = nc.dram_tensor("bk", [128, 8], F32, kind="ExternalInput")
    bvrd = nc.dram_tensor("bvr", [1, 1024], F32, kind="ExternalInput")
    bod = nc.dram_tensor("bo", [128, 8], F32, kind="ExternalInput")
    bfd = nc.dram_tensor("bfn", [128, 8], F32, kind="ExternalInput")
    reld = nc.dram_tensor("rel", [320, 16], F32, kind="ExternalInput")
    ohtd = nc.dram_tensor("oht", [320, 4096], BF16, kind="ExternalInput")
    gwd_d = nc.dram_tensor("gw", [8, 64], F32, kind="ExternalInput")
    gbid = nc.dram_tensor("gbi", [8, 1], F32, kind="ExternalInput")
    gcid = nc.dram_tensor("gci", [16, 1], F32, kind="ExternalInput")
    sclid = nc.dram_tensor("scli", [1, 2], F32, kind="ExternalInput")
    i128d = nc.dram_tensor("i128", [128, 128], BF16, kind="ExternalInput")
    selRd = nc.dram_tensor("selR", [2, 128], BF16, kind="ExternalInput")
    sel232d = nc.dram_tensor("sel232", [2, 32], BF16, kind="ExternalInput")
    p82d = nc.dram_tensor("p82", [8, 2], BF16, kind="ExternalInput")

    outd = nc.dram_tensor("out", [B_C, NDC, 128, T], F32, kind="ExternalOutput")

    rbd = nc.dram_tensor("rbd", [16, 4096], BF16, kind="Internal")
    x1d = nc.dram_tensor("x1d", [B_C, NDC, 128, T], F32, kind="Internal")
    statscr = nc.dram_tensor("statscr", [4096], F32, kind="Internal")
    denscr = nc.dram_tensor("denscr", [4096], F32, kind="Internal")
    gatescr = nc.dram_tensor("gatescr", [16, 2048], BF16, kind="Internal")

    es = contextlib.ExitStack()
    nc._kernel_es = es

    def sb(name, shape, dt_):
        return es.enter_context(nc.sbuf_tensor(name, shape, dt_))

    def sem(name):
        return es.enter_context(nc.semaphore(name))

    arA = sb("arA", [128, 49408], U8)
    arB = sb("arB", [128, 32768], U8)
    arC = sb("arC", [128, 32768], U8)

    vT = arA[:, :].bitcast(BF16)
    vTr = vT.rearrange("p (s h c) -> p s h c", s=NS0, h=NHP, c=193)
    wosv = arA[:, 0:16384].bitcast(BF16)
    w8sv = arA[:, 16384:24576].bitcast(FP8)
    z8v = arA[:, 24576:40960].bitcast(FP8)
    xbf = arB[:, :].bitcast(BF16)
    ctxn = arC[:, :].bitcast(BF16)
    mbc = arC[:, 0:8192].bitcast(F32)
    ibc = arC[:, 8192:16384].bitcast(F32)
    gsigA = arC[:, 16384:24576].bitcast(F32)
    gsigB = arC[:, 24576:32768].bitcast(F32)
    wvsv = arC[:, 16384:32768].bitcast(BF16)

    xstage_t = [sb(f"xst{i}", [128, 2432], F32) for i in range(2)]
    xstage = [x[:, 0:2048] for x in xstage_t]
    ctxu = [sb(f"ctxu{i}", [128, T], BF16) for i in range(2)]
    kqp = sb("kqp", [128, 2 * T], BF16)
    sqst = kqp[:, 0:2048]
    sq2 = kqp[:, 2048:4096]
    gbct = [sb(f"gbc{i}", [128, 2 * T], BF16) for i in range(2)]
    tstt = [g[:, :].bitcast(F32) for g in gbct]
    wqs = [sb(f"wqs{i}", [128, 1024], BF16) for i in range(2)]
    wks = [sb(f"wks{i}", [128, 1024], BF16) for i in range(2)]
    obuf = [w[:, :].bitcast(F32) for w in wqs]
    xrst = [w[:, :].bitcast(F32) for w in wks]
    pbT = [x[:, :].bitcast(BF16) for x in xstage_t]
    Gb = [sb(f"Gb{i}", [128, 1024], BF16) for i in range(2)]
    exb = [sb(f"exb{i}", [128, 1024], BF16) for i in range(3)]
    o2buf = [g[:, :].bitcast(F32) for g in Gb]
    dstg = sb("dstg", [128, 2048], F32)
    sstv = dstg[0:1, 0:2048]
    d128 = sb("d128", [128, 32], F32)
    d128r = sb("d128r", [128, 32], F32)
    bcps = sb("bcps", [128, 2048], BF16)
    gate16 = sb("gate16", [16, 2048], BF16)
    dstat = sb("dstat", [128, 48], F32)

    bq8 = sb("bq8", [128, 8], F32)
    bqraw = sb("bqraw", [128, 8], F32)
    bks = sb("bks", [128, 8], F32)
    bos = sb("bos", [128, 8], F32)
    bfs = sb("bfs", [128, 8], F32)
    vbr = sb("vbr", [1, 1024], BF16)
    gwdb = sb("gwdb", [8, 128], BF16)
    gbr = sb("gbr", [8, 1], BF16)
    gcs = sb("gcs", [16, 1], F32)
    scl = sb("scl", [1, 2], F32)
    invs_sb = sb("invs", [1, 1], F32)
    sws_sb = sb("sws", [1, 1], F32)
    isc = sb("isc", [128, 2], F32)
    i128s = sb("i128s", [128, 128], BF16)
    selRs = sb("selRs", [2, 128], BF16)
    sel232s = sb("sel232s", [2, 32], BF16)
    p82s = sb("p82s", [8, 2], BF16)
    Tg = sb("Tg", [128, 256], BF16)
    gb2s = sb("gb2s", [2, 1], BF16)
    gb32f = sb("gb32f", [16, 2], F32)
    smalls = sb("smalls", [128, 4], F32)
    onesb = smalls[:, 0:1].bitcast(BF16)[:, 0:1]
    ones1b = sb("ones1b", [1, 128], BF16)
    ones1f = sb("ones1f", [1, 128], F32)
    ebias = smalls[:, 1:2]
    eighth = smalls[:, 2:3]
    dumm = smalls[0:1, 3:4]

    sWD = sem("sWD")
    sGS = sem("sGS")
    sGDp = [sem("sGD0"), sem("sGD1")]
    sXp = [sem("sX0"), sem("sX1")]
    sPp = [sem(f"sP{i}") for i in range(2)]
    sWqp = [sem("sWq0"), sem("sWq1")]
    sWkp = [sem("sWk0"), sem("sWk1")]
    sXWp = [sem("sXW0"), sem("sXW1")]
    sOp = [sem("sO0"), sem("sO1")]
    sSQ = sem("sSQ")
    sCB = sem("sCB")
    sST = sem("sST")
    sSC = sem("sSC")
    sBD = sem("sBD")
    sAP = sem("sAP")
    sQK = sem("sQK")
    sEV = sem("sEV")
    sG = sem("sG")
    sS = sem("sS")
    sE = sem("sE")
    sA = sem("sA")
    sV = sem("sV")
    sDD = sem("sDD")
    sR = sem("sR")
    sB = sem("sB")
    sN = sem("sN")
    sOT = sem("sOT")
    sOE = sem("sOE")
    sRV = sem("sRV")
    sTI = sem("sTI")
    sVI = sem("sVI")
    sGE = sem("sGE")
    sVC = sem("sVC")
    sVS = sem("sVS")
    sXs = [sem("sXs0"), sem("sXs1")]
    sBs = sem("sBs")

    C = {k: 0 for k in ("sWD sGS sGD0 sGD1 sX0 sX1 sP0 sP1 sWq0 sWq1 sWk0 sWk1 "
                        "sXW0 sXW1 sO0 sO1 sSQ sCB sST sSC sBD sAP sQK sEV sDD "
                        "sOT sOE sRV sGE sVC sXs0 sXs1 sBs sWv").split()}

    def ig(b, il):
        return b * NB_IL + il

    def jslot(b, hp, tq):
        return (b * NHP + hp) * NTQ + tq

    ohts = arA[:, 0:24576].bitcast(BF16)
    rbsb = arA[:, 24576:40960].bitcast(BF16)[0:16, 0:4096]
    rels = arC[:, 0:96].bitcast(BF16)
    with nc.Block() as blk:
        @blk.gpsimd
        def _(g):
            for dst, src in (
                (bqraw[:, :], bqd[:, :]), (bks[:, :], bkd[:, :]),
                (bos[:, :], bod[:, :]), (bfs[:, :], bfd[:, :]),
                (vbr[:, :], bvrd[:, :]),
                (rels[0:128, 0:16], reld[0:128, :]),
                (rels[0:128, 16:32], reld[128:256, :]),
                (rels[0:64, 32:48], reld[256:320, :]),
                (ohts[0:128, 0:4096], ohtd[0:128, :]),
                (ohts[0:128, 4096:8192], ohtd[128:256, :]),
                (ohts[0:64, 8192:12288], ohtd[256:320, :]),
                (gwdb[:, 0:64], gwd_d[:, :]), (gwdb[:, 64:128], gwd_d[:, :]),
                (gbr[:, :], gbid[:, :]), (gcs[:, :], gcid[:, :]),
                (scl[:, :], sclid[:, :]), (i128s[:, :], i128d[:, :]),
                (selRs[:, :], selRd[:, :]), (sel232s[:, :], sel232d[:, :]),
                (p82s[:, :], p82d[:, :]),
            ):
                g.dma_start(dst, src).then_inc(sWD, 16)
            g.memset(onesb[:, :], 1.0)
            g.memset(ones1b[:, :], 1.0)
            g.memset(ones1f[:, :], 1.0)
            g.memset(ebias[:, :], EXPB)
            g.memset(eighth[:, :], 0.125)
            g.memset(dumm[:, :], 0.0)
            g.memset(dstg[:, :], 0.0)
            g.memset(Tg[:, :], 0.0).then_inc(sTI, 1)
            g.wait_ge(sVI, 4)
            g.dma_start(rbd[:, :], rbsb[:, :]).then_inc(sWD, 16)

        with (
            nc.psum_tensor("rbps", [16, 1024], F32) as rbps,
            nc.psum_tensor("gw2ps", [128, 2], F32) as gw2ps,
            nc.psum_tensor("gb2ps", [2, 1], F32) as gb2ps,
            nc.psum_tensor("gb32ps", [16, 2], F32) as gb32ps,
            nc.psum_tensor("iscps", [128, 2], F32) as iscps,
        ):
            @blk.tensor
            def _(t):
                t.wait_ge(sWD, 16 * 20)
                t.wait_ge(sTI, 1)
                cnts = [128, 128, 64]
                for n4 in range(4):
                    if n4 >= 1:
                        t.wait_ge(sVI, n4)
                    for c in range(3):
                        for half in range(2):
                            mm = t.matmul(rbps[:, half * 512:(half + 1) * 512],
                                          rels[0:cnts[c], 16 * c:16 * (c + 1)],
                                          ohts[0:cnts[c],
                                               c * 4096 + n4 * 1024 + half * 512:
                                               c * 4096 + n4 * 1024 + (half + 1) * 512],
                                          start=(c == 0), stop=(c == 2))
                    mm.then_inc(sTI, 1)
                t.matmul(gw2ps[:, :], gwdb[:, :], p82s[:, :],
                         start=True, stop=True).then_inc(sTI, 1)
                t.matmul(gb2ps[:, :], p82s[:, :], gbr[:, :],
                         start=True, stop=True).then_inc(sTI, 1)
                t.wait_ge(sVI, 6)
                t.matmul(gb32ps[:, 0:1], sel232s[:, 0:16], gb2s[:, :],
                         start=True, stop=True)
                t.matmul(gb32ps[:, 1:2], sel232s[:, 16:32], gb2s[:, :],
                         start=True, stop=True).then_inc(sTI, 1)
                t.wait_ge(sVI, 7)
                t.matmul(iscps[:, 0:1], ones1f[:, :], invs_sb[:, :], start=True, stop=True)
                t.matmul(iscps[:, 1:2], ones1f[:, :], sws_sb[:, :],
                         start=True, stop=True).then_inc(sTI, 1)

            @blk.vector
            def _(v):
                for n4 in range(4):
                    v.wait_ge(sTI, 2 + n4)
                    v.tensor_copy(rbsb[:, n4 * 1024:(n4 + 1) * 1024],
                                  rbps[:, :]).then_inc(sVI, 1)
                v.wait_ge(sTI, 6)
                for hp in range(NHP):
                    a0 = hp * 16
                    b0 = 128 + hp * 16
                    v.tensor_copy(Tg[0:64, a0 + 2 * hp:a0 + 2 * hp + 1], gw2ps[0:64, 0:1])
                    v.tensor_copy(Tg[64:128, a0 + 2 * hp + 1:a0 + 2 * hp + 2], gw2ps[64:128, 0:1])
                    v.tensor_copy(Tg[0:64, b0 + 2 * hp:b0 + 2 * hp + 1], gw2ps[0:64, 1:2])
                    cp = v.tensor_copy(Tg[64:128, b0 + 2 * hp + 1:b0 + 2 * hp + 2], gw2ps[64:128, 1:2])
                cp.then_inc(sVI, 1)
                v.wait_ge(sTI, 7)
                v.tensor_copy(gb2s[:, :], gb2ps[:, :]).then_inc(sVI, 1)
                v.reciprocal(invs_sb[:, :], scl[0:1, 0:1])
                v.tensor_mul(sws_sb[:, :], scl[0:1, 0:1], scl[0:1, 1:2]).then_inc(sVI, 1)
                v.wait_ge(sTI, 8)
                v.tensor_copy(gb32f[:, :], gb32ps[:, :])
                v.wait_ge(sTI, 9)
                v.tensor_copy(isc[:, :], iscps[:, :])
                v.tensor_scalar(bq8[:, :], bqraw[:, :], 0.125, None,
                                OP.mult).then_inc(sVI, 1)

    C["sWD"] = 16 * 21
    RBD_READY = C["sWD"]

    def ln_phase(b, which):
        src = xd if which == 0 else x1d
        with (
            nc.Block() as blk,
            nc.psum_tensor(f"sxp{b}_{which}", [1, 2048], F32) as sxp,
            nc.psum_tensor(f"sx2p{b}_{which}", [1, 2048], F32) as sx2p
        ):
            x0 = [C["sX0"], C["sX1"]]
            cb0 = C["sCB"]
            sq0 = C["sSQ"]
            st0 = C["sST"]
            sc0 = C["sSC"]
            bd0 = C["sBD"]
            bsF = C["sBs"]
            evL = C["sEV"]

            @blk.gpsimd
            def _(g):
                if which == 1:
                    g.wait_ge(sXWp[0], 16 * 16 * (b + 1))
                    g.wait_ge(sXWp[1], 16 * 16 * (b + 1))
                for dc in range(NDC):
                    if dc >= 2:
                        g.wait_ge(sCB, cb0 + dc - 1)
                        g.wait_ge(sSQ, sq0 + dc - 1)
                    g.dma_start(xstage[dc % 2], src[b, dc, :, :]).then_inc(sPq := sXp[dc % 2], 16)
            C["sX0"] += 16 * (NDC // 2)
            C["sX1"] += 16 * (NDC // 2)

            @blk.sync
            def _(sy):
                sy.wait_ge(sBD, bd0)
                sy.wait_ge(sBs, bsF)
                sy.wait_ge(sSC, sc0 + 1)
                sy.dma_start(bass.AP(statscr, 0, [[2048, 1], [1, 2048]]), sstv).then_inc(sBD, 16)
                sy.wait_ge(sSC, sc0 + 2)
                sy.dma_start(bass.AP(statscr, 2048, [[2048, 1], [1, 2048]]), sstv).then_inc(sBD, 16)
                sy.wait_ge(sBD, bd0 + 32)
                sy.dma_start(bass.AP(dstat, 0, [[48, 128], [1, 16]]),
                             bass.AP(statscr, 0, [[16, 128], [1, 16]])).then_inc(sBD, 16)
                sy.dma_start(bass.AP(dstat, 16, [[48, 128], [1, 16]]),
                             bass.AP(statscr, 2048, [[16, 128], [1, 16]])).then_inc(sBD, 16)
                sy.wait_ge(sSC, sc0 + 3)
                sy.dma_start(bass.AP(statscr, 0, [[16, 128], [1, 16]]),
                             dstat[:, 0:16]).then_inc(sBD, 16)
                sy.wait_ge(sSC, sc0 + 4)
                sy.dma_start(bass.AP(statscr, 2048, [[16, 128], [1, 16]]),
                             dstat[:, 32:48]).then_inc(sBD, 16)
                sy.wait_ge(sBD, bd0 + 96)
                sy.dma_start(mbc, bass.AP(statscr, 0, [[0, 128], [1, 2048]])).then_inc(sBD, 16)
                sy.dma_start(ibc, bass.AP(statscr, 2048, [[0, 128], [1, 2048]])).then_inc(sBD, 16)
            C["sBD"] += 16 * 8

            @blk.vector
            def _(v):
                for dc in range(NDC):
                    v.wait_ge(sXp[dc % 2], x0[dc % 2] + 16 * (dc // 2 + 1))
                    if which == 1 and dc >= 1:
                        v.wait_ge(sST, st0 + dc)
                    tgt = xbf[:, dc * T:(dc + 1) * T] if which == 0 else sqst
                    v.tensor_copy(tgt, xstage[dc % 2]).then_inc(sCB, 1)
                v.wait_ge(sST, st0 + NDC)
                v.wait_ge(sDD, 16 * 11 * NHP * (b + which))
                v.tensor_copy(sstv, sxp[0:1, :]).then_inc(sSC, 1)
                v.wait_ge(sBD, bd0 + 16)
                v.tensor_copy(sstv, sx2p[0:1, :]).then_inc(sSC, 1)
                v.wait_ge(sBD, bd0 + 64)
                v.tensor_scalar(dstat[:, 0:16], dstat[:, 0:16], 1.0 / 1024.0, None,
                                OP.mult).then_inc(sVC, 1)
                v.tensor_scalar(dstat[:, 16:32], dstat[:, 16:32], 1.0 / 1024.0, 1e-5,
                                OP.mult, OP.add).then_inc(sVC, 1)
                v.wait_ge(sVC, C["sVC"] + 2)
                v.tensor_mul(dstat[:, 32:48], dstat[:, 0:16],
                             dstat[:, 0:16]).then_inc(sVC, 1)
                v.wait_ge(sVC, C["sVC"] + 3)
                v.tensor_sub(dstat[:, 16:32], dstat[:, 16:32],
                             dstat[:, 32:48]).then_inc(sVC, 1)
                v.wait_ge(sVC, C["sVC"] + 4)
                v.reciprocal(dstat[:, 32:48], dstat[:, 16:32]).then_inc(sSC, 1)
                v.wait_ge(sSQ, sq0 + NDC + 1)
                if which == 1:
                    v.tensor_scalar(dstat[:, 32:48], dstat[:, 32:48], isc[:, 0:1],
                                    None, OP.mult).then_inc(sSC, 1)
                else:
                    v.tensor_copy(dumm[:, :], dumm[:, :]).then_inc(sSC, 1)
            C["sCB"] += NDC
            C["sSC"] += 4
            C["sVC"] += 4

            @blk.scalar
            def _(s):
                for dc in range(NDC):
                    s.wait_ge(sXp[dc % 2], x0[dc % 2] + 16 * (dc // 2 + 1))
                    if dc >= 1:
                        s.wait_ge(sST, st0 + dc)
                    tgt = sqst if which == 0 else sq2
                    s.activation(tgt, xstage[dc % 2], AF.Square).then_inc(sSQ, 1)
                s.wait_ge(sSC, sc0 + 3)
                s.activation(dstat[:, 32:48], dstat[:, 32:48], AF.Sqrt).then_inc(sSQ, 1)
            C["sSQ"] += NDC + 1

            @blk.tensor
            def _(t):
                t.wait_ge(sEV, evL)
                for dc in range(NDC):
                    t.wait_ge(sCB, cb0 + dc + 1)
                    t.wait_ge(sSQ, sq0 + dc + 1)
                    xsrc = xbf[:, dc * T:(dc + 1) * T] if which == 0 else sqst
                    qsrc = sqst if which == 0 else sq2
                    for sl in range(4):
                        t.matmul(sxp[0:1, sl * 512:(sl + 1) * 512], onesb[:, :],
                                 xsrc[:, sl * 512:(sl + 1) * 512],
                                 start=(dc == 0), stop=(dc == NDC - 1))
                    for sl in range(4):
                        mm = t.matmul(sx2p[0:1, sl * 512:(sl + 1) * 512], onesb[:, :],
                                      qsrc[:, sl * 512:(sl + 1) * 512],
                                      start=(dc == 0), stop=(dc == NDC - 1))
                    mm.then_inc(sST, 1)
            C["sST"] += NDC

        with nc.Block() as blk:
            x0 = [C["sXs0"], C["sXs1"]]
            ap0 = C["sAP"]
            bd_all = C["sBD"]
            cbA = C["sCB"]
            sqA = C["sSQ"]

            apc_g = 2 if which == 0 else 3

            @blk.sync
            def _(sy):
                sy.wait_ge(sCB, cbA)
                sy.wait_ge(sSQ, sqA)
                for dc in range(NDC):
                    if dc >= 2:
                        sy.wait_ge(sAP, ap0 + apc_g * (dc - 2) + 1)
                    sy.dma_start(xstage[dc % 2], src[b, dc, :, :]).then_inc(sXs[dc % 2], 16)
            C["sXs0"] += 16 * (NDC // 2)
            C["sXs1"] += 16 * (NDC // 2)

            @blk.vector
            def _(v):
                v.wait_ge(sBD, bd_all)
                for dc in range(NDC):
                    v.wait_ge(sXs[dc % 2], x0[dc % 2] + 16 * (dc // 2 + 1))
                    if dc >= 1:
                        v.wait_ge(sAP, ap0 + apc_g * (dc - 1) + 2)
                    v.tensor_sub(tstt[0], xstage[dc % 2], mbc).then_inc(sAP, 1)
                    v.wait_ge(sAP, ap0 + apc_g * dc + 1)
                    if which == 0:
                        v.tensor_mul(xbf[:, dc * T:(dc + 1) * T], tstt[0],
                                     ibc).then_inc(sAP, 1)
                    else:
                        v.tensor_mul(tstt[1], tstt[0], ibc).then_inc(sAP, 1)
                        v.wait_ge(sAP, ap0 + apc_g * dc + 2)
                        v.tensor_scalar(z8v[:, dc * T:(dc + 1) * T], tstt[1],
                                        448.0, -448.0, OP.min,
                                        OP.max).then_inc(sAP, 1)
            C["sAP"] += apc_g * NDC

    ln0_shadow = {}

    def ln0_pre(b):
        with (
            nc.Block() as blk,
            nc.psum_tensor(f"sxpS{b}", [1, 2048], F32) as sxp,
            nc.psum_tensor(f"sx2pS{b}", [1, 2048], F32) as sx2p
        ):
            xs0 = [C["sXs0"], C["sXs1"]]
            cb0 = C["sCB"]
            sq0 = C["sSQ"]
            st0 = C["sST"]
            sc0 = C["sSC"]
            bd0 = C["sBD"]
            vc0 = C["sVC"]

            bs0 = C["sBs"]
            ln0_shadow[b] = bs0

            @blk.sync
            def _(sy):
                sy.wait_ge(sG, NB_IL * b)
                sy.wait_ge(sBD, bd0)
                sy.wait_ge(sBs, bs0)
                for dc in range(0, NDC, 2):
                    if dc >= 2:
                        sy.wait_ge(sCB, cb0 + dc - 1)
                        sy.wait_ge(sSQ, sq0 + dc - 1)
                    sy.dma_start(xstage[0], xd[b, dc, :, :]).then_inc(sXs[0], 16)
                sy.wait_ge(sSC, sc0 + 1)
                sy.dma_start(bass.AP(statscr, 0, [[2048, 1], [1, 2048]]), sstv).then_inc(sBs, 16)
                sy.wait_ge(sSC, sc0 + 2)
                sy.dma_start(bass.AP(statscr, 2048, [[2048, 1], [1, 2048]]), sstv).then_inc(sBs, 16)
                sy.wait_ge(sBs, bs0 + 32)
                sy.dma_start(bass.AP(dstat, 0, [[48, 128], [1, 16]]),
                             bass.AP(statscr, 0, [[16, 128], [1, 16]])).then_inc(sBs, 16)
                sy.dma_start(bass.AP(dstat, 16, [[48, 128], [1, 16]]),
                             bass.AP(statscr, 2048, [[16, 128], [1, 16]])).then_inc(sBs, 16)
                sy.wait_ge(sSC, sc0 + 3)
                sy.dma_start(bass.AP(statscr, 0, [[16, 128], [1, 16]]),
                             dstat[:, 0:16]).then_inc(sBs, 16)
                sy.wait_ge(sSC, sc0 + 4)
                sy.dma_start(bass.AP(statscr, 2048, [[16, 128], [1, 16]]),
                             dstat[:, 32:48]).then_inc(sBs, 16)
            C["sXs0"] += 16 * (NDC // 2)
            C["sBs"] += 16 * 6

            @blk.scalar
            def _(s):
                s.dma_start(xstage[1], xd[b, 1, :, :]).then_inc(sXs[1], 16)
                for dc in range(NDC):
                    s.wait_ge(sXs[dc % 2], xs0[dc % 2] + 16 * (dc // 2 + 1))
                    if dc >= 1:
                        s.wait_ge(sST, st0 + dc)
                    s.activation(sqst, xstage[dc % 2], AF.Square).then_inc(sSQ, 1)
                    if dc % 2 == 1 and dc + 2 < NDC:
                        s.wait_ge(sCB, cb0 + dc + 1)
                        s.wait_ge(sSQ, sq0 + dc + 1)
                        s.dma_start(xstage[1], xd[b, dc + 2, :, :]).then_inc(sXs[1], 16)
                s.wait_ge(sSC, sc0 + 3)
                s.activation(dstat[:, 32:48], dstat[:, 32:48], AF.Sqrt).then_inc(sSQ, 1)
            C["sSQ"] += NDC + 1
            C["sXs1"] += 16 * (NDC // 2)

            @blk.vector
            def _(v):
                for dc in range(NDC):
                    v.wait_ge(sXs[dc % 2], xs0[dc % 2] + 16 * (dc // 2 + 1))
                    v.tensor_copy(xbf[:, dc * T:(dc + 1) * T],
                                  xstage[dc % 2]).then_inc(sCB, 1)
                v.wait_ge(sST, st0 + NDC)
                v.wait_ge(sDD, 16 * 11 * NHP * b)
                v.tensor_copy(sstv, sxp[0:1, :]).then_inc(sSC, 1)
                v.wait_ge(sBs, bs0 + 16)
                v.tensor_copy(sstv, sx2p[0:1, :]).then_inc(sSC, 1)
                v.wait_ge(sBs, bs0 + 64)
                v.tensor_scalar(dstat[:, 0:16], dstat[:, 0:16], 1.0 / 1024.0, None,
                                OP.mult).then_inc(sVC, 1)
                v.tensor_scalar(dstat[:, 16:32], dstat[:, 16:32], 1.0 / 1024.0, 1e-5,
                                OP.mult, OP.add).then_inc(sVC, 1)
                v.wait_ge(sVC, vc0 + 2)
                v.tensor_mul(dstat[:, 32:48], dstat[:, 0:16],
                             dstat[:, 0:16]).then_inc(sVC, 1)
                v.wait_ge(sVC, vc0 + 3)
                v.tensor_sub(dstat[:, 16:32], dstat[:, 16:32],
                             dstat[:, 32:48]).then_inc(sVC, 1)
                v.wait_ge(sVC, vc0 + 4)
                v.reciprocal(dstat[:, 32:48], dstat[:, 16:32]).then_inc(sSC, 1)
                v.wait_ge(sSQ, sq0 + NDC + 1)
                v.tensor_copy(dumm[:, :], dumm[:, :]).then_inc(sSC, 1)
            C["sCB"] += NDC
            C["sSC"] += 4
            C["sVC"] += 4

            @blk.tensor
            def _(t):
                for dc in range(NDC):
                    t.wait_ge(sCB, cb0 + dc + 1)
                    t.wait_ge(sSQ, sq0 + dc + 1)
                    for sl in range(4):
                        t.matmul(sxp[0:1, sl * 512:(sl + 1) * 512], onesb[:, :],
                                 xbf[:, dc * T + sl * 512:dc * T + (sl + 1) * 512],
                                 start=(dc == 0), stop=(dc == NDC - 1))
                    for sl in range(4):
                        mm = t.matmul(sx2p[0:1, sl * 512:(sl + 1) * 512], onesb[:, :],
                                      sqst[:, sl * 512:(sl + 1) * 512],
                                      start=(dc == 0), stop=(dc == NDC - 1))
                    mm.then_inc(sST, 1)
            C["sST"] += NDC

    def ln0_post(b):
        with nc.Block() as blk:
            bs0 = ln0_shadow.pop(b)
            xs0 = [C["sXs0"], C["sXs1"]]
            ap0 = C["sAP"]
            ot0 = C["sOT"]

            @blk.sync
            def _(sy):
                sy.wait_ge(sOT, ot0)
                sy.wait_ge(sBs, bs0 + 96)
                sy.dma_start(mbc, bass.AP(statscr, 0, [[0, 128], [1, 2048]])).then_inc(sBs, 16)
                sy.dma_start(ibc, bass.AP(statscr, 2048, [[0, 128], [1, 2048]])).then_inc(sBs, 16)
                for dc in range(NDC):
                    if dc >= 2:
                        sy.wait_ge(sAP, ap0 + 2 * (dc - 2) + 2)
                    sy.dma_start(xstage[dc % 2], xd[b, dc, :, :]).then_inc(sXs[dc % 2], 16)
            C["sBs"] += 16 * 2
            C["sXs0"] += 16 * (NDC // 2)
            C["sXs1"] += 16 * (NDC // 2)

            @blk.vector
            def _(v):
                v.wait_ge(sBs, bs0 + 128)
                for dc in range(NDC):
                    v.wait_ge(sXs[dc % 2], xs0[dc % 2] + 16 * (dc // 2 + 1))
                    if dc >= 1:
                        v.wait_ge(sAP, ap0 + 2 * (dc - 1) + 2)
                    v.tensor_sub(tstt[0], xstage[dc % 2], mbc).then_inc(sAP, 1)
                    v.wait_ge(sAP, ap0 + 2 * dc + 1)
                    v.tensor_mul(xbf[:, dc * T:(dc + 1) * T], tstt[0],
                                 ibc).then_inc(sAP, 1)
            C["sAP"] += 2 * NDC

    KS1 = [(tb, th) for tb in (0, 1, 2, 3, 4, 14, 15) for th in range(2)]
    KS2 = [(tb, th) for tb in range(5, 14) for th in range(2)]
    sWv = sem("sWv")

    def vga_phase(b):
        with (
            nc.Block() as blk,
            nc.psum_tensor(f"gapsA_{b}", [16, 1024], F32) as gapsA,
            nc.psum_tensor(f"gapsB_{b}", [16, 1024], F32) as gapsB
        ):
            qk0 = C["sQK"]
            ev0 = C["sEV"]
            ge0 = C["sGE"]
            ap_vg = C["sAP"]
            oe0 = C["sOE"]
            vc0 = C["sVC"]
            gs0 = C["sGS"]

            @blk.tensor
            def _(t):
                t.wait_ge(sWD, RBD_READY)
                t.wait_ge(sOE, oe0)
                for gh in range(2):
                    if gh == 1:
                        t.wait_ge(sEV, ev0 + 1)
                    for sl in range(2):
                        for hp in range(NHP):
                            t.wait_ge(sAP, ap_vg - 2 * NDC + 2 * hp + 2)
                            xsl = xbf[:, hp * T + gh * 1024 + sl * 512:
                                      hp * T + gh * 1024 + (sl + 1) * 512]
                            t.matmul(gapsA[:, sl * 512:(sl + 1) * 512],
                                     Tg[:, hp * 16:(hp + 1) * 16], xsl,
                                     start=(hp == 0), stop=(hp == NHP - 1))
                            mm = t.matmul(gapsB[:, sl * 512:(sl + 1) * 512],
                                          Tg[:, 128 + hp * 16:128 + (hp + 1) * 16], xsl,
                                          start=(hp == 0), stop=(hp == NHP - 1))
                    mm.then_inc(sQK, 1)
            C["sQK"] += 2

            @blk.scalar
            def _(s):
                for gh in range(2):
                    s.wait_ge(sQK, qk0 + gh + 1)
                    s.activation(gsigA[0:16, gh * 1024:(gh + 1) * 1024], gapsA[:, :],
                                 AF.Sigmoid, bias=gb32f[:, 0:1])
                    s.activation(gsigB[0:16, gh * 1024:(gh + 1) * 1024], gapsB[:, :],
                                 AF.Sigmoid, bias=gb32f[:, 1:2]).then_inc(sEV, 1)
                s.wait_ge(sGE, ge0 + (2 if b > 0 else 1))
                s.dma_start(gatescr[:, :], gate16[:, :]).then_inc(sGS, 16)
            C["sEV"] += 2

            @blk.vector
            def _(v):
                v.wait_ge(sEV, ev0 + 2)
                if b > 0:
                    v.memset(vTr[:, 0:5, :, 64:65], 1.0)
                    v.memset(vTr[:, 0:5, :, 97:98], 1.0)
                    v.memset(vTr[:, 14:16, :, 64:65], 1.0)
                    v.memset(vTr[:, 14:16, :, 97:98], 1.0).then_inc(sGE, 1)
                v.tensor_scalar(tstt[0][0:16, 0:T], gsigB[0:16, 0:T], gcs[:, :], -1.0,
                                OP.mult, OP.add).then_inc(sVC, 1)
                v.wait_ge(sVC, vc0 + 1)
                v.tensor_mul(tstt[1][0:16, 0:T], tstt[0][0:16, 0:T],
                             gsigA[0:16, 0:T]).then_inc(sVC, 1)
                v.wait_ge(sVC, vc0 + 2)
                v.wait_ge(sGS, gs0)
                v.tensor_scalar(gate16[:, :], tstt[1][0:16, 0:T], 2.0, None,
                                OP.add).then_inc(sGE, 1)
            C["sGE"] += 2 if b > 0 else 1
            C["sVC"] += 2
            C["sGS"] += 16

    def _vproj(blk, b, ks, vps, qk0, ev0, ge_need, wv_need):
        @blk.tensor
        def _(t):
            t.wait_ge(sWv, wv_need)
            for ki, (tb, th) in enumerate(ks):
                if ki >= 2:
                    t.wait_ge(sEV, ev0 + 2 * ki - 2)
                for dc in range(NDC):
                    t.matmul(vps[ki % 2][:, :],
                             xbf[:, dc * T + tb * 128:dc * T + (tb + 1) * 128],
                             wvsv[:, dc * 1024 + th * 512:dc * 1024 + (th + 1) * 512],
                             start=(dc == 0), stop=False)
                t.matmul(vps[ki % 2][:, :], ones1b[:, :],
                         vbr[0:1, th * 512:(th + 1) * 512],
                         start=False, stop=True).then_inc(sQK, 1)
        C["sQK"] += len(ks)

        @blk.scalar
        def _(s):
            s.wait_ge(sGE, ge_need)
            for ki, (tb, th) in enumerate(ks):
                s.wait_ge(sQK, qk0 + ki + 1)
                vsrc = vps[ki % 2][:, :].rearrange("p (m x d) -> p m x d",
                                                   m=4, x=2, d=64)
                s.activation(vTr[:, tb, th * 4:(th + 1) * 4, 0:64],
                             vsrc[:, :, 0, :], AF.Identity)
                s.activation(vTr[:, tb, th * 4:(th + 1) * 4, 129:193],
                             vsrc[:, :, 1, :], AF.Identity).then_inc(sEV, 2)
        C["sEV"] += 2 * len(ks)

    def vgb1_phase(b):
        with (
            nc.Block() as blk,
            nc.psum_tensor(f"vps0_{b}", [128, 512], F32) as vps0,
            nc.psum_tensor(f"vps1_{b}", [128, 512], F32) as vps1
        ):
            qk0 = C["sQK"]
            ev0 = C["sEV"]
            ge0 = C["sGE"]
            wv0 = C["sWv"]

            @blk.gpsimd
            def _(g):
                g.wait_ge(sWD, RBD_READY)
                if b == 0:
                    g.memset(vT[:, :], 0.0).then_inc(sGE, 1)
                else:
                    g.nop().then_inc(sGE, 1)
                g.wait_ge(sGE, ge0)
                g.dma_start(wvsv, wvd[:, :]).then_inc(sWv, 16)
            C["sWv"] += 16

            if b == 0:
                @blk.vector
                def _(v):
                    v.wait_ge(sGE, ge0 + 1)
                    v.memset(vTr[:, 0:5, :, 64:65], 1.0)
                    v.memset(vTr[:, 0:5, :, 97:98], 1.0)
                    v.memset(vTr[:, 14:16, :, 64:65], 1.0)
                    v.memset(vTr[:, 14:16, :, 97:98], 1.0).then_inc(sGE, 1)
                C["sGE"] += 2
                ge_need = ge0 + 2
            else:
                C["sGE"] += 1
                ge_need = ge0 + 1

            _vproj(blk, b, KS1, [vps0, vps1], qk0, ev0, ge_need, wv0 + 16)

    def vgb2_phase(b):
        with (
            nc.Block() as blk,
            nc.psum_tensor(f"vps0b_{b}", [128, 512], F32) as vps0,
            nc.psum_tensor(f"vps1b_{b}", [128, 512], F32) as vps1
        ):
            qk0 = C["sQK"]
            ev0 = C["sEV"]
            ge0 = C["sGE"]
            oe0 = C["sOE"]
            wv_need = C["sWv"]

            @blk.vector
            def _(v):
                v.wait_ge(sGE, ge0)
                v.memset(vTr[:, 5:14, :, 64:65], 1.0)
                v.memset(vTr[:, 5:14, :, 97:98], 1.0).then_inc(sGE, 1)
            C["sGE"] += 1

            @blk.tensor
            def _(t):
                t.wait_ge(sOE, oe0)
                t.wait_ge(sGE, ge0 + 1)
            _vproj(blk, b, KS2, [vps0, vps1], qk0, ev0, ge0 + 1, wv_need)

    def attn_phase(b):
        with (
            nc.Block() as blk,
            nc.psum_tensor(f"scp0_{b}", [128, 1024], F32) as scp0,
            nc.psum_tensor(f"scp1_{b}", [128, 1024], F32) as scp1,
            nc.psum_tensor(f"cxA0_{b}", [65, 512], F32) as cxA0,
            nc.psum_tensor(f"cxB0_{b}", [128, 512], F32) as cxB0,
            nc.psum_tensor(f"cxA1_{b}", [65, 512], F32) as cxA1,
            nc.psum_tensor(f"cxB1_{b}", [128, 512], F32) as cxB1
        ):
            scp = [scp0, scp1]
            cxA = [cxA0, cxA1]
            cxB = [cxB0, cxB1]
            qk0 = C["sQK"]
            ev0 = C["sEV"]
            wq0 = [C["sWq0"], C["sWq1"]]
            wk0 = [C["sWk0"], C["sWk1"]]
            gd0 = [C["sGD0"], C["sGD1"]]
            p0 = [C["sP0"], C["sP1"]]
            ap_all = C["sAP"]
            cb_all = C["sCB"]
            sq_all = C["sSQ"]

            def emit_av(t, il):
                hp = il // (NTQ * NS0)
                tq = (il // NS0) % NTQ
                s0 = il % NS0
                i_g = ig(b, il)
                slot = jslot(b, hp, tq)
                t.wait_ge(sE, i_g + 1)
                if s0 == 0 and slot > 1:
                    t.wait_ge(sV, slot - 1)
                    t.wait_ge(sVS, slot - 1)
                base = s0 * 1544 + hp * 193
                t.matmul(cxA[slot % 2][:, :], vT[:, base:base + 65], exb[il % 3][:, 0:512],
                         start=(s0 == 0), stop=(s0 == NS0 - 1))
                t.matmul(cxB[slot % 2][:, :], vT[:, base + 65:base + 193], exb[il % 3][:, 512:1024],
                         start=(s0 == 0), stop=(s0 == NS0 - 1)).then_inc(sA, 1)

            @blk.gpsimd
            def _(g):
                def load_pair(hp):
                    g.dma_start(gbct[hp % 2][:, :].rearrange("p (h j) -> p h j", h=2, j=2048),
                                bass.AP(gatescr, (2 * hp) * 2048,
                                        [[0, 128], [2048, 2], [1, 2048]])).then_inc(sGDp[hp % 2], 16)
                    g.dma_start(wqs[hp % 2][:, :].rearrange("p (c d) -> p c d", c=8, d=128),
                                bass.AP(wqd, hp * 128,
                                        [[8192, 128], [1024, 8], [1, 128]])).then_inc(sWqp[hp % 2], 16)
                    g.dma_start(wks[hp % 2][:, :].rearrange("p (c d) -> p c d", c=8, d=128),
                                bass.AP(wkd, hp * 128,
                                        [[8192, 128], [1024, 8], [1, 128]])).then_inc(sWkp[hp % 2], 16)

                g.wait_ge(sGS, 16 * (b + 1))
                load_pair(0)
                load_pair(1)
                for hp in range(NHP):
                    hpg = b * NHP + hp
                    if hp + 2 < NHP:
                        nhp = hp + 2
                        g.wait_ge(sG, ig(b, (hp + 1) * NTQ * NS0))
                        g.wait_ge(sQK, qk0 + 4 * (nhp - 1))
                        load_pair(nhp)
                    g.wait_ge(sVS, jslot(b, hp, NTQ - 1) + 1)
                    g.wait_ge(sV, jslot(b, hp, NTQ - 1) + 1)
                    if hpg > 0:
                        g.wait_ge(sDD, 16 * (11 * (hpg - 1) + 11))
                        g.wait_ge(sN, 4 * hpg)
                    g.dma_start(bass.AP(denscr, 0, [[4096, 1], [1, 4096]]),
                                bass.AP(dstg, 32 * 2048, [[32 * 2048, 2], [1, 2048]])).then_inc(sDD, 16)
                    g.wait_ge(sDD, 16 * (11 * hpg + 1))
                    g.dma_start(bass.AP(d128, 0, [[32, 128], [1, 32]]),
                                bass.AP(denscr, 0, [[32, 128], [1, 32]])).then_inc(sDD, 16)
                    g.wait_ge(sR, hpg + 1)
                    g.dma_start(bass.AP(denscr, 0, [[32, 128], [1, 32]]),
                                d128r[:, :]).then_inc(sDD, 16)
                    g.wait_ge(sDD, 16 * (11 * hpg + 3))
                    for idx in range(4):
                        g.dma_start(bcps[0:64, idx * 512:(idx + 1) * 512],
                                    bass.AP(denscr, 2048 + idx * 512, [[0, 64], [1, 512]])).then_inc(sDD, 16)
                        g.dma_start(bcps[64:128, idx * 512:(idx + 1) * 512],
                                    bass.AP(denscr, idx * 512, [[0, 64], [1, 512]])).then_inc(sDD, 16)
            C["sWq0"] += 16 * (NHP // 2)
            C["sWq1"] += 16 * (NHP // 2)
            C["sWk0"] += 16 * (NHP // 2)
            C["sWk1"] += 16 * (NHP // 2)
            C["sGD0"] += 16 * (NHP // 2)
            C["sGD1"] += 16 * (NHP // 2)
            C["sDD"] += 16 * 11 * NHP

            @blk.sync
            def _(sy):
                sy.wait_ge(sWD, RBD_READY)
                sy.wait_ge(sCB, cb_all)
                sy.wait_ge(sSQ, sq_all)
                sy.wait_ge(sAP, ap_all)
                for hp in range(NHP):
                    for tq in range(NTQ):
                        idx = hp * NTQ + tq
                        if idx >= 2:
                            sy.wait_ge(sG, ig(b, (idx - 1) * NS0))
                        for h in range(2):
                            sy.dma_start(pbT[idx % 2][:, h * 2432:(h + 1) * 2432],
                                         bass.AP(rbd, (2 * hp + h) * 4096 + 1536 - tq * 512,
                                                 [[1, 128], [1, 2432]])).then_inc(sPp[idx % 2], 16)

            @blk.tensor
            def _(t):
                t.wait_ge(sEV, ev0)
                for hp in range(NHP):
                    hpg = b * NHP + hp
                    t.wait_ge(sWqp[hp % 2], wq0[hp % 2] + 16 * (hp // 2 + 1))
                    t.wait_ge(sWkp[hp % 2], wk0[hp % 2] + 16 * (hp // 2 + 1))
                    if hp == 0:
                        t.wait_ge(sAP, ap_all)
                    prev_last = ig(b, hp * NTQ * NS0)
                    if prev_last > 0:
                        t.wait_ge(sE, prev_last)
                    for which, wsl in ((0, wks), (1, wqs)):
                        for half in range(2):
                            if which == 1:
                                t.wait_ge(sEV, ev0 + 4 * hp + 1 + half)
                            for dc in range(NDC):
                                for sl in range(2):
                                    mm = t.matmul(scp[half][:, sl * 512:(sl + 1) * 512],
                                                  wsl[hp % 2][:, dc * 128:(dc + 1) * 128],
                                                  xbf[:, dc * T + half * 1024 + sl * 512:
                                                      dc * T + half * 1024 + (sl + 1) * 512],
                                                  start=(dc == 0), stop=(dc == NDC - 1))
                            mm.then_inc(sQK, 1)
                    for tq in range(NTQ):
                        for s0 in range(NS0):
                            il = (hp * NTQ + tq) * NS0 + s0
                            i_g = ig(b, il)
                            if i_g >= 2:
                                t.wait_ge(sE, i_g - 1)
                            if tq == 0 and s0 == 0:
                                t.wait_ge(sEV, ev0 + 4 * hp + 3)
                            if tq == 0 and s0 == 1:
                                t.wait_ge(sEV, ev0 + 4 * hp + 4)
                            t.matmul(scp[il % 2][:, 0:512],
                                     kqp[0:64, s0 * 128:(s0 + 1) * 128],
                                     kqp[0:64, T + tq * 512:T + (tq + 1) * 512],
                                     start=True, stop=False, tile_position=(0, 0))
                            t.matmul(scp[il % 2][:, 512:1024],
                                     kqp[64:128, s0 * 128:(s0 + 1) * 128],
                                     kqp[64:128, T + tq * 512:T + (tq + 1) * 512],
                                     start=True, stop=False, tile_position=(64, 0))
                            if il > 1:
                                emit_av(t, il - 2)
                            t.wait_ge(sG, i_g + 1)
                            t.matmul(scp[il % 2][:, 0:512], i128s[:, :],
                                     Gb[il % 2][:, 0:512], start=False, stop=True)
                            t.matmul(scp[il % 2][:, 512:1024], i128s[:, :],
                                     Gb[il % 2][:, 512:1024],
                                     start=False, stop=True).then_inc(sS, 1)
                emit_av(t, NB_IL - 2)
                emit_av(t, NB_IL - 1)
            for _i in range(2):
                C[f"sP{_i}"] += 32 * (NHP * NTQ // 2)
            C["sQK"] += 4 * NHP

            @blk.scalar
            def _(s):
                den_pending = []

                for hp in range(NHP):
                    s.wait_ge(sQK, qk0 + 4 * hp + 1)
                    s.activation(kqp[:, 0:1024], scp0[:, :], AF.Identity,
                                 bias=bks[:, hp:hp + 1]).then_inc(sEV, 1)
                    s.wait_ge(sQK, qk0 + 4 * hp + 2)
                    s.activation(kqp[:, 1024:2048], scp1[:, :], AF.Identity,
                                 bias=bks[:, hp:hp + 1]).then_inc(sEV, 1)
                    s.wait_ge(sQK, qk0 + 4 * hp + 3)
                    s.activation(kqp[:, T:T + 1024], scp0[:, :], AF.Identity,
                                 bias=bq8[:, hp:hp + 1], scale=eighth[:, :]).then_inc(sEV, 1)
                    s.wait_ge(sQK, qk0 + 4 * hp + 4)
                    s.activation(kqp[:, T + 1024:2 * T], scp1[:, :], AF.Identity,
                                 bias=bq8[:, hp:hp + 1], scale=eighth[:, :]).then_inc(sEV, 1)
                    for tq in range(NTQ):
                        for s0 in range(NS0):
                            il = (hp * NTQ + tq) * NS0 + s0
                            i_g = ig(b, il)
                            if s0 == 3 and den_pending:
                                hp_e, tq_e, igl = den_pending[0]
                                sl_e = jslot(b, hp_e, tq_e)
                                s.wait_ge(sA, igl + 1)
                                s.activation(ctxu[hp_e % 2][64:128, tq_e * 512:(tq_e + 1) * 512],
                                             cxB[sl_e % 2][64:128, :], AF.Identity)
                            if s0 == 5 and den_pending:
                                hp_e, tq_e, igl = den_pending.pop(0)
                                hpg_e = b * NHP + hp_e
                                sl_e = jslot(b, hp_e, tq_e)
                                if tq_e == 0 and hpg_e > 0:
                                    s.wait_ge(sDD, 16 * (11 * (hpg_e - 1) + 1))
                                s.activation(dstg[32:33, tq_e * 512:(tq_e + 1) * 512],
                                             cxB[sl_e % 2][32:33, :], AF.Identity).then_inc(sVS, 1)
                            s.wait_ge(sS, i_g + 1)
                            if i_g >= 3:
                                s.wait_ge(sA, i_g - 2)
                            s.activation(exb[il % 3][:, :], scp[il % 2][:, :], AF.Exp,
                                         bias=ebias[:, :]).then_inc(sE, 1)
                            if s0 == NS0 - 1:
                                den_pending.append((hp, tq, i_g))
                while den_pending:
                    hp_e, tq_e, igl = den_pending.pop(0)
                    sl_e = jslot(b, hp_e, tq_e)
                    s.wait_ge(sA, igl + 1)
                    s.activation(ctxu[hp_e % 2][64:128, tq_e * 512:(tq_e + 1) * 512],
                                 cxB[sl_e % 2][64:128, :], AF.Identity)
                    s.activation(dstg[32:33, tq_e * 512:(tq_e + 1) * 512],
                                 cxB[sl_e % 2][32:33, :], AF.Identity).then_inc(sVS, 1)
            C["sEV"] += 4 * NHP

            @blk.vector
            def _(v):
                v.wait_ge(sXWp[0], 16 * 16 * b)
                v.wait_ge(sXWp[1], 16 * 16 * b)
                v.wait_ge(sOp[0], 16 * 16 * b)
                v.wait_ge(sOp[1], 16 * 16 * b)
                pending = []

                def emit_evac(vv, step):
                    hp_e, tq_e, igl = pending[0]
                    hpg_e = b * NHP + hp_e
                    sl = jslot(b, hp_e, tq_e)
                    cu = ctxu[hp_e % 2]
                    c0 = tq_e * 512
                    if step == 0:
                        vv.wait_ge(sA, igl + 1)
                        vv.tensor_copy(cu[0:64, c0:c0 + 512], cxA[sl % 2][0:64, :])
                    else:
                        if tq_e == 0 and hpg_e > 0:
                            vv.wait_ge(sDD, 16 * (11 * (hpg_e - 1) + 1))
                        vv.tensor_copy(dstg[64:65, c0:c0 + 512],
                                       cxA[sl % 2][64:65, :]).then_inc(sV, 1)
                        pending.pop(0)

                for hp in range(NHP):
                    hpg = b * NHP + hp
                    for tq in range(NTQ):
                        for s0 in range(NS0):
                            il = (hp * NTQ + tq) * NS0 + s0
                            i_g = ig(b, il)
                            if s0 == 2 and pending:
                                emit_evac(v, 0)
                            if s0 == 4 and pending:
                                emit_evac(v, 1)
                            if tq == 0 and s0 == 7 and hp > 0:
                                v.wait_ge(sDD, 16 * (11 * (hpg - 1) + 2))
                                v.reciprocal(d128r[:, :], d128[:, :]).then_inc(sR, 1)
                            if hp > 0 and tq == 1 and s0 in (2, 6, 10, 14):
                                idx = {2: 0, 6: 1, 10: 2, 14: 3}[s0]
                                hpp = hp - 1
                                v.wait_ge(sDD, 16 * (11 * (hpg - 1) + 11))
                                v.wait_ge(sV, jslot(b, hpp, NTQ - 1) + 1)
                                v.wait_ge(sVS, jslot(b, hpp, NTQ - 1) + 1)
                                v.tensor_mul(
                                    ctxn[:, hpp * T + idx * 512:hpp * T + (idx + 1) * 512],
                                    ctxu[hpp % 2][:, idx * 512:(idx + 1) * 512],
                                    bcps[:, idx * 512:(idx + 1) * 512]).then_inc(sN, 1)
                            idx = hp * NTQ + tq
                            if s0 == 0:
                                v.wait_ge(sPp[idx % 2], p0[idx % 2] + 32 * (idx // 2 + 1))
                            if il == 0 or (s0 == 0 and tq == 0):
                                v.wait_ge(sGDp[hp % 2], gd0[hp % 2] + 16 * (hp // 2 + 1))
                            if i_g >= 2:
                                v.wait_ge(sS, i_g - 1)
                            rev = bass.AP(pbT[idx % 2].tensor, s0 * 128 + 511,
                                          [[4864, 128], [2432, 2], [-1, 512]])
                            gsl = bass.AP(gbct[hp % 2], tq * 512,
                                          [[4096, 128], [2048, 2], [1, 512]])
                            v.tensor_mul(Gb[il % 2][:, :], rev, gsl).then_inc(sG, 1)
                            if s0 == NS0 - 1:
                                pending.append((hp, tq, i_g))
                while pending:
                    emit_evac(v, 0)
                    emit_evac(v, 1)
                hpgl = b * NHP + NHP - 1
                v.wait_ge(sDD, 16 * (11 * hpgl + 2))
                v.reciprocal(d128r[:, :], d128[:, :]).then_inc(sR, 1)
                v.wait_ge(sDD, 16 * (11 * hpgl + 11))
                for idx in range(4):
                    v.tensor_mul(ctxn[:, (NHP - 1) * T + idx * 512:(NHP - 1) * T + (idx + 1) * 512],
                                 ctxu[(NHP - 1) % 2][:, idx * 512:(idx + 1) * 512],
                                 bcps[:, idx * 512:(idx + 1) * 512]).then_inc(sN, 1)

    def out_phase(b):
        with (
            nc.Block() as blk,
            nc.psum_tensor(f"oup0_{b}", [128, 512], F32) as oup0,
            nc.psum_tensor(f"oup1_{b}", [128, 512], F32) as oup1
        ):
            oup = [oup0, oup1]
            ot0 = C["sOT"]
            oe0 = C["sOE"]
            rv0 = C["sRV"]
            x0 = [C["sX0"], C["sX1"]]
            xw0 = [C["sXW0"], C["sXW1"]]
            wd0 = C["sWD"]
            sc_all = C["sSC"]

            @blk.gpsimd
            def _(g):
                g.dma_start(wosv, wod[:, :]).then_inc(sWD, 16)
                for j in range(32):
                    tqq, dcO = divmod(j, 8)
                    if j >= 2:
                        g.wait_ge(sRV, rv0 + j - 1)
                    g.dma_start(xrst[j % 2],
                                xd[b, dcO, :, tqq * 512:(tqq + 1) * 512]).then_inc(sXp[j % 2], 16)
            C["sWD"] += 16
            C["sX0"] += 16 * 16
            C["sX1"] += 16 * 16

            @blk.tensor
            def _(t):
                t.wait_ge(sWD, wd0 + 16)
                t.wait_ge(sSC, sc_all)
                for j in range(32):
                    tqq, dcO = divmod(j, 8)
                    if j >= 2:
                        t.wait_ge(sOE, oe0 + j - 1)
                    for hp in range(NHP):
                        mm = t.matmul(oup[j % 2][:, :],
                                      wosv[:, (hp * 8 + dcO) * 128:(hp * 8 + dcO + 1) * 128],
                                      ctxn[:, hp * T + tqq * 512:hp * T + (tqq + 1) * 512],
                                      start=(hp == 0), stop=(hp == NHP - 1))
                    mm.then_inc(sOT, 1)
            C["sOT"] += 32

            @blk.scalar
            def _(s):
                for j in range(32):
                    tqq, dcO = divmod(j, 8)
                    s.wait_ge(sOT, ot0 + j + 1)
                    if j >= 2:
                        s.wait_ge(sRV, rv0 + j - 1)
                    s.activation(obuf[j % 2], oup[j % 2][:, :], AF.Identity,
                                 bias=bos[:, dcO:dcO + 1]).then_inc(sOE, 1)
            C["sOE"] += 32

            @blk.vector
            def _(v):
                for j in range(32):
                    v.wait_ge(sOE, oe0 + j + 1)
                    v.wait_ge(sXp[j % 2], x0[j % 2] + 16 * (j // 2 + 1))
                    if j >= 2:
                        v.wait_ge(sXWp[j % 2], xw0[j % 2] + 16 * (j // 2))
                    v.tensor_add(o2buf[j % 2], obuf[j % 2],
                                 xrst[j % 2]).then_inc(sRV, 1)
            C["sRV"] += 32

            @blk.sync
            def _(sy):
                for j in range(32):
                    tqq, dcO = divmod(j, 8)
                    sy.wait_ge(sRV, rv0 + j + 1)
                    sy.dma_start(x1d[b, dcO, :, tqq * 512:(tqq + 1) * 512],
                                 o2buf[j % 2]).then_inc(sXWp[j % 2], 16)
            C["sXW0"] += 16 * 16
            C["sXW1"] += 16 * 16

    def ffn_phase(b):
        with (
            nc.Block() as blk,
            nc.psum_tensor(f"fps0_{b}", [128, 512], F32) as fps0,
            nc.psum_tensor(f"fps1_{b}", [128, 512], F32) as fps1
        ):
            fps = [fps0, fps1]
            ot0 = C["sOT"]
            oe0 = C["sOE"]
            rv0 = C["sRV"]
            x0 = [C["sX0"], C["sX1"]]
            o0 = [C["sO0"], C["sO1"]]
            wd0 = C["sWD"]
            ap_f = C["sAP"]

            @blk.gpsimd
            def _(g):
                g.dma_start(w8sv, w8d[:, :]).then_inc(sWD, 16)
                for j in range(32):
                    tqq, dcO = divmod(j, 8)
                    if j >= 2:
                        g.wait_ge(sRV, rv0 + j - 1)
                    g.dma_start(xrst[j % 2],
                                x1d[b, dcO, :, tqq * 512:(tqq + 1) * 512]).then_inc(sXp[j % 2], 16)
            C["sWD"] += 16
            C["sX0"] += 16 * 16
            C["sX1"] += 16 * 16

            @blk.tensor
            def _(t):
                t.wait_ge(sWD, wd0 + 16)
                t.wait_ge(sAP, ap_f)
                for j in range(32):
                    tqq, dcO = divmod(j, 8)
                    if j >= 2:
                        t.wait_ge(sOE, oe0 + j - 1)
                    for dc in range(NDC):
                        mm = t.matmul(fps[j % 2][:, :],
                                      w8sv[:, dc * 1024 + dcO * 128:dc * 1024 + (dcO + 1) * 128],
                                      z8v[:, dc * T + tqq * 512:dc * T + (tqq + 1) * 512],
                                      start=(dc == 0), stop=(dc == NDC - 1))
                    mm.then_inc(sOT, 1)
            C["sOT"] += 32

            @blk.scalar
            def _(s):
                for j in range(32):
                    tqq, dcO = divmod(j, 8)
                    s.wait_ge(sOT, ot0 + j + 1)
                    if j >= 2:
                        s.wait_ge(sRV, rv0 + j - 1)
                    s.activation(obuf[j % 2], fps[j % 2][:, :], AF.Identity,
                                 bias=bfs[:, dcO:dcO + 1], scale=isc[:, 1:2]).then_inc(sOE, 1)
            C["sOE"] += 32

            @blk.vector
            def _(v):
                for j in range(32):
                    v.wait_ge(sOE, oe0 + j + 1)
                    v.wait_ge(sXp[j % 2], x0[j % 2] + 16 * (j // 2 + 1))
                    if j >= 2:
                        v.wait_ge(sOp[j % 2], o0[j % 2] + 16 * (j // 2))
                    v.tensor_add(o2buf[j % 2], obuf[j % 2],
                                 xrst[j % 2]).then_inc(sRV, 1)
            C["sRV"] += 32

            @blk.sync
            def _(sy):
                for j in range(32):
                    tqq, dcO = divmod(j, 8)
                    sy.wait_ge(sRV, rv0 + j + 1)
                    sy.dma_start(outd[b, dcO, :, tqq * 512:(tqq + 1) * 512],
                                 o2buf[j % 2]).then_inc(sOp[j % 2], 16)
            C["sO0"] += 16 * 16
            C["sO1"] += 16 * 16

    import os
    nph = int(os.environ.get("ATT_NPH", "999"))
    done = [0]

    def _run(f, *a):
        if done[0] < nph:
            f(*a)
            done[0] += 1

    for b in range(B_C):
        if b == 0:
            _run(ln_phase, 0, 0)
            _run(vga_phase, 0)
            _run(vgb1_phase, 0)
        _run(vgb2_phase, b)
        _run(attn_phase, b)
        if b + 1 < B_C:
            _run(ln0_pre, b + 1)
        _run(out_phase, b)
        if b + 1 < B_C:
            _run(ln0_post, b + 1)
            _run(vga_phase, b + 1)
        _run(ln_phase, b, 1)
        if b + 1 < B_C:
            _run(vgb1_phase, b + 1)
        _run(ffn_phase, b)

    if nph >= 35:
        with nc.Block() as blk:
            @blk.sync
            def _(sy):
                sy.wait_ge(sXWp[0], 16 * 16 * B_C)
                sy.wait_ge(sXWp[1], 16 * 16 * B_C)
                sy.wait_ge(sOp[0], 16 * 16 * B_C)
                sy.wait_ge(sOp[1], 16 * 16 * B_C)

    return nc


_CACHE = {}


def _consts():
    if "oht" not in _CACHE:
        bt = _bucket_table()
        oh = np.zeros((320, 4096), dtype=BF)
        oh[bt, np.arange(4095)] = 1.0
        _CACHE["oht"] = oh
    oh = _CACHE["oht"]
    i128 = np.ascontiguousarray(np.eye(128, dtype=BF))
    selR = np.zeros((2, 128), dtype=BF)
    selR[1, 0:64] = 1.0
    selR[0, 64:128] = 1.0
    sel232 = np.zeros((2, 32), dtype=BF)
    sel232[0, 0:16] = 1.0
    sel232[1, 16:32] = 1.0
    p82 = np.zeros((8, 2), dtype=BF)
    p82[0:4, 0] = 1.0
    p82[4:8, 1] = 1.0
    return oh, i128, selR, sel232, p82


def _pack_w(w):
    return np.ascontiguousarray(
        np.asarray(w, np.float32).T.reshape(8, 128, 1024).transpose(1, 0, 2).reshape(128, 8192))


def _pack_wo(w):
    return np.ascontiguousarray(
        np.asarray(w, np.float32).T.reshape(8, 128, 8, 128).transpose(1, 0, 2, 3).reshape(128, 8192))


def kernel(L0_attn_hidden_states, L0_attn_q_w, L0_attn_q_b, L0_attn_k_w, L0_attn_k_b,
           L0_attn_v_w, L0_attn_v_b, L0_attn_out_w, L0_attn_out_b, L0_attn_rel_embed,
           L0_attn_gru_const, L0_attn_gru_w, L0_attn_gru_b, L0_ffn_weight_fp8,
           L0_ffn_bias, L0_ffn_input_scale, L0_ffn_weight_scale):
    x = np.asarray(L0_attn_hidden_states, dtype=np.float32)
    oh, i128, selR, sel232, p82 = _consts()

    shared = {
        "wq": _pack_w(L0_attn_q_w),
        "wk": _pack_w(L0_attn_k_w),
        "wv": _pack_w(L0_attn_v_w),
        "wo": _pack_wo(L0_attn_out_w),
        "w8": _pack_w(L0_ffn_weight_fp8),
        "bq": np.ascontiguousarray(np.asarray(L0_attn_q_b, np.float32).reshape(8, 128).T),
        "bk": np.ascontiguousarray(np.asarray(L0_attn_k_b, np.float32).reshape(8, 128).T),
        "bvr": np.ascontiguousarray(np.asarray(L0_attn_v_b, np.float32)[None, :]),
        "bo": np.ascontiguousarray(np.asarray(L0_attn_out_b, np.float32).reshape(8, 128).T),
        "bfn": np.ascontiguousarray(np.asarray(L0_ffn_bias, np.float32).reshape(8, 128).T),
        "rel": np.ascontiguousarray(np.asarray(L0_attn_rel_embed, np.float32)),
        "oht": oh,
        "gw": np.ascontiguousarray(np.asarray(L0_attn_gru_w, np.float32)),
        "gbi": np.ascontiguousarray(np.asarray(L0_attn_gru_b, np.float32).reshape(8, 1)),
        "gci": np.ascontiguousarray(np.asarray(L0_attn_gru_const, np.float32).reshape(16, 1)),
        "scli": np.array([[float(np.asarray(L0_ffn_input_scale).ravel()[0]),
                           float(np.asarray(L0_ffn_weight_scale).ravel()[0])]], np.float32),
        "i128": i128, "selR": selR, "sel232": sel232, "p82": p82,
    }
    xt = np.ascontiguousarray(x.reshape(32, T, 8, 128).transpose(0, 2, 3, 1))

    nc = build_nc()
    in_maps = []
    for c in range(8):
        m = dict(shared)
        m["x"] = np.ascontiguousarray(xt[4 * c:4 * c + 4])
        in_maps.append(m)
    res = run_bass_kernel_spmd(nc, in_maps, core_ids=list(range(8)))
    outs = []
    for c in range(8):
        o = np.asarray(res.results[c]["out"])
        outs.append(o.reshape(4, 1024, T).transpose(0, 2, 1))
    return np.ascontiguousarray(np.concatenate(outs, axis=0), dtype=np.float32)


# version 15
# speedup vs baseline: 1.6006x; 1.0258x over previous
import contextlib

import numpy as np
import ml_dtypes

import concourse.bass as bass
import concourse.mybir as mybir
from concourse.bass_utils import run_bass_kernel_spmd

F32 = mybir.dt.float32
BF16 = mybir.dt.bfloat16
FP8 = mybir.dt.float8e4
U8 = mybir.dt.uint8
AF = mybir.ActivationFunctionType
OP = mybir.AluOpType
BF = ml_dtypes.bfloat16

B_C = 4
T = 2048
NHP = 8
NTQ = 4
TQ = 512
NS0 = 16
NDC = 8
EXPB = -4.0
NB_IL = NHP * NTQ * NS0
GDPB = 16 + 16 * 2 * NHP

NUM_BUCKETS = 320
MAX_DISTANCE = 800


def _bucket_table():
    import math
    d = np.arange(-(T - 1), T)
    nb = NUM_BUCKETS // 2
    rel_b = (d > 0).astype(np.int64) * nb
    rp = np.abs(d)
    me = nb // 2
    small = rp < me
    rl = np.log(np.maximum(rp, 1).astype(np.float32) / me)
    rl = rl / math.log(MAX_DISTANCE / me)
    rl = rl * (nb - me)
    rl = (me + rl).astype(np.int64)
    rl = np.minimum(rl, nb - 1)
    return rel_b + np.where(small, rp, rl)


def build_nc():
    nc = bass.Bass("TRN2", target_bir_lowering=False)

    xd = nc.dram_tensor("x", [B_C, NDC, 128, T], F32, kind="ExternalInput")
    wqd = nc.dram_tensor("wq", [128, 8192], F32, kind="ExternalInput")
    wkd = nc.dram_tensor("wk", [128, 8192], F32, kind="ExternalInput")
    wvd = nc.dram_tensor("wv", [128, 8192], F32, kind="ExternalInput")
    wod = nc.dram_tensor("wo", [128, 8192], F32, kind="ExternalInput")
    w8d = nc.dram_tensor("w8", [128, 8192], F32, kind="ExternalInput")
    bqd = nc.dram_tensor("bq", [128, 8], F32, kind="ExternalInput")
    bkd = nc.dram_tensor("bk", [128, 8], F32, kind="ExternalInput")
    bvrd = nc.dram_tensor("bvr", [1, 1024], F32, kind="ExternalInput")
    bod = nc.dram_tensor("bo", [128, 8], F32, kind="ExternalInput")
    bfd = nc.dram_tensor("bfn", [128, 8], F32, kind="ExternalInput")
    reld = nc.dram_tensor("rel", [320, 16], F32, kind="ExternalInput")
    ohtd = nc.dram_tensor("oht", [320, 4096], BF16, kind="ExternalInput")
    gwd_d = nc.dram_tensor("gw", [8, 64], F32, kind="ExternalInput")
    gbid = nc.dram_tensor("gbi", [8, 1], F32, kind="ExternalInput")
    gcid = nc.dram_tensor("gci", [16, 1], F32, kind="ExternalInput")
    sclid = nc.dram_tensor("scli", [1, 2], F32, kind="ExternalInput")
    i128d = nc.dram_tensor("i128", [128, 128], BF16, kind="ExternalInput")
    selRd = nc.dram_tensor("selR", [2, 128], BF16, kind="ExternalInput")
    sel232d = nc.dram_tensor("sel232", [2, 32], BF16, kind="ExternalInput")
    p82d = nc.dram_tensor("p82", [8, 2], BF16, kind="ExternalInput")

    outd = nc.dram_tensor("out", [B_C, NDC, 128, T], F32, kind="ExternalOutput")

    rbd = nc.dram_tensor("rbd", [16, 4096], BF16, kind="Internal")
    x1d = nc.dram_tensor("x1d", [B_C, NDC, 128, T], F32, kind="Internal")
    statscr = nc.dram_tensor("statscr", [4096], F32, kind="Internal")
    denscr = nc.dram_tensor("denscr", [4096], F32, kind="Internal")
    gatescr = nc.dram_tensor("gatescr", [16, 2048], BF16, kind="Internal")

    es = contextlib.ExitStack()
    nc._kernel_es = es

    def sb(name, shape, dt_):
        return es.enter_context(nc.sbuf_tensor(name, shape, dt_))

    def sem(name):
        return es.enter_context(nc.semaphore(name))

    arA = sb("arA", [128, 49408], U8)
    arB = sb("arB", [128, 32768], U8)
    arC = sb("arC", [128, 32768], U8)

    vT = arA[:, :].bitcast(BF16)
    vTr = vT.rearrange("p (s h c) -> p s h c", s=NS0, h=NHP, c=193)
    wosv = arA[:, 0:16384].bitcast(BF16)
    w8sv = arA[:, 16384:24576].bitcast(FP8)
    z8v = arA[:, 24576:40960].bitcast(FP8)
    xbf = arB[:, :].bitcast(BF16)
    ctxn = arC[:, :].bitcast(BF16)
    mbc = arC[:, 0:8192].bitcast(F32)
    ibc = arC[:, 8192:16384].bitcast(F32)
    gsigA = arC[:, 16384:24576].bitcast(F32)
    gsigB = arC[:, 24576:32768].bitcast(F32)
    wvsv = arC[:, 16384:32768].bitcast(BF16)

    xstage_t = [sb(f"xst{i}", [128, 2432], F32) for i in range(2)]
    xstage = [x[:, 0:2048] for x in xstage_t]
    ctxu = [sb(f"ctxu{i}", [128, T], BF16) for i in range(2)]
    kqp = sb("kqp", [128, 2 * T], BF16)
    sqst = kqp[:, 0:2048]
    sq2 = kqp[:, 2048:4096]
    gbct = [sb(f"gbc{i}", [128, 2 * T], BF16) for i in range(2)]
    tstt = [g[:, :].bitcast(F32) for g in gbct]
    wqs = [sb(f"wqs{i}", [128, 1024], BF16) for i in range(2)]
    wks = [sb(f"wks{i}", [128, 1024], BF16) for i in range(2)]
    obuf = [w[:, :].bitcast(F32) for w in wqs]
    xrst = [w[:, :].bitcast(F32) for w in wks]
    pbT = [x[:, :].bitcast(BF16) for x in xstage_t]
    Gb = [sb(f"Gb{i}", [128, 1024], BF16) for i in range(2)]
    exb = [sb(f"exb{i}", [128, 1024], BF16) for i in range(3)]
    o2buf = [g[:, :].bitcast(F32) for g in Gb]
    dstg = sb("dstg", [128, 2048], F32)
    sstv = dstg[0:1, 0:2048]
    d128 = sb("d128", [128, 32], F32)
    d128r = sb("d128r", [128, 32], F32)
    bcps = sb("bcps", [128, 2048], BF16)
    gate16 = sb("gate16", [16, 2048], BF16)
    dstat = sb("dstat", [128, 48], F32)

    bq8 = sb("bq8", [128, 8], F32)
    bqraw = sb("bqraw", [128, 8], F32)
    bks = sb("bks", [128, 8], F32)
    bos = sb("bos", [128, 8], F32)
    bfs = sb("bfs", [128, 8], F32)
    vbr = sb("vbr", [1, 1024], BF16)
    gwdb = sb("gwdb", [8, 128], BF16)
    gbr = sb("gbr", [8, 1], BF16)
    gcs = sb("gcs", [16, 1], F32)
    scl = sb("scl", [1, 2], F32)
    invs_sb = sb("invs", [1, 1], F32)
    sws_sb = sb("sws", [1, 1], F32)
    isc = sb("isc", [128, 2], F32)
    i128s = sb("i128s", [128, 128], BF16)
    selRs = sb("selRs", [2, 128], BF16)
    sel232s = sb("sel232s", [2, 32], BF16)
    p82s = sb("p82s", [8, 2], BF16)
    Tg = sb("Tg", [128, 256], BF16)
    gb2s = sb("gb2s", [2, 1], BF16)
    gb32f = sb("gb32f", [16, 2], F32)
    smalls = sb("smalls", [128, 4], F32)
    onesb = smalls[:, 0:1].bitcast(BF16)[:, 0:1]
    ones1b = sb("ones1b", [1, 128], BF16)
    ones1f = sb("ones1f", [1, 128], F32)
    ebias = smalls[:, 1:2]
    eighth = smalls[:, 2:3]
    dumm = smalls[0:1, 3:4]

    sWD = sem("sWD")
    sGS = sem("sGS")
    sGDp = [sem("sGD0"), sem("sGD1")]
    sXp = [sem("sX0"), sem("sX1")]
    sPp = [sem(f"sP{i}") for i in range(2)]
    sWqp = [sem("sWq0"), sem("sWq1")]
    sWkp = [sem("sWk0"), sem("sWk1")]
    sXWp = [sem("sXW0"), sem("sXW1")]
    sOp = [sem("sO0"), sem("sO1")]
    sSQ = sem("sSQ")
    sCB = sem("sCB")
    sST = sem("sST")
    sSC = sem("sSC")
    sBD = sem("sBD")
    sAP = sem("sAP")
    sQK = sem("sQK")
    sEV = sem("sEV")
    sG = sem("sG")
    sS = sem("sS")
    sE = sem("sE")
    sA = sem("sA")
    sV = sem("sV")
    sDD = sem("sDD")
    sR = sem("sR")
    sB = sem("sB")
    sN = sem("sN")
    sOT = sem("sOT")
    sOE = sem("sOE")
    sRV = sem("sRV")
    sTI = sem("sTI")
    sVI = sem("sVI")
    sGE = sem("sGE")
    sVC = sem("sVC")
    sVS = sem("sVS")
    sXs = [sem("sXs0"), sem("sXs1")]
    sBs = sem("sBs")

    C = {k: 0 for k in ("sWD sGS sGD0 sGD1 sX0 sX1 sP0 sP1 sWq0 sWq1 sWk0 sWk1 "
                        "sXW0 sXW1 sO0 sO1 sSQ sCB sST sSC sBD sAP sQK sEV sDD "
                        "sOT sOE sRV sGE sVC sXs0 sXs1 sBs sWv").split()}

    def ig(b, il):
        return b * NB_IL + il

    def jslot(b, hp, tq):
        return (b * NHP + hp) * NTQ + tq

    ohts = arA[:, 0:24576].bitcast(BF16)
    rbsb = arA[:, 24576:40960].bitcast(BF16)[0:16, 0:4096]
    rels = arC[:, 0:96].bitcast(BF16)
    with nc.Block() as blk:
        @blk.gpsimd
        def _(g):
            for dst, src in (
                (bqraw[:, :], bqd[:, :]), (bks[:, :], bkd[:, :]),
                (bos[:, :], bod[:, :]), (bfs[:, :], bfd[:, :]),
                (vbr[:, :], bvrd[:, :]),
                (rels[0:128, 0:16], reld[0:128, :]),
                (rels[0:128, 16:32], reld[128:256, :]),
                (rels[0:64, 32:48], reld[256:320, :]),
                (ohts[0:128, 0:4096], ohtd[0:128, :]),
                (ohts[0:128, 4096:8192], ohtd[128:256, :]),
                (ohts[0:64, 8192:12288], ohtd[256:320, :]),
                (gwdb[:, 0:64], gwd_d[:, :]), (gwdb[:, 64:128], gwd_d[:, :]),
                (gbr[:, :], gbid[:, :]), (gcs[:, :], gcid[:, :]),
                (scl[:, :], sclid[:, :]), (i128s[:, :], i128d[:, :]),
                (selRs[:, :], selRd[:, :]), (sel232s[:, :], sel232d[:, :]),
                (p82s[:, :], p82d[:, :]),
            ):
                g.dma_start(dst, src).then_inc(sWD, 16)
            g.memset(onesb[:, :], 1.0)
            g.memset(ones1b[:, :], 1.0)
            g.memset(ones1f[:, :], 1.0)
            g.memset(ebias[:, :], EXPB)
            g.memset(eighth[:, :], 0.125)
            g.memset(dumm[:, :], 0.0)
            g.memset(dstg[:, :], 0.0)
            g.memset(Tg[:, :], 0.0).then_inc(sTI, 1)
            g.wait_ge(sVI, 4)
            g.dma_start(rbd[:, :], rbsb[:, :]).then_inc(sWD, 16)

        with (
            nc.psum_tensor("rbps", [16, 1024], F32) as rbps,
            nc.psum_tensor("gw2ps", [128, 2], F32) as gw2ps,
            nc.psum_tensor("gb2ps", [2, 1], F32) as gb2ps,
            nc.psum_tensor("gb32ps", [16, 2], F32) as gb32ps,
            nc.psum_tensor("iscps", [128, 2], F32) as iscps,
        ):
            @blk.tensor
            def _(t):
                t.wait_ge(sWD, 16 * 20)
                t.wait_ge(sTI, 1)
                cnts = [128, 128, 64]
                for n4 in range(4):
                    if n4 >= 1:
                        t.wait_ge(sVI, n4)
                    for c in range(3):
                        for half in range(2):
                            mm = t.matmul(rbps[:, half * 512:(half + 1) * 512],
                                          rels[0:cnts[c], 16 * c:16 * (c + 1)],
                                          ohts[0:cnts[c],
                                               c * 4096 + n4 * 1024 + half * 512:
                                               c * 4096 + n4 * 1024 + (half + 1) * 512],
                                          start=(c == 0), stop=(c == 2))
                    mm.then_inc(sTI, 1)
                t.matmul(gw2ps[:, :], gwdb[:, :], p82s[:, :],
                         start=True, stop=True).then_inc(sTI, 1)
                t.matmul(gb2ps[:, :], p82s[:, :], gbr[:, :],
                         start=True, stop=True).then_inc(sTI, 1)
                t.wait_ge(sVI, 6)
                t.matmul(gb32ps[:, 0:1], sel232s[:, 0:16], gb2s[:, :],
                         start=True, stop=True)
                t.matmul(gb32ps[:, 1:2], sel232s[:, 16:32], gb2s[:, :],
                         start=True, stop=True).then_inc(sTI, 1)
                t.wait_ge(sVI, 7)
                t.matmul(iscps[:, 0:1], ones1f[:, :], invs_sb[:, :], start=True, stop=True)
                t.matmul(iscps[:, 1:2], ones1f[:, :], sws_sb[:, :],
                         start=True, stop=True).then_inc(sTI, 1)

            @blk.vector
            def _(v):
                for n4 in range(4):
                    v.wait_ge(sTI, 2 + n4)
                    v.tensor_copy(rbsb[:, n4 * 1024:(n4 + 1) * 1024],
                                  rbps[:, :]).then_inc(sVI, 1)
                v.wait_ge(sTI, 6)
                for hp in range(NHP):
                    a0 = hp * 16
                    b0 = 128 + hp * 16
                    v.tensor_copy(Tg[0:64, a0 + 2 * hp:a0 + 2 * hp + 1], gw2ps[0:64, 0:1])
                    v.tensor_copy(Tg[64:128, a0 + 2 * hp + 1:a0 + 2 * hp + 2], gw2ps[64:128, 0:1])
                    v.tensor_copy(Tg[0:64, b0 + 2 * hp:b0 + 2 * hp + 1], gw2ps[0:64, 1:2])
                    cp = v.tensor_copy(Tg[64:128, b0 + 2 * hp + 1:b0 + 2 * hp + 2], gw2ps[64:128, 1:2])
                cp.then_inc(sVI, 1)
                v.wait_ge(sTI, 7)
                v.tensor_copy(gb2s[:, :], gb2ps[:, :]).then_inc(sVI, 1)
                v.reciprocal(invs_sb[:, :], scl[0:1, 0:1])
                v.tensor_mul(sws_sb[:, :], scl[0:1, 0:1], scl[0:1, 1:2]).then_inc(sVI, 1)
                v.wait_ge(sTI, 8)
                v.tensor_copy(gb32f[:, :], gb32ps[:, :])
                v.wait_ge(sTI, 9)
                v.tensor_copy(isc[:, :], iscps[:, :])
                v.tensor_scalar(bq8[:, :], bqraw[:, :], 0.125, None,
                                OP.mult).then_inc(sVI, 1)

    C["sWD"] = 16 * 21
    RBD_READY = C["sWD"]

    def ln_phase(b, which):
        src = xd if which == 0 else x1d
        with (
            nc.Block() as blk,
            nc.psum_tensor(f"sxp{b}_{which}", [1, 2048], F32) as sxp,
            nc.psum_tensor(f"sx2p{b}_{which}", [1, 2048], F32) as sx2p
        ):
            x0 = [C["sX0"], C["sX1"]]
            cb0 = C["sCB"]
            sq0 = C["sSQ"]
            st0 = C["sST"]
            sc0 = C["sSC"]
            bd0 = C["sBD"]
            bsF = C["sBs"]
            evL = C["sEV"]

            @blk.gpsimd
            def _(g):
                if which == 1:
                    g.wait_ge(sXWp[0], 16 * 16 * (b + 1))
                    g.wait_ge(sXWp[1], 16 * 16 * (b + 1))
                for dc in range(NDC):
                    if dc >= 2:
                        g.wait_ge(sCB, cb0 + dc - 1)
                        g.wait_ge(sSQ, sq0 + dc - 1)
                    g.dma_start(xstage[dc % 2], src[b, dc, :, :]).then_inc(sPq := sXp[dc % 2], 16)
            C["sX0"] += 16 * (NDC // 2)
            C["sX1"] += 16 * (NDC // 2)

            @blk.sync
            def _(sy):
                sy.wait_ge(sBD, bd0)
                sy.wait_ge(sBs, bsF)
                sy.wait_ge(sSC, sc0 + 1)
                sy.dma_start(bass.AP(statscr, 0, [[2048, 1], [1, 2048]]), sstv).then_inc(sBD, 16)
                sy.wait_ge(sSC, sc0 + 2)
                sy.dma_start(bass.AP(statscr, 2048, [[2048, 1], [1, 2048]]), sstv).then_inc(sBD, 16)
                sy.wait_ge(sBD, bd0 + 32)
                sy.dma_start(bass.AP(dstat, 0, [[48, 128], [1, 16]]),
                             bass.AP(statscr, 0, [[16, 128], [1, 16]])).then_inc(sBD, 16)
                sy.dma_start(bass.AP(dstat, 16, [[48, 128], [1, 16]]),
                             bass.AP(statscr, 2048, [[16, 128], [1, 16]])).then_inc(sBD, 16)
                sy.wait_ge(sSC, sc0 + 3)
                sy.dma_start(bass.AP(statscr, 0, [[16, 128], [1, 16]]),
                             dstat[:, 0:16]).then_inc(sBD, 16)
                sy.wait_ge(sSC, sc0 + 4)
                sy.dma_start(bass.AP(statscr, 2048, [[16, 128], [1, 16]]),
                             dstat[:, 32:48]).then_inc(sBD, 16)
                sy.wait_ge(sBD, bd0 + 96)
                sy.dma_start(mbc, bass.AP(statscr, 0, [[0, 128], [1, 2048]])).then_inc(sBD, 16)
                sy.dma_start(ibc, bass.AP(statscr, 2048, [[0, 128], [1, 2048]])).then_inc(sBD, 16)
            C["sBD"] += 16 * 8

            @blk.vector
            def _(v):
                for dc in range(NDC):
                    v.wait_ge(sXp[dc % 2], x0[dc % 2] + 16 * (dc // 2 + 1))
                    if which == 1 and dc >= 1:
                        v.wait_ge(sST, st0 + dc)
                    tgt = xbf[:, dc * T:(dc + 1) * T] if which == 0 else sqst
                    v.tensor_copy(tgt, xstage[dc % 2]).then_inc(sCB, 1)
                v.wait_ge(sST, st0 + NDC)
                v.wait_ge(sDD, 16 * 11 * NHP * (b + which))
                v.tensor_copy(sstv, sxp[0:1, :]).then_inc(sSC, 1)
                v.wait_ge(sBD, bd0 + 16)
                v.tensor_copy(sstv, sx2p[0:1, :]).then_inc(sSC, 1)
                v.wait_ge(sBD, bd0 + 64)
                v.tensor_scalar(dstat[:, 0:16], dstat[:, 0:16], 1.0 / 1024.0, None,
                                OP.mult).then_inc(sVC, 1)
                v.tensor_scalar(dstat[:, 16:32], dstat[:, 16:32], 1.0 / 1024.0, 1e-5,
                                OP.mult, OP.add).then_inc(sVC, 1)
                v.wait_ge(sVC, C["sVC"] + 2)
                v.tensor_mul(dstat[:, 32:48], dstat[:, 0:16],
                             dstat[:, 0:16]).then_inc(sVC, 1)
                v.wait_ge(sVC, C["sVC"] + 3)
                v.tensor_sub(dstat[:, 16:32], dstat[:, 16:32],
                             dstat[:, 32:48]).then_inc(sVC, 1)
                v.wait_ge(sVC, C["sVC"] + 4)
                v.reciprocal(dstat[:, 32:48], dstat[:, 16:32]).then_inc(sSC, 1)
                v.wait_ge(sSQ, sq0 + NDC + 1)
                if which == 1:
                    v.tensor_scalar(dstat[:, 32:48], dstat[:, 32:48], isc[:, 0:1],
                                    None, OP.mult).then_inc(sSC, 1)
                else:
                    v.tensor_copy(dumm[:, :], dumm[:, :]).then_inc(sSC, 1)
            C["sCB"] += NDC
            C["sSC"] += 4
            C["sVC"] += 4

            @blk.scalar
            def _(s):
                for dc in range(NDC):
                    s.wait_ge(sXp[dc % 2], x0[dc % 2] + 16 * (dc // 2 + 1))
                    if dc >= 1:
                        s.wait_ge(sST, st0 + dc)
                    tgt = sqst if which == 0 else sq2
                    s.activation(tgt, xstage[dc % 2], AF.Square).then_inc(sSQ, 1)
                s.wait_ge(sSC, sc0 + 3)
                s.activation(dstat[:, 32:48], dstat[:, 32:48], AF.Sqrt).then_inc(sSQ, 1)
            C["sSQ"] += NDC + 1

            @blk.tensor
            def _(t):
                t.wait_ge(sEV, evL)
                for dc in range(NDC):
                    t.wait_ge(sCB, cb0 + dc + 1)
                    t.wait_ge(sSQ, sq0 + dc + 1)
                    xsrc = xbf[:, dc * T:(dc + 1) * T] if which == 0 else sqst
                    qsrc = sqst if which == 0 else sq2
                    for sl in range(4):
                        t.matmul(sxp[0:1, sl * 512:(sl + 1) * 512], onesb[:, :],
                                 xsrc[:, sl * 512:(sl + 1) * 512],
                                 start=(dc == 0), stop=(dc == NDC - 1))
                    for sl in range(4):
                        mm = t.matmul(sx2p[0:1, sl * 512:(sl + 1) * 512], onesb[:, :],
                                      qsrc[:, sl * 512:(sl + 1) * 512],
                                      start=(dc == 0), stop=(dc == NDC - 1))
                    mm.then_inc(sST, 1)
            C["sST"] += NDC

        with nc.Block() as blk:
            x0 = [C["sXs0"], C["sXs1"]]
            ap0 = C["sAP"]
            bd_all = C["sBD"]
            cbA = C["sCB"]
            sqA = C["sSQ"]

            apc_g = 2 if which == 0 else 3

            @blk.sync
            def _(sy):
                sy.wait_ge(sCB, cbA)
                sy.wait_ge(sSQ, sqA)
                for dc in range(NDC):
                    if dc >= 2:
                        sy.wait_ge(sAP, ap0 + apc_g * (dc - 2) + 1)
                    sy.dma_start(xstage[dc % 2], src[b, dc, :, :]).then_inc(sXs[dc % 2], 16)
            C["sXs0"] += 16 * (NDC // 2)
            C["sXs1"] += 16 * (NDC // 2)

            @blk.vector
            def _(v):
                v.wait_ge(sBD, bd_all)
                for dc in range(NDC):
                    v.wait_ge(sXs[dc % 2], x0[dc % 2] + 16 * (dc // 2 + 1))
                    if dc >= 1:
                        v.wait_ge(sAP, ap0 + apc_g * (dc - 1) + 2)
                    v.tensor_sub(tstt[0], xstage[dc % 2], mbc).then_inc(sAP, 1)
                    v.wait_ge(sAP, ap0 + apc_g * dc + 1)
                    if which == 0:
                        v.tensor_mul(xbf[:, dc * T:(dc + 1) * T], tstt[0],
                                     ibc).then_inc(sAP, 1)
                    else:
                        v.tensor_mul(tstt[1], tstt[0], ibc).then_inc(sAP, 1)
                        v.wait_ge(sAP, ap0 + apc_g * dc + 2)
                        v.tensor_scalar(z8v[:, dc * T:(dc + 1) * T], tstt[1],
                                        448.0, -448.0, OP.min,
                                        OP.max).then_inc(sAP, 1)
            C["sAP"] += apc_g * NDC

    ln0_shadow = {}

    def ln0_pre(b):
        with (
            nc.Block() as blk,
            nc.psum_tensor(f"sxpS{b}", [1, 2048], F32) as sxp,
            nc.psum_tensor(f"sx2pS{b}", [1, 2048], F32) as sx2p
        ):
            xs0 = [C["sXs0"], C["sXs1"]]
            cb0 = C["sCB"]
            sq0 = C["sSQ"]
            st0 = C["sST"]
            sc0 = C["sSC"]
            bd0 = C["sBD"]
            vc0 = C["sVC"]

            bs0 = C["sBs"]
            ln0_shadow[b] = bs0

            @blk.sync
            def _(sy):
                sy.wait_ge(sG, NB_IL * b)
                sy.wait_ge(sBD, bd0)
                sy.wait_ge(sBs, bs0)
                for dc in range(0, NDC, 2):
                    if dc >= 2:
                        sy.wait_ge(sCB, cb0 + dc - 1)
                        sy.wait_ge(sSQ, sq0 + dc - 1)
                    sy.dma_start(xstage[0], xd[b, dc, :, :]).then_inc(sXs[0], 16)
                sy.wait_ge(sSC, sc0 + 1)
                sy.dma_start(bass.AP(statscr, 0, [[2048, 1], [1, 2048]]), sstv).then_inc(sBs, 16)
                sy.wait_ge(sSC, sc0 + 2)
                sy.dma_start(bass.AP(statscr, 2048, [[2048, 1], [1, 2048]]), sstv).then_inc(sBs, 16)
                sy.wait_ge(sBs, bs0 + 32)
                sy.dma_start(bass.AP(dstat, 0, [[48, 128], [1, 16]]),
                             bass.AP(statscr, 0, [[16, 128], [1, 16]])).then_inc(sBs, 16)
                sy.dma_start(bass.AP(dstat, 16, [[48, 128], [1, 16]]),
                             bass.AP(statscr, 2048, [[16, 128], [1, 16]])).then_inc(sBs, 16)
                sy.wait_ge(sSC, sc0 + 3)
                sy.dma_start(bass.AP(statscr, 0, [[16, 128], [1, 16]]),
                             dstat[:, 0:16]).then_inc(sBs, 16)
                sy.wait_ge(sSC, sc0 + 4)
                sy.dma_start(bass.AP(statscr, 2048, [[16, 128], [1, 16]]),
                             dstat[:, 32:48]).then_inc(sBs, 16)
            C["sXs0"] += 16 * (NDC // 2)
            C["sBs"] += 16 * 6

            @blk.scalar
            def _(s):
                s.dma_start(xstage[1], xd[b, 1, :, :]).then_inc(sXs[1], 16)
                for dc in range(NDC):
                    s.wait_ge(sXs[dc % 2], xs0[dc % 2] + 16 * (dc // 2 + 1))
                    if dc >= 1:
                        s.wait_ge(sST, st0 + dc)
                    s.activation(sqst, xstage[dc % 2], AF.Square).then_inc(sSQ, 1)
                    if dc % 2 == 1 and dc + 2 < NDC:
                        s.wait_ge(sCB, cb0 + dc + 1)
                        s.wait_ge(sSQ, sq0 + dc + 1)
                        s.dma_start(xstage[1], xd[b, dc + 2, :, :]).then_inc(sXs[1], 16)
                s.wait_ge(sSC, sc0 + 3)
                s.activation(dstat[:, 32:48], dstat[:, 32:48], AF.Sqrt).then_inc(sSQ, 1)
            C["sSQ"] += NDC + 1
            C["sXs1"] += 16 * (NDC // 2)

            @blk.vector
            def _(v):
                for dc in range(NDC):
                    v.wait_ge(sXs[dc % 2], xs0[dc % 2] + 16 * (dc // 2 + 1))
                    v.tensor_copy(xbf[:, dc * T:(dc + 1) * T],
                                  xstage[dc % 2]).then_inc(sCB, 1)
                v.wait_ge(sST, st0 + NDC)
                v.wait_ge(sDD, 16 * 11 * NHP * b)
                v.tensor_copy(sstv, sxp[0:1, :]).then_inc(sSC, 1)
                v.wait_ge(sBs, bs0 + 16)
                v.tensor_copy(sstv, sx2p[0:1, :]).then_inc(sSC, 1)
                v.wait_ge(sBs, bs0 + 64)
                v.tensor_scalar(dstat[:, 0:16], dstat[:, 0:16], 1.0 / 1024.0, None,
                                OP.mult).then_inc(sVC, 1)
                v.tensor_scalar(dstat[:, 16:32], dstat[:, 16:32], 1.0 / 1024.0, 1e-5,
                                OP.mult, OP.add).then_inc(sVC, 1)
                v.wait_ge(sVC, vc0 + 2)
                v.tensor_mul(dstat[:, 32:48], dstat[:, 0:16],
                             dstat[:, 0:16]).then_inc(sVC, 1)
                v.wait_ge(sVC, vc0 + 3)
                v.tensor_sub(dstat[:, 16:32], dstat[:, 16:32],
                             dstat[:, 32:48]).then_inc(sVC, 1)
                v.wait_ge(sVC, vc0 + 4)
                v.reciprocal(dstat[:, 32:48], dstat[:, 16:32]).then_inc(sSC, 1)
                v.wait_ge(sSQ, sq0 + NDC + 1)
                v.tensor_copy(dumm[:, :], dumm[:, :]).then_inc(sSC, 1)
            C["sCB"] += NDC
            C["sSC"] += 4
            C["sVC"] += 4

            @blk.tensor
            def _(t):
                for dc in range(NDC):
                    t.wait_ge(sCB, cb0 + dc + 1)
                    t.wait_ge(sSQ, sq0 + dc + 1)
                    for sl in range(4):
                        t.matmul(sxp[0:1, sl * 512:(sl + 1) * 512], onesb[:, :],
                                 xbf[:, dc * T + sl * 512:dc * T + (sl + 1) * 512],
                                 start=(dc == 0), stop=(dc == NDC - 1))
                    for sl in range(4):
                        mm = t.matmul(sx2p[0:1, sl * 512:(sl + 1) * 512], onesb[:, :],
                                      sqst[:, sl * 512:(sl + 1) * 512],
                                      start=(dc == 0), stop=(dc == NDC - 1))
                    mm.then_inc(sST, 1)
            C["sST"] += NDC

    def ln0_post(b):
        with nc.Block() as blk:
            bs0 = ln0_shadow.pop(b)
            xs0 = [C["sX0"], C["sX1"]]
            ap0 = C["sAP"]
            ot0 = C["sOT"]
            cbP = C["sCB"]
            sqP = C["sSQ"]

            @blk.sync
            def _(sy):
                sy.wait_ge(sOT, ot0)
                sy.wait_ge(sBs, bs0 + 96)
                sy.dma_start(mbc, bass.AP(statscr, 0, [[0, 128], [1, 2048]])).then_inc(sBs, 16)
                sy.dma_start(ibc, bass.AP(statscr, 2048, [[0, 128], [1, 2048]])).then_inc(sBs, 16)
            C["sBs"] += 16 * 2

            @blk.gpsimd
            def _(g):
                g.wait_ge(sCB, cbP)
                g.wait_ge(sSQ, sqP)
                for dc in range(NDC):
                    if dc >= 2:
                        g.wait_ge(sAP, ap0 + 2 * (dc - 2) + 2)
                    g.dma_start(xstage[dc % 2], xd[b, dc, :, :]).then_inc(sXp[dc % 2], 16)
            C["sX0"] += 16 * (NDC // 2)
            C["sX1"] += 16 * (NDC // 2)

            @blk.vector
            def _(v):
                v.wait_ge(sBs, bs0 + 128)
                for dc in range(NDC):
                    v.wait_ge(sXp[dc % 2], xs0[dc % 2] + 16 * (dc // 2 + 1))
                    if dc >= 1:
                        v.wait_ge(sAP, ap0 + 2 * (dc - 1) + 2)
                    v.tensor_sub(tstt[0], xstage[dc % 2], mbc).then_inc(sAP, 1)
                    v.wait_ge(sAP, ap0 + 2 * dc + 1)
                    v.tensor_mul(xbf[:, dc * T:(dc + 1) * T], tstt[0],
                                 ibc).then_inc(sAP, 1)
            C["sAP"] += 2 * NDC

    KS1 = [(tb, th) for tb in (0, 1, 2, 3, 4, 14, 15) for th in range(2)]
    KS2 = [(tb, th) for tb in range(5, 14) for th in range(2)]
    sWv = sem("sWv")

    def vga_phase(b):
        with (
            nc.Block() as blk,
            nc.psum_tensor(f"gapsA_{b}", [16, 1024], F32) as gapsA,
            nc.psum_tensor(f"gapsB_{b}", [16, 1024], F32) as gapsB
        ):
            qk0 = C["sQK"]
            ev0 = C["sEV"]
            ge0 = C["sGE"]
            ap_vg = C["sAP"]
            oe0 = C["sOE"]
            vc0 = C["sVC"]
            gs0 = C["sGS"]

            @blk.tensor
            def _(t):
                t.wait_ge(sWD, RBD_READY)
                t.wait_ge(sOE, oe0)
                for gh in range(2):
                    if gh == 1:
                        t.wait_ge(sEV, ev0 + 1)
                    for sl in range(2):
                        for hp in range(NHP):
                            t.wait_ge(sAP, ap_vg - 2 * NDC + 2 * hp + 2)
                            xsl = xbf[:, hp * T + gh * 1024 + sl * 512:
                                      hp * T + gh * 1024 + (sl + 1) * 512]
                            t.matmul(gapsA[:, sl * 512:(sl + 1) * 512],
                                     Tg[:, hp * 16:(hp + 1) * 16], xsl,
                                     start=(hp == 0), stop=(hp == NHP - 1))
                            mm = t.matmul(gapsB[:, sl * 512:(sl + 1) * 512],
                                          Tg[:, 128 + hp * 16:128 + (hp + 1) * 16], xsl,
                                          start=(hp == 0), stop=(hp == NHP - 1))
                    mm.then_inc(sQK, 1)
            C["sQK"] += 2

            @blk.scalar
            def _(s):
                for gh in range(2):
                    s.wait_ge(sQK, qk0 + gh + 1)
                    s.activation(gsigA[0:16, gh * 1024:(gh + 1) * 1024], gapsA[:, :],
                                 AF.Sigmoid, bias=gb32f[:, 0:1])
                    s.activation(gsigB[0:16, gh * 1024:(gh + 1) * 1024], gapsB[:, :],
                                 AF.Sigmoid, bias=gb32f[:, 1:2]).then_inc(sEV, 1)
                s.wait_ge(sGE, ge0 + (2 if b > 0 else 1))
                s.dma_start(gatescr[:, :], gate16[:, :]).then_inc(sGS, 16)
            C["sEV"] += 2

            @blk.vector
            def _(v):
                v.wait_ge(sEV, ev0 + 2)
                if b > 0:
                    v.memset(vTr[:, 0:5, :, 64:65], 1.0)
                    v.memset(vTr[:, 0:5, :, 97:98], 1.0)
                    v.memset(vTr[:, 14:16, :, 64:65], 1.0)
                    v.memset(vTr[:, 14:16, :, 97:98], 1.0).then_inc(sGE, 1)
                v.tensor_scalar(tstt[0][0:16, 0:T], gsigB[0:16, 0:T], gcs[:, :], -1.0,
                                OP.mult, OP.add).then_inc(sVC, 1)
                v.wait_ge(sVC, vc0 + 1)
                v.tensor_mul(tstt[1][0:16, 0:T], tstt[0][0:16, 0:T],
                             gsigA[0:16, 0:T]).then_inc(sVC, 1)
                v.wait_ge(sVC, vc0 + 2)
                v.wait_ge(sGS, gs0)
                v.tensor_scalar(gate16[:, :], tstt[1][0:16, 0:T], 2.0, None,
                                OP.add).then_inc(sGE, 1)
            C["sGE"] += 2 if b > 0 else 1
            C["sVC"] += 2
            C["sGS"] += 16

    def _vproj(blk, b, ks, vps, qk0, ev0, ge_need, wv_need):
        @blk.tensor
        def _(t):
            t.wait_ge(sWv, wv_need)
            for ki, (tb, th) in enumerate(ks):
                if ki >= 2:
                    t.wait_ge(sEV, ev0 + 2 * ki - 2)
                for dc in range(NDC):
                    t.matmul(vps[ki % 2][:, :],
                             xbf[:, dc * T + tb * 128:dc * T + (tb + 1) * 128],
                             wvsv[:, dc * 1024 + th * 512:dc * 1024 + (th + 1) * 512],
                             start=(dc == 0), stop=False)
                t.matmul(vps[ki % 2][:, :], ones1b[:, :],
                         vbr[0:1, th * 512:(th + 1) * 512],
                         start=False, stop=True).then_inc(sQK, 1)
        C["sQK"] += len(ks)

        @blk.scalar
        def _(s):
            s.wait_ge(sGE, ge_need)
            for ki, (tb, th) in enumerate(ks):
                s.wait_ge(sQK, qk0 + ki + 1)
                vsrc = vps[ki % 2][:, :].rearrange("p (m x d) -> p m x d",
                                                   m=4, x=2, d=64)
                s.activation(vTr[:, tb, th * 4:(th + 1) * 4, 0:64],
                             vsrc[:, :, 0, :], AF.Identity)
                s.activation(vTr[:, tb, th * 4:(th + 1) * 4, 129:193],
                             vsrc[:, :, 1, :], AF.Identity).then_inc(sEV, 2)
        C["sEV"] += 2 * len(ks)

    def vgb1_phase(b):
        with (
            nc.Block() as blk,
            nc.psum_tensor(f"vps0_{b}", [128, 512], F32) as vps0,
            nc.psum_tensor(f"vps1_{b}", [128, 512], F32) as vps1
        ):
            qk0 = C["sQK"]
            ev0 = C["sEV"]
            ge0 = C["sGE"]
            wv0 = C["sWv"]

            @blk.gpsimd
            def _(g):
                g.wait_ge(sWD, RBD_READY)
                if b == 0:
                    g.memset(vT[:, :], 0.0).then_inc(sGE, 1)
                else:
                    g.nop().then_inc(sGE, 1)
                g.wait_ge(sGE, ge0)
                g.dma_start(wvsv, wvd[:, :]).then_inc(sWv, 16)
            C["sWv"] += 16

            if b == 0:
                @blk.vector
                def _(v):
                    v.wait_ge(sGE, ge0 + 1)
                    v.memset(vTr[:, 0:5, :, 64:65], 1.0)
                    v.memset(vTr[:, 0:5, :, 97:98], 1.0)
                    v.memset(vTr[:, 14:16, :, 64:65], 1.0)
                    v.memset(vTr[:, 14:16, :, 97:98], 1.0).then_inc(sGE, 1)
                C["sGE"] += 2
                ge_need = ge0 + 2
            else:
                C["sGE"] += 1
                ge_need = ge0 + 1

            _vproj(blk, b, KS1, [vps0, vps1], qk0, ev0, ge_need, wv0 + 16)

    def vgb2_phase(b):
        with (
            nc.Block() as blk,
            nc.psum_tensor(f"vps0b_{b}", [128, 512], F32) as vps0,
            nc.psum_tensor(f"vps1b_{b}", [128, 512], F32) as vps1
        ):
            qk0 = C["sQK"]
            ev0 = C["sEV"]
            ge0 = C["sGE"]
            oe0 = C["sOE"]
            wv_need = C["sWv"]

            @blk.vector
            def _(v):
                v.wait_ge(sGE, ge0)
                v.memset(vTr[:, 5:14, :, 64:65], 1.0)
                v.memset(vTr[:, 5:14, :, 97:98], 1.0).then_inc(sGE, 1)
            C["sGE"] += 1

            @blk.tensor
            def _(t):
                t.wait_ge(sOE, oe0)
                t.wait_ge(sGE, ge0 + 1)
            _vproj(blk, b, KS2, [vps0, vps1], qk0, ev0, ge0 + 1, wv_need)

    def attn_phase(b):
        with (
            nc.Block() as blk,
            nc.psum_tensor(f"scp0_{b}", [128, 1024], F32) as scp0,
            nc.psum_tensor(f"scp1_{b}", [128, 1024], F32) as scp1,
            nc.psum_tensor(f"cxA0_{b}", [65, 512], F32) as cxA0,
            nc.psum_tensor(f"cxB0_{b}", [128, 512], F32) as cxB0,
            nc.psum_tensor(f"cxA1_{b}", [65, 512], F32) as cxA1,
            nc.psum_tensor(f"cxB1_{b}", [128, 512], F32) as cxB1
        ):
            scp = [scp0, scp1]
            cxA = [cxA0, cxA1]
            cxB = [cxB0, cxB1]
            qk0 = C["sQK"]
            ev0 = C["sEV"]
            wq0 = [C["sWq0"], C["sWq1"]]
            wk0 = [C["sWk0"], C["sWk1"]]
            gd0 = [C["sGD0"], C["sGD1"]]
            p0 = [C["sP0"], C["sP1"]]
            ap_all = C["sAP"]
            cb_all = C["sCB"]
            sq_all = C["sSQ"]

            def emit_av(t, il):
                hp = il // (NTQ * NS0)
                tq = (il // NS0) % NTQ
                s0 = il % NS0
                i_g = ig(b, il)
                slot = jslot(b, hp, tq)
                t.wait_ge(sE, i_g + 1)
                if s0 == 0 and slot > 1:
                    t.wait_ge(sV, slot - 1)
                    t.wait_ge(sVS, slot - 1)
                base = s0 * 1544 + hp * 193
                t.matmul(cxA[slot % 2][:, :], vT[:, base:base + 65], exb[il % 3][:, 0:512],
                         start=(s0 == 0), stop=(s0 == NS0 - 1))
                t.matmul(cxB[slot % 2][:, :], vT[:, base + 65:base + 193], exb[il % 3][:, 512:1024],
                         start=(s0 == 0), stop=(s0 == NS0 - 1)).then_inc(sA, 1)

            @blk.gpsimd
            def _(g):
                def load_pair(hp):
                    g.dma_start(gbct[hp % 2][:, :].rearrange("p (h j) -> p h j", h=2, j=2048),
                                bass.AP(gatescr, (2 * hp) * 2048,
                                        [[0, 128], [2048, 2], [1, 2048]])).then_inc(sGDp[hp % 2], 16)
                    g.dma_start(wqs[hp % 2][:, :].rearrange("p (c d) -> p c d", c=8, d=128),
                                bass.AP(wqd, hp * 128,
                                        [[8192, 128], [1024, 8], [1, 128]])).then_inc(sWqp[hp % 2], 16)
                    g.dma_start(wks[hp % 2][:, :].rearrange("p (c d) -> p c d", c=8, d=128),
                                bass.AP(wkd, hp * 128,
                                        [[8192, 128], [1024, 8], [1, 128]])).then_inc(sWkp[hp % 2], 16)

                g.wait_ge(sGS, 16 * (b + 1))
                load_pair(0)
                load_pair(1)
                for hp in range(NHP):
                    hpg = b * NHP + hp
                    if hp + 2 < NHP:
                        nhp = hp + 2
                        g.wait_ge(sG, ig(b, (hp + 1) * NTQ * NS0))
                        g.wait_ge(sQK, qk0 + 4 * (nhp - 1))
                        load_pair(nhp)
                    g.wait_ge(sVS, jslot(b, hp, NTQ - 1) + 1)
                    g.wait_ge(sV, jslot(b, hp, NTQ - 1) + 1)
                    if hpg > 0:
                        g.wait_ge(sDD, 16 * (11 * (hpg - 1) + 11))
                        g.wait_ge(sN, 4 * hpg)
                    g.dma_start(bass.AP(denscr, 0, [[4096, 1], [1, 4096]]),
                                bass.AP(dstg, 32 * 2048, [[32 * 2048, 2], [1, 2048]])).then_inc(sDD, 16)
                    g.wait_ge(sDD, 16 * (11 * hpg + 1))
                    g.dma_start(bass.AP(d128, 0, [[32, 128], [1, 32]]),
                                bass.AP(denscr, 0, [[32, 128], [1, 32]])).then_inc(sDD, 16)
                    g.wait_ge(sR, hpg + 1)
                    g.dma_start(bass.AP(denscr, 0, [[32, 128], [1, 32]]),
                                d128r[:, :]).then_inc(sDD, 16)
                    g.wait_ge(sDD, 16 * (11 * hpg + 3))
                    for idx in range(4):
                        g.dma_start(bcps[0:64, idx * 512:(idx + 1) * 512],
                                    bass.AP(denscr, 2048 + idx * 512, [[0, 64], [1, 512]])).then_inc(sDD, 16)
                        g.dma_start(bcps[64:128, idx * 512:(idx + 1) * 512],
                                    bass.AP(denscr, idx * 512, [[0, 64], [1, 512]])).then_inc(sDD, 16)
            C["sWq0"] += 16 * (NHP // 2)
            C["sWq1"] += 16 * (NHP // 2)
            C["sWk0"] += 16 * (NHP // 2)
            C["sWk1"] += 16 * (NHP // 2)
            C["sGD0"] += 16 * (NHP // 2)
            C["sGD1"] += 16 * (NHP // 2)
            C["sDD"] += 16 * 11 * NHP

            @blk.sync
            def _(sy):
                sy.wait_ge(sWD, RBD_READY)
                sy.wait_ge(sCB, cb_all)
                sy.wait_ge(sSQ, sq_all)
                sy.wait_ge(sAP, ap_all)
                for hp in range(NHP):
                    for tq in range(NTQ):
                        idx = hp * NTQ + tq
                        if idx >= 2:
                            sy.wait_ge(sG, ig(b, (idx - 1) * NS0))
                        for h in range(2):
                            sy.dma_start(pbT[idx % 2][:, h * 2432:(h + 1) * 2432],
                                         bass.AP(rbd, (2 * hp + h) * 4096 + 1536 - tq * 512,
                                                 [[1, 128], [1, 2432]])).then_inc(sPp[idx % 2], 16)

            @blk.tensor
            def _(t):
                t.wait_ge(sEV, ev0)
                for hp in range(NHP):
                    hpg = b * NHP + hp
                    t.wait_ge(sWqp[hp % 2], wq0[hp % 2] + 16 * (hp // 2 + 1))
                    t.wait_ge(sWkp[hp % 2], wk0[hp % 2] + 16 * (hp // 2 + 1))
                    if hp == 0:
                        t.wait_ge(sAP, ap_all)
                    prev_last = ig(b, hp * NTQ * NS0)
                    if prev_last > 0:
                        t.wait_ge(sE, prev_last)
                    for which, wsl in ((0, wks), (1, wqs)):
                        for half in range(2):
                            if which == 1:
                                t.wait_ge(sEV, ev0 + 4 * hp + 1 + half)
                            for dc in range(NDC):
                                for sl in range(2):
                                    mm = t.matmul(scp[half][:, sl * 512:(sl + 1) * 512],
                                                  wsl[hp % 2][:, dc * 128:(dc + 1) * 128],
                                                  xbf[:, dc * T + half * 1024 + sl * 512:
                                                      dc * T + half * 1024 + (sl + 1) * 512],
                                                  start=(dc == 0), stop=(dc == NDC - 1))
                            mm.then_inc(sQK, 1)
                    for tq in range(NTQ):
                        for s0 in range(NS0):
                            il = (hp * NTQ + tq) * NS0 + s0
                            i_g = ig(b, il)
                            if i_g >= 2:
                                t.wait_ge(sE, i_g - 1)
                            if tq == 0 and s0 == 0:
                                t.wait_ge(sEV, ev0 + 4 * hp + 3)
                            if tq == 0 and s0 == 1:
                                t.wait_ge(sEV, ev0 + 4 * hp + 4)
                            t.matmul(scp[il % 2][:, 0:512],
                                     kqp[0:64, s0 * 128:(s0 + 1) * 128],
                                     kqp[0:64, T + tq * 512:T + (tq + 1) * 512],
                                     start=True, stop=False, tile_position=(0, 0))
                            t.matmul(scp[il % 2][:, 512:1024],
                                     kqp[64:128, s0 * 128:(s0 + 1) * 128],
                                     kqp[64:128, T + tq * 512:T + (tq + 1) * 512],
                                     start=True, stop=False, tile_position=(64, 0))
                            if il > 1:
                                emit_av(t, il - 2)
                            t.wait_ge(sG, i_g + 1)
                            t.matmul(scp[il % 2][:, 0:512], i128s[:, :],
                                     Gb[il % 2][:, 0:512], start=False, stop=True)
                            t.matmul(scp[il % 2][:, 512:1024], i128s[:, :],
                                     Gb[il % 2][:, 512:1024],
                                     start=False, stop=True).then_inc(sS, 1)
                emit_av(t, NB_IL - 2)
                emit_av(t, NB_IL - 1)
            for _i in range(2):
                C[f"sP{_i}"] += 32 * (NHP * NTQ // 2)
            C["sQK"] += 4 * NHP

            @blk.scalar
            def _(s):
                den_pending = []

                for hp in range(NHP):
                    s.wait_ge(sQK, qk0 + 4 * hp + 1)
                    s.activation(kqp[:, 0:1024], scp0[:, :], AF.Identity,
                                 bias=bks[:, hp:hp + 1]).then_inc(sEV, 1)
                    s.wait_ge(sQK, qk0 + 4 * hp + 2)
                    s.activation(kqp[:, 1024:2048], scp1[:, :], AF.Identity,
                                 bias=bks[:, hp:hp + 1]).then_inc(sEV, 1)
                    s.wait_ge(sQK, qk0 + 4 * hp + 3)
                    s.activation(kqp[:, T:T + 1024], scp0[:, :], AF.Identity,
                                 bias=bq8[:, hp:hp + 1], scale=eighth[:, :]).then_inc(sEV, 1)
                    s.wait_ge(sQK, qk0 + 4 * hp + 4)
                    s.activation(kqp[:, T + 1024:2 * T], scp1[:, :], AF.Identity,
                                 bias=bq8[:, hp:hp + 1], scale=eighth[:, :]).then_inc(sEV, 1)
                    for tq in range(NTQ):
                        for s0 in range(NS0):
                            il = (hp * NTQ + tq) * NS0 + s0
                            i_g = ig(b, il)
                            if s0 == 3 and den_pending:
                                hp_e, tq_e, igl = den_pending[0]
                                sl_e = jslot(b, hp_e, tq_e)
                                s.wait_ge(sA, igl + 1)
                                s.activation(ctxu[hp_e % 2][64:128, tq_e * 512:(tq_e + 1) * 512],
                                             cxB[sl_e % 2][64:128, :], AF.Identity)
                            if s0 == 5 and den_pending:
                                hp_e, tq_e, igl = den_pending.pop(0)
                                hpg_e = b * NHP + hp_e
                                sl_e = jslot(b, hp_e, tq_e)
                                if tq_e == 0 and hpg_e > 0:
                                    s.wait_ge(sDD, 16 * (11 * (hpg_e - 1) + 1))
                                s.activation(dstg[32:33, tq_e * 512:(tq_e + 1) * 512],
                                             cxB[sl_e % 2][32:33, :], AF.Identity).then_inc(sVS, 1)
                            s.wait_ge(sS, i_g + 1)
                            if i_g >= 3:
                                s.wait_ge(sA, i_g - 2)
                            s.activation(exb[il % 3][:, :], scp[il % 2][:, :], AF.Exp,
                                         bias=ebias[:, :]).then_inc(sE, 1)
                            if s0 == NS0 - 1:
                                den_pending.append((hp, tq, i_g))
                while den_pending:
                    hp_e, tq_e, igl = den_pending.pop(0)
                    sl_e = jslot(b, hp_e, tq_e)
                    s.wait_ge(sA, igl + 1)
                    s.activation(ctxu[hp_e % 2][64:128, tq_e * 512:(tq_e + 1) * 512],
                                 cxB[sl_e % 2][64:128, :], AF.Identity)
                    s.activation(dstg[32:33, tq_e * 512:(tq_e + 1) * 512],
                                 cxB[sl_e % 2][32:33, :], AF.Identity).then_inc(sVS, 1)
            C["sEV"] += 4 * NHP

            @blk.vector
            def _(v):
                v.wait_ge(sXWp[0], 16 * 16 * b)
                v.wait_ge(sXWp[1], 16 * 16 * b)
                v.wait_ge(sOp[0], 16 * 16 * b)
                v.wait_ge(sOp[1], 16 * 16 * b)
                pending = []

                def emit_evac(vv, step):
                    hp_e, tq_e, igl = pending[0]
                    hpg_e = b * NHP + hp_e
                    sl = jslot(b, hp_e, tq_e)
                    cu = ctxu[hp_e % 2]
                    c0 = tq_e * 512
                    if step == 0:
                        vv.wait_ge(sA, igl + 1)
                        vv.tensor_copy(cu[0:64, c0:c0 + 512], cxA[sl % 2][0:64, :])
                    else:
                        if tq_e == 0 and hpg_e > 0:
                            vv.wait_ge(sDD, 16 * (11 * (hpg_e - 1) + 1))
                        vv.tensor_copy(dstg[64:65, c0:c0 + 512],
                                       cxA[sl % 2][64:65, :]).then_inc(sV, 1)
                        pending.pop(0)

                for hp in range(NHP):
                    hpg = b * NHP + hp
                    for tq in range(NTQ):
                        for s0 in range(NS0):
                            il = (hp * NTQ + tq) * NS0 + s0
                            i_g = ig(b, il)
                            if s0 == 2 and pending:
                                emit_evac(v, 0)
                            if s0 == 4 and pending:
                                emit_evac(v, 1)
                            if tq == 0 and s0 == 7 and hp > 0:
                                v.wait_ge(sDD, 16 * (11 * (hpg - 1) + 2))
                                v.reciprocal(d128r[:, :], d128[:, :]).then_inc(sR, 1)
                            if hp > 0 and tq == 1 and s0 in (2, 6, 10, 14):
                                idx = {2: 0, 6: 1, 10: 2, 14: 3}[s0]
                                hpp = hp - 1
                                v.wait_ge(sDD, 16 * (11 * (hpg - 1) + 11))
                                v.wait_ge(sV, jslot(b, hpp, NTQ - 1) + 1)
                                v.wait_ge(sVS, jslot(b, hpp, NTQ - 1) + 1)
                                v.tensor_mul(
                                    ctxn[:, hpp * T + idx * 512:hpp * T + (idx + 1) * 512],
                                    ctxu[hpp % 2][:, idx * 512:(idx + 1) * 512],
                                    bcps[:, idx * 512:(idx + 1) * 512]).then_inc(sN, 1)
                            idx = hp * NTQ + tq
                            if s0 == 0:
                                v.wait_ge(sPp[idx % 2], p0[idx % 2] + 32 * (idx // 2 + 1))
                            if il == 0 or (s0 == 0 and tq == 0):
                                v.wait_ge(sGDp[hp % 2], gd0[hp % 2] + 16 * (hp // 2 + 1))
                            if i_g >= 2:
                                v.wait_ge(sS, i_g - 1)
                            rev = bass.AP(pbT[idx % 2].tensor, s0 * 128 + 511,
                                          [[4864, 128], [2432, 2], [-1, 512]])
                            gsl = bass.AP(gbct[hp % 2], tq * 512,
                                          [[4096, 128], [2048, 2], [1, 512]])
                            v.tensor_mul(Gb[il % 2][:, :], rev, gsl).then_inc(sG, 1)
                            if s0 == NS0 - 1:
                                pending.append((hp, tq, i_g))
                while pending:
                    emit_evac(v, 0)
                    emit_evac(v, 1)
                hpgl = b * NHP + NHP - 1
                v.wait_ge(sDD, 16 * (11 * hpgl + 2))
                v.reciprocal(d128r[:, :], d128[:, :]).then_inc(sR, 1)
                v.wait_ge(sDD, 16 * (11 * hpgl + 11))
                for idx in range(4):
                    v.tensor_mul(ctxn[:, (NHP - 1) * T + idx * 512:(NHP - 1) * T + (idx + 1) * 512],
                                 ctxu[(NHP - 1) % 2][:, idx * 512:(idx + 1) * 512],
                                 bcps[:, idx * 512:(idx + 1) * 512]).then_inc(sN, 1)

    def out_phase(b):
        with (
            nc.Block() as blk,
            nc.psum_tensor(f"oup0_{b}", [128, 512], F32) as oup0,
            nc.psum_tensor(f"oup1_{b}", [128, 512], F32) as oup1,
            nc.psum_tensor(f"x1s_{b}", [1, 1024], F32) as x1s,
            nc.psum_tensor(f"x1q_{b}", [1, 1024], F32) as x1q
        ):
            oup = [oup0, oup1]
            ot0 = C["sOT"]
            oe0 = C["sOE"]
            rv0 = C["sRV"]
            x0 = [C["sX0"], C["sX1"]]
            xw0 = [C["sXW0"], C["sXW1"]]
            wd0 = C["sWD"]
            sc_all = C["sSC"]
            cbO = C["sCB"]
            sqO = C["sSQ"]
            stO = C["sST"]
            sdd_need = 16 * 11 * NHP * (b + 1)
            xbb = [sq2[:, k * 512:(k + 1) * 512] for k in range(2)]
            sqb = [sq2[:, 1024 + k * 512:1024 + (k + 1) * 512] for k in range(2)]

            def emit_sum(t, jj):
                tqq_j, dcO_j = divmod(jj, 8)
                co = (tqq_j % 2) * 512
                t.wait_ge(sCB, cbO + jj + 1)
                t.wait_ge(sSQ, sqO + jj + 1)
                if dcO_j == 0 and tqq_j >= 2:
                    t.wait_ge(sSC, sc_all + 1)
                t.matmul(x1s[0:1, co:co + 512], onesb[:, :], xbb[jj % 2],
                         start=(dcO_j == 0), stop=(dcO_j == 7))
                t.matmul(x1q[0:1, co:co + 512], onesb[:, :], sqb[jj % 2],
                         start=(dcO_j == 0), stop=(dcO_j == 7)).then_inc(sST, 1)

            @blk.gpsimd
            def _(g):
                g.dma_start(wosv, wod[:, :]).then_inc(sWD, 16)
                for j in range(32):
                    tqq, dcO = divmod(j, 8)
                    if j >= 2:
                        g.wait_ge(sRV, rv0 + j - 1)
                    g.dma_start(xrst[j % 2],
                                xd[b, dcO, :, tqq * 512:(tqq + 1) * 512]).then_inc(sXp[j % 2], 16)
            C["sWD"] += 16
            C["sX0"] += 16 * 16
            C["sX1"] += 16 * 16

            @blk.tensor
            def _(t):
                t.wait_ge(sWD, wd0 + 16)
                t.wait_ge(sSC, sc_all)
                for j in range(32):
                    tqq, dcO = divmod(j, 8)
                    if j >= 2:
                        t.wait_ge(sOE, oe0 + j - 1)
                    for hp in range(NHP):
                        mm = t.matmul(oup[j % 2][:, :],
                                      wosv[:, (hp * 8 + dcO) * 128:(hp * 8 + dcO + 1) * 128],
                                      ctxn[:, hp * T + tqq * 512:hp * T + (tqq + 1) * 512],
                                      start=(hp == 0), stop=(hp == NHP - 1))
                    mm.then_inc(sOT, 1)
                    if j >= 2:
                        emit_sum(t, j - 2)
                emit_sum(t, 30)
                emit_sum(t, 31)
            C["sOT"] += 32
            C["sST"] += 32

            @blk.scalar
            def _(s):
                for j in range(32):
                    tqq, dcO = divmod(j, 8)
                    s.wait_ge(sOT, ot0 + j + 1)
                    if j >= 2:
                        s.wait_ge(sRV, rv0 + j - 1)
                    s.activation(obuf[j % 2], oup[j % 2][:, :], AF.Identity,
                                 bias=bos[:, dcO:dcO + 1]).then_inc(sOE, 1)
                    s.wait_ge(sRV, rv0 + j + 1)
                    if j >= 2:
                        s.wait_ge(sST, stO + j - 1)
                    s.activation(sqb[j % 2], o2buf[j % 2], AF.Square).then_inc(sSQ, 1)
            C["sOE"] += 32
            C["sSQ"] += 32

            @blk.vector
            def _(v):
                for j in range(32):
                    v.wait_ge(sOE, oe0 + j + 1)
                    v.wait_ge(sXp[j % 2], x0[j % 2] + 16 * (j // 2 + 1))
                    if j >= 2:
                        v.wait_ge(sXWp[j % 2], xw0[j % 2] + 16 * (j // 2))
                        v.wait_ge(sSQ, sqO + j - 1)
                    v.tensor_add(o2buf[j % 2], obuf[j % 2],
                                 xrst[j % 2]).then_inc(sRV, 1)
                    if j >= 2:
                        v.wait_ge(sST, stO + j - 1)
                    v.wait_ge(sRV, rv0 + j + 1)
                    v.tensor_copy(xbb[j % 2], o2buf[j % 2]).then_inc(sCB, 1)
                    if j == 17:
                        v.wait_ge(sST, stO + 16)
                        v.wait_ge(sDD, sdd_need)
                        v.tensor_copy(dstg[64:65, 0:1024], x1s[0:1, :])
                        v.tensor_copy(dstg[96:97, 0:1024], x1q[0:1, :]).then_inc(sSC, 1)
                v.wait_ge(sST, stO + 32)
                v.tensor_copy(dstg[64:65, 1024:2048], x1s[0:1, :])
                v.tensor_copy(dstg[96:97, 1024:2048], x1q[0:1, :]).then_inc(sSC, 1)
            C["sRV"] += 32
            C["sCB"] += 32
            C["sSC"] += 2

            @blk.sync
            def _(sy):
                for j in range(32):
                    tqq, dcO = divmod(j, 8)
                    sy.wait_ge(sRV, rv0 + j + 1)
                    sy.dma_start(x1d[b, dcO, :, tqq * 512:(tqq + 1) * 512],
                                 o2buf[j % 2]).then_inc(sXWp[j % 2], 16)
            C["sXW0"] += 16 * 16
            C["sXW1"] += 16 * 16

    def ln1f_phase(b):
        with nc.Block() as blk:
            bd0 = C["sBD"]
            bs_all = C["sBs"]
            sc0 = C["sSC"]
            xs0 = [C["sXs0"], C["sXs1"]]
            ap0 = C["sAP"]
            vc0 = C["sVC"]
            sq0 = C["sSQ"]
            cbP = C["sCB"]
            xw_end = [C["sXW0"], C["sXW1"]]

            @blk.sync
            def _(sy):
                sy.wait_ge(sBD, bd0)
                sy.wait_ge(sBs, bs_all)
                sy.wait_ge(sSC, sc0)
                sy.dma_start(bass.AP(statscr, 0, [[2048, 2], [1, 2048]]),
                             bass.AP(dstg, 64 * 2048,
                                     [[32 * 2048, 2], [1, 2048]])).then_inc(sBD, 16)
                sy.wait_ge(sBD, bd0 + 16)
                sy.dma_start(bass.AP(dstat, 0, [[48, 128], [1, 16]]),
                             bass.AP(statscr, 0, [[16, 128], [1, 16]])).then_inc(sBD, 16)
                sy.dma_start(bass.AP(dstat, 16, [[48, 128], [1, 16]]),
                             bass.AP(statscr, 2048, [[16, 128], [1, 16]])).then_inc(sBD, 16)
                sy.wait_ge(sSC, sc0 + 1)
                sy.dma_start(bass.AP(statscr, 0, [[16, 128], [1, 16]]),
                             dstat[:, 0:16]).then_inc(sBD, 16)
                sy.wait_ge(sSC, sc0 + 2)
                sy.dma_start(bass.AP(statscr, 2048, [[16, 128], [1, 16]]),
                             dstat[:, 32:48]).then_inc(sBD, 16)
                sy.wait_ge(sBD, bd0 + 80)
                sy.wait_ge(sAP, ap0)
                sy.dma_start(mbc, bass.AP(statscr, 0, [[0, 128], [1, 2048]])).then_inc(sBD, 16)
                sy.dma_start(ibc, bass.AP(statscr, 2048, [[0, 128], [1, 2048]])).then_inc(sBD, 16)
                sy.wait_ge(sCB, cbP)
                for dc in range(NDC):
                    sy.wait_ge(sXWp[dc % 2],
                               xw_end[dc % 2] - 256 + 16 * ((24 + dc) // 2 + 1))
                    if dc >= 2:
                        sy.wait_ge(sAP, ap0 + 3 * (dc - 2) + 1)
                    sy.dma_start(xstage[dc % 2], x1d[b, dc, :, :]).then_inc(sXs[dc % 2], 16)
            C["sBD"] += 16 * 7
            C["sXs0"] += 16 * (NDC // 2)
            C["sXs1"] += 16 * (NDC // 2)

            @blk.vector
            def _(v):
                v.wait_ge(sBD, bd0 + 48)
                v.tensor_scalar(dstat[:, 0:16], dstat[:, 0:16], 1.0 / 1024.0, None,
                                OP.mult).then_inc(sVC, 1)
                v.tensor_scalar(dstat[:, 16:32], dstat[:, 16:32], 1.0 / 1024.0, 1e-5,
                                OP.mult, OP.add).then_inc(sVC, 1)
                v.wait_ge(sVC, vc0 + 2)
                v.tensor_mul(dstat[:, 32:48], dstat[:, 0:16],
                             dstat[:, 0:16]).then_inc(sVC, 1)
                v.wait_ge(sVC, vc0 + 3)
                v.tensor_sub(dstat[:, 16:32], dstat[:, 16:32],
                             dstat[:, 32:48]).then_inc(sVC, 1)
                v.wait_ge(sVC, vc0 + 4)
                v.reciprocal(dstat[:, 32:48], dstat[:, 16:32]).then_inc(sSC, 1)
                v.wait_ge(sSQ, sq0 + 1)
                v.tensor_scalar(dstat[:, 32:48], dstat[:, 32:48], isc[:, 0:1],
                                None, OP.mult).then_inc(sSC, 1)
                v.wait_ge(sBD, bd0 + 112)
                for dc in range(NDC):
                    v.wait_ge(sXs[dc % 2], xs0[dc % 2] + 16 * (dc // 2 + 1))
                    if dc >= 1:
                        v.wait_ge(sAP, ap0 + 3 * (dc - 1) + 3)
                    v.tensor_sub(tstt[0], xstage[dc % 2], mbc).then_inc(sAP, 1)
                    v.wait_ge(sAP, ap0 + 3 * dc + 1)
                    v.tensor_mul(tstt[1], tstt[0], ibc).then_inc(sAP, 1)
                    v.wait_ge(sAP, ap0 + 3 * dc + 2)
                    v.tensor_scalar(z8v[:, dc * T:(dc + 1) * T], tstt[1],
                                    448.0, -448.0, OP.min,
                                    OP.max).then_inc(sAP, 1)
            C["sVC"] += 4
            C["sSC"] += 2
            C["sAP"] += 3 * NDC

            @blk.scalar
            def _(s):
                s.wait_ge(sSC, sc0 + 1)
                s.activation(dstat[:, 32:48], dstat[:, 32:48], AF.Sqrt).then_inc(sSQ, 1)
            C["sSQ"] += 1

    def ffn_phase(b):
        with (
            nc.Block() as blk,
            nc.psum_tensor(f"fps0_{b}", [128, 512], F32) as fps0,
            nc.psum_tensor(f"fps1_{b}", [128, 512], F32) as fps1
        ):
            fps = [fps0, fps1]
            ot0 = C["sOT"]
            oe0 = C["sOE"]
            rv0 = C["sRV"]
            x0 = [C["sX0"], C["sX1"]]
            o0 = [C["sO0"], C["sO1"]]
            wd0 = C["sWD"]
            ap_f = C["sAP"]

            @blk.gpsimd
            def _(g):
                g.dma_start(w8sv, w8d[:, :]).then_inc(sWD, 16)
                for j in range(32):
                    tqq, dcO = divmod(j, 8)
                    if j >= 2:
                        g.wait_ge(sRV, rv0 + j - 1)
                    g.dma_start(xrst[j % 2],
                                x1d[b, dcO, :, tqq * 512:(tqq + 1) * 512]).then_inc(sXp[j % 2], 16)
            C["sWD"] += 16
            C["sX0"] += 16 * 16
            C["sX1"] += 16 * 16

            @blk.tensor
            def _(t):
                t.wait_ge(sWD, wd0 + 16)
                t.wait_ge(sAP, ap_f)
                for j in range(32):
                    tqq, dcO = divmod(j, 8)
                    if j >= 2:
                        t.wait_ge(sOE, oe0 + j - 1)
                    for dc in range(NDC):
                        mm = t.matmul(fps[j % 2][:, :],
                                      w8sv[:, dc * 1024 + dcO * 128:dc * 1024 + (dcO + 1) * 128],
                                      z8v[:, dc * T + tqq * 512:dc * T + (tqq + 1) * 512],
                                      start=(dc == 0), stop=(dc == NDC - 1))
                    mm.then_inc(sOT, 1)
            C["sOT"] += 32

            @blk.scalar
            def _(s):
                for j in range(32):
                    tqq, dcO = divmod(j, 8)
                    s.wait_ge(sOT, ot0 + j + 1)
                    if j >= 2:
                        s.wait_ge(sRV, rv0 + j - 1)
                    s.activation(obuf[j % 2], fps[j % 2][:, :], AF.Identity,
                                 bias=bfs[:, dcO:dcO + 1], scale=isc[:, 1:2]).then_inc(sOE, 1)
            C["sOE"] += 32

            @blk.vector
            def _(v):
                for j in range(32):
                    v.wait_ge(sOE, oe0 + j + 1)
                    v.wait_ge(sXp[j % 2], x0[j % 2] + 16 * (j // 2 + 1))
                    if j >= 2:
                        v.wait_ge(sOp[j % 2], o0[j % 2] + 16 * (j // 2))
                    v.tensor_add(o2buf[j % 2], obuf[j % 2],
                                 xrst[j % 2]).then_inc(sRV, 1)
            C["sRV"] += 32

            @blk.sync
            def _(sy):
                for j in range(32):
                    tqq, dcO = divmod(j, 8)
                    sy.wait_ge(sRV, rv0 + j + 1)
                    sy.dma_start(outd[b, dcO, :, tqq * 512:(tqq + 1) * 512],
                                 o2buf[j % 2]).then_inc(sOp[j % 2], 16)
            C["sO0"] += 16 * 16
            C["sO1"] += 16 * 16

    import os
    nph = int(os.environ.get("ATT_NPH", "999"))
    done = [0]

    def _run(f, *a):
        if done[0] < nph:
            f(*a)
            done[0] += 1

    for b in range(B_C):
        if b == 0:
            _run(ln_phase, 0, 0)
            _run(vga_phase, 0)
            _run(vgb1_phase, 0)
        _run(vgb2_phase, b)
        _run(attn_phase, b)
        if b + 1 < B_C:
            _run(ln0_pre, b + 1)
        _run(out_phase, b)
        if b + 1 < B_C:
            _run(ln0_post, b + 1)
            _run(vga_phase, b + 1)
        _run(ln1f_phase, b)
        if b + 1 < B_C:
            _run(vgb1_phase, b + 1)
        _run(ffn_phase, b)

    if nph >= 35:
        with nc.Block() as blk:
            @blk.sync
            def _(sy):
                sy.wait_ge(sXWp[0], 16 * 16 * B_C)
                sy.wait_ge(sXWp[1], 16 * 16 * B_C)
                sy.wait_ge(sOp[0], 16 * 16 * B_C)
                sy.wait_ge(sOp[1], 16 * 16 * B_C)

    return nc


_CACHE = {}


def _consts():
    if "oht" not in _CACHE:
        bt = _bucket_table()
        oh = np.zeros((320, 4096), dtype=BF)
        oh[bt, np.arange(4095)] = 1.0
        _CACHE["oht"] = oh
    oh = _CACHE["oht"]
    i128 = np.ascontiguousarray(np.eye(128, dtype=BF))
    selR = np.zeros((2, 128), dtype=BF)
    selR[1, 0:64] = 1.0
    selR[0, 64:128] = 1.0
    sel232 = np.zeros((2, 32), dtype=BF)
    sel232[0, 0:16] = 1.0
    sel232[1, 16:32] = 1.0
    p82 = np.zeros((8, 2), dtype=BF)
    p82[0:4, 0] = 1.0
    p82[4:8, 1] = 1.0
    return oh, i128, selR, sel232, p82


def _pack_w(w):
    return np.ascontiguousarray(
        np.asarray(w, np.float32).T.reshape(8, 128, 1024).transpose(1, 0, 2).reshape(128, 8192))


def _pack_wo(w):
    return np.ascontiguousarray(
        np.asarray(w, np.float32).T.reshape(8, 128, 8, 128).transpose(1, 0, 2, 3).reshape(128, 8192))


def kernel(L0_attn_hidden_states, L0_attn_q_w, L0_attn_q_b, L0_attn_k_w, L0_attn_k_b,
           L0_attn_v_w, L0_attn_v_b, L0_attn_out_w, L0_attn_out_b, L0_attn_rel_embed,
           L0_attn_gru_const, L0_attn_gru_w, L0_attn_gru_b, L0_ffn_weight_fp8,
           L0_ffn_bias, L0_ffn_input_scale, L0_ffn_weight_scale):
    x = np.asarray(L0_attn_hidden_states, dtype=np.float32)
    oh, i128, selR, sel232, p82 = _consts()

    shared = {
        "wq": _pack_w(L0_attn_q_w),
        "wk": _pack_w(L0_attn_k_w),
        "wv": _pack_w(L0_attn_v_w),
        "wo": _pack_wo(L0_attn_out_w),
        "w8": _pack_w(L0_ffn_weight_fp8),
        "bq": np.ascontiguousarray(np.asarray(L0_attn_q_b, np.float32).reshape(8, 128).T),
        "bk": np.ascontiguousarray(np.asarray(L0_attn_k_b, np.float32).reshape(8, 128).T),
        "bvr": np.ascontiguousarray(np.asarray(L0_attn_v_b, np.float32)[None, :]),
        "bo": np.ascontiguousarray(np.asarray(L0_attn_out_b, np.float32).reshape(8, 128).T),
        "bfn": np.ascontiguousarray(np.asarray(L0_ffn_bias, np.float32).reshape(8, 128).T),
        "rel": np.ascontiguousarray(np.asarray(L0_attn_rel_embed, np.float32)),
        "oht": oh,
        "gw": np.ascontiguousarray(np.asarray(L0_attn_gru_w, np.float32)),
        "gbi": np.ascontiguousarray(np.asarray(L0_attn_gru_b, np.float32).reshape(8, 1)),
        "gci": np.ascontiguousarray(np.asarray(L0_attn_gru_const, np.float32).reshape(16, 1)),
        "scli": np.array([[float(np.asarray(L0_ffn_input_scale).ravel()[0]),
                           float(np.asarray(L0_ffn_weight_scale).ravel()[0])]], np.float32),
        "i128": i128, "selR": selR, "sel232": sel232, "p82": p82,
    }
    xt = np.ascontiguousarray(x.reshape(32, T, 8, 128).transpose(0, 2, 3, 1))

    nc = build_nc()
    in_maps = []
    for c in range(8):
        m = dict(shared)
        m["x"] = np.ascontiguousarray(xt[4 * c:4 * c + 4])
        in_maps.append(m)
    res = run_bass_kernel_spmd(nc, in_maps, core_ids=list(range(8)))
    outs = []
    for c in range(8):
        o = np.asarray(res.results[c]["out"])
        outs.append(o.reshape(4, 1024, T).transpose(0, 2, 1))
    return np.ascontiguousarray(np.concatenate(outs, axis=0), dtype=np.float32)
